# version 16
# baseline (speedup 1.0000x reference)
import os
import numpy as np
import ml_dtypes
import concourse.bass as bass
import concourse.mybir as mybir
from concourse.bass_utils import run_bass_kernel_spmd

F32 = mybir.dt.float32
BF16 = mybir.dt.bfloat16
AF = mybir.ActivationFunctionType
OP = mybir.AluOpType
AX = mybir.AxisListType
NBF = ml_dtypes.bfloat16

P = 128
RND = 12582912.0
EPS_LN = 1e-5
EPS_RMS = 1.1920929e-07

CFG = dict(L=2048, D=2048, DIN=4096, NST=16, RANK=128, HID=8192)
ENGS = ("sync", "gpsimd", "tensor", "vector", "scalar")
KB = 1024


class Sem:
    __slots__ = ("h", "n")

    def __init__(self, h):
        self.h = h
        self.n = 0


class Plan:
    def __init__(self):
        self.steps = []
        self.last_dma = {}
        self.dma_sems = {}
        self.dma_rr = {}

    def set_dma_sems(self, eng, sems):
        self.dma_sems[eng] = sems
        self.dma_rr[eng] = 0

    def step(self, eng, fn, waits=(), sem=None, inck=1):
        waits = [w for w in waits if w is not None]
        if inck == 16 and sem is not None:
            sems = self.dma_sems.get(eng)
            if sems:
                r = self.dma_rr[eng]
                self.dma_rr[eng] = (r + 1) % len(sems)
                sem = sems[r]
            prev = self.last_dma.get((eng, id(sem)))
            if prev is not None:
                waits.append(prev)
        tok = None
        if sem is not None:
            sem.n += inck
            tok = (sem, sem.n)
            if inck == 16:
                self.last_dma[(eng, id(sem))] = tok
        self.steps.append((eng, tuple(waits), fn, sem, inck))
        return tok

    def play(self, block):
        def mk(ename):
            def body(E):
                hi = {}
                for eng, waits, fn, sem, inck in self.steps:
                    if eng != ename:
                        continue
                    for (ws, wv) in waits:
                        if wv > 0 and hi.get(id(ws), -1) < wv:
                            E.wait_ge(ws.h, wv)
                            hi[id(ws)] = wv
                    inst = fn(E)
                    if sem is not None:
                        assert inst is not None, f"inc step returned None on {ename}"
                        inst.then_inc(sem.h, inck)
            return body
        for e in ENGS:
            getattr(block, e)(mk(e))


def build(cfg, stop_after=None):
    L = cfg["L"]; D = cfg["D"]; DIN = cfg["DIN"]; NST = cfg["NST"]
    RANK = cfg["RANK"]; HID = cfg["HID"]
    DH = DIN // 2; NG = DH // P; KD = D // P; LH = L // 2; NTT = L // P
    HF = 2 * HID; GQ = 64
    QGU_R = HF // 8; QP_R = D // 8
    NKH = HID // P
    NMT = KD
    NHT = NKH
    FCH = 256; NFC = LH // FCH
    XDR = RANK + 2 * NST
    KDIN = DIN // P
    NCHB = LH // 512
    NCHL = L // 512

    nc = bass.Bass("TRN2", target_bir_lowering=False, detect_race_conditions=False)

    x_h = nc.dram_tensor("x_tm", [L, D], F32, kind="ExternalInput")
    xT_h = nc.dram_tensor("x_fm", [D, LH], F32, kind="ExternalInput")
    w_inT_h = nc.dram_tensor("w_inT", [D, 2 * DH], F32, kind="ExternalInput")
    convw_h = nc.dram_tensor("convw", [DH, 4], F32, kind="ExternalInput")
    vecs_h = nc.dram_tensor("vecs", [NG, P, 4], F32, kind="ExternalInput")
    xprojT_h = nc.dram_tensor("xprojT", [DH, XDR], F32, kind="ExternalInput")
    dtprojT_h = nc.dram_tensor("dtprojT", [RANK, DH], F32, kind="ExternalInput")
    alog_h = nc.dram_tensor("alog", [DH, NST], F32, kind="ExternalInput")
    oprojT_h = nc.dram_tensor("oprojT", [DIN, D], F32, kind="ExternalInput")
    normw_h = nc.dram_tensor("normw", [P, KD], F32, kind="ExternalInput")
    wgu_q_h = nc.dram_tensor("wgu_q", [QGU_R, D], F32, kind="ExternalInput")
    wp_q_h = nc.dram_tensor("wp_q", [QP_R, HID], F32, kind="ExternalInput")
    identb_h = nc.dram_tensor("ident_bf", [P, P], BF16, kind="ExternalInput")
    onesf_h = nc.dram_tensor("ones_f", [P, 4], F32, kind="ExternalInput")
    onesr_h = nc.dram_tensor("ones_r", [1, P], F32, kind="ExternalInput")
    onesb_h = nc.dram_tensor("ones_b", [P, 4], BF16, kind="ExternalInput")
    consts_h = nc.dram_tensor("consts", [P, 4], F32, kind="ExternalInput")
    sel_h = nc.dram_tensor("sel_bf", [2 * NST, 2 * NST * P], BF16, kind="ExternalInput")

    out_h = nc.dram_tensor("out_fm", [D, LH], F32, kind="ExternalOutput")
    DBG = bool(int(os.environ.get("MK_DEBUG", "0")))
    if DBG:
        dbg_xnT = nc.dram_tensor("dbg_xnT", [P, 1024], BF16, kind="ExternalOutput")
        dbg_a1 = nc.dram_tensor("dbg_a1", [P, D], BF16, kind="ExternalOutput")
        dbg_a2 = nc.dram_tensor("dbg_a2", [P, 16], F32, kind="ExternalOutput")
        dbg_a3 = nc.dram_tensor("dbg_a3", [P, 1024], BF16, kind="ExternalOutput")
        dbg_a4 = nc.dram_tensor("dbg_a4", [P, D], F32, kind="ExternalOutput")
        dbg_xc = nc.dram_tensor("dbg_xc", [P, 1024], BF16, kind="ExternalOutput")
        dbg_sr = nc.dram_tensor("dbg_sr", [P, 1024], BF16, kind="ExternalOutput")
        dbg_xd = nc.dram_tensor("dbg_xd", [XDR, 512], F32, kind="ExternalOutput")
        dbg_dl = nc.dram_tensor("dbg_dl", [P, 512], F32, kind="ExternalOutput")
        dbg_y = nc.dram_tensor("dbg_y", [P, 512], BF16, kind="ExternalOutput")
        dbg_mo = nc.dram_tensor("dbg_mo", [P, 512], F32, kind="ExternalOutput")
        dbg_att = nc.dram_tensor("dbg_att", [P, 512], F32, kind="ExternalOutput")
        dbg_x2n = nc.dram_tensor("dbg_x2n", [P, 512], BF16, kind="ExternalOutput")
        dbg_fo = nc.dram_tensor("dbg_fo", [P, 256], F32, kind="ExternalOutput")
        dbg_wg = nc.dram_tensor("dbg_wg", [P, D], BF16, kind="ExternalOutput")
        dbg_wp2 = nc.dram_tensor("dbg_wp2", [P, HID], BF16, kind="ExternalOutput")

    xc_sp = nc.dram_tensor("xc_sp", [DH, L], BF16)
    x2_sp = nc.dram_tensor("x2_sp", [D, LH], F32)
    sres_sp = nc.dram_tensor("sres_sp", [DH, L], BF16)
    cc_xd_in = nc.dram_tensor("cc_xd_in", [XDR, L], F32)
    cc_xd_out = nc.dram_tensor("cc_xd_out", [XDR, L], F32)
    cc_y_in = nc.dram_tensor("cc_y_in", [2 * DH, LH], BF16)
    cc_y_ag = nc.dram_tensor("cc_y_ag", [2 * 2 * DH, LH], BF16)
    cc_gu_in = nc.dram_tensor("cc_gu_in", [D, QGU_R], BF16)
    cc_gu_out = nc.dram_tensor("cc_gu_out", [8 * D, QGU_R], BF16)
    cc_wp_in = nc.dram_tensor("cc_wp_in", [HID, QP_R], BF16)
    cc_wp_out = nc.dram_tensor("cc_wp_out", [8 * HID, QP_R], BF16)

    groups2 = [[0, 1], [2, 3], [4, 5], [6, 7]]
    groups8 = [list(range(8))]

    SB_LIM = nc.SBUF_PARTITION_SIZE_BYTES

    def sb(name, pdim, nel, dtype, off):
        nbytes = nel * (2 if dtype == BF16 else 4)
        assert off % 32 == 0 and off + nbytes <= SB_LIM, f"SBUF {name}: {off}+{nbytes}"
        return nc.alloc_sbuf_tensor_at(name, [pdim, nel], dtype, offset=off)

    o = 17 * KB + 512
    def nxt(nbytes):
        nonlocal o
        r = o
        o = (o + nbytes + 63) & ~63
        return r
    identb = sb("identb", P, P, BF16, nxt(256))
    onesf = sb("onesf", P, 4, F32, nxt(16))
    onesr = sb("onesr", 1, P, F32, nxt(P * 4))
    onesb = sb("onesb", P, 4, BF16, nxt(32))
    constsb = sb("constsb", P, 4, F32, nxt(16))
    vecs = sb("vecs", P, NG * 4, F32, nxt(NG * 16))
    convw = sb("convw", P, NG * 4, F32, nxt(NG * 16))
    a_sb = sb("a_sb", P, NG * NST, F32, nxt(NG * NST * 4))
    normw_c = sb("normw_c", P, KD, F32, nxt(KD * 4))
    bc4a = sb("bc4a", P, FCH, F32, nxt(FCH * 4))
    bc4b = sb("bc4b", P, FCH, F32, nxt(FCH * 4))
    outD = [sb(f"outD{i}", P, FCH, F32, nxt(FCH * 4)) for i in range(2)]
    assert o <= 25 * KB, o

    o = 25 * KB
    browb = sb("browb", 2 * NST, L, BF16, nxt(L * 2))
    delta_r = sb("delta_r", P, L, BF16, nxt(L * 2))
    state = sb("state", P, NG * NST, F32, nxt(NG * NST * 4))
    dtprojT = sb("dtprojT", P, DH, BF16, nxt(DH * 2))
    xprojT = sb("xprojT", P, NG * XDR, BF16, nxt(NG * XDR * 2))
    assert o <= 44 * KB, o
    R0 = 44 * KB

    o = R0
    x_nT = sb("x_nT", P, KD * L, BF16, nxt(KD * L * 2))
    xin = [sb(f"xin{i}", P, D, F32, nxt(D * 4)) for i in range(2)]
    sq_scr = sb("sq_scr", P, D, F32, nxt(D * 4))
    xn_tm = [sb(f"xntm{i}", P, D, BF16, nxt(D * 2)) for i in range(2)]
    wslab = [sb(f"wslab{i}", P, KD * P, BF16, nxt(KD * P * 2)) for i in range(3)]
    xc_pre = [sb(f"xc_pre{i}", P, L + 8, BF16, nxt((L + 8) * 2)) for i in range(2)]
    conv_acc = sb("conv_acc", P, L, F32, nxt(L * 4))
    xcO = [sb(f"xcO{i}", P, L, BF16, nxt(L * 2)) for i in range(2)]
    sresO = [sb(f"sresO{i}", P, L, BF16, nxt(L * 2)) for i in range(2)]
    colt = sb("colt", P, 16, F32, nxt(64))
    acols = sb("acols", P, 8 * NTT, F32, nxt(8 * NTT * 4))
    xdacc1 = sb("xdacc1", P, L, F32, nxt(L * 4))
    xdacc2 = sb("xdacc2", 32, L, F32, nxt(L * 4))
    xcA2 = [sb(f"xcA2{i}", P, L, BF16, nxt(L * 2)) for i in range(2)]
    brow = sb("brow", 2 * NST, L, F32, nxt(L * 4))
    assert o <= SB_LIM, o

    o = R0
    b_bc = sb("b_bc", P, NST * LH, BF16, nxt(NST * LH * 2))
    c_bc = sb("c_bc", P, NST * LH, BF16, nxt(NST * LH * 2))
    dAb = [sb(f"dAb{i}", P, LH, BF16, nxt(LH * 2)) for i in range(2)]
    dBu = [sb(f"dBu{i}", P, LH, BF16, nxt(LH * 2)) for i in range(2)]
    hbf = [sb(f"hbf{i}", P, LH, BF16, nxt(LH * 2)) for i in range(2)]
    deltg = sb("deltg", P, LH, F32, nxt(LH * 4))
    ug = sb("ug", P, LH, BF16, nxt(LH * 2))
    xcB = [sb(f"xcB{i}", P, LH, BF16, nxt(LH * 2)) for i in range(2)]
    srB = [sb(f"srB{i}", P, LH, BF16, nxt(LH * 2)) for i in range(2)]
    sp_t1 = sb("sp_t1", P, LH, F32, nxt(LH * 4))
    sp_t2 = sb("sp_t2", P, LH, F32, nxt(LH * 4))
    yf32 = sb("yf32", P, LH, F32, nxt(LH * 4))
    ybf = [sb(f"ybf{i}", P, LH, BF16, nxt(LH * 2)) for i in range(2)]
    selb = sb("selb", 2 * NST, 2 * NST * P, BF16, nxt(2 * NST * P * 2))
    b_bc2 = sb("b_bc2", P, NST * LH, BF16, nxt(NST * LH * 2))
    c_bc2 = sb("c_bc2", P, NST * LH, BF16, nxt(NST * LH * 2))
    assert o <= SB_LIM, o

    QCW = 1024
    o = R0 + 164 * KB
    q_wbf = [sb(f"q_wbf{i}", P, QCW, BF16, nxt(QCW * 2)) for i in range(2)]
    q_wh = sb("q_wh", P, QCW, BF16, nxt(QCW * 2))
    q_t2 = sb("q_t2", P, QCW, BF16, nxt(QCW * 2))
    q_fin = sb("q_fin", P, QCW, BF16, nxt(QCW * 2))
    q_red = sb("q_red", P, 64, F32, nxt(256))
    q_scb = sb("q_scb", P, 64, BF16, nxt(128))
    q_scf = sb("q_scf", P, 64, F32, nxt(256))
    q_inv = sb("q_inv", P, 64, F32, nxt(256))
    q_ev = sb("q_ev", P, 1024, BF16, nxt(2048))
    assert o <= R0 + 177 * KB, o

    y_sb = sb("y_sb", P, 2 * NG * LH, BF16, R0)
    x2 = sb("x2", P, KD * LH, F32, R0)
    oslab = [sb(f"oslab{i}", P, KDIN * P, BF16, R0 + 64 * KB + i * 8 * KB) for i in range(2)]
    bigbc1 = sb("bigbc1", P, LH, F32, R0 + 64 * KB)
    bigbc2 = sb("bigbc2", P, LH, F32, R0 + 72 * KB)
    mo = sb("mo", P, KD * LH, F32, R0 + 80 * KB)
    x2n = sb("x2n", P, KD * LH, BF16, R0 + 80 * KB)
    h_sb = sb("h_sb", P, NHT * 512, BF16, R0)
    fo = sb("fo", P, KD * FCH, F32, R0 + 64 * KB)
    x2ch = sb("x2ch", P, KD * FCH, F32, R0 + 112 * KB)
    guslab = [sb(f"guslab{i}", P, KD * P, BF16, R0 + 128 * KB + i * 4 * KB) for i in range(4)]
    wpslab = [sb(f"wpslab{i}", P, (NKH // 2) * P, BF16, R0 + 144 * KB + i * 8 * KB)
              for i in range(2)]
    sqD = sb("sqD", P, 512, F32, R0 + 160 * KB)
    sqE = sb("sqE", P, 512, F32, R0 + 177 * KB)
    sqDB = [sqD, sqE]
    srow = sb("srow", 1, 2 * 512, F32, R0 + 162 * KB)
    srow2 = sb("srow2", 1, 512 + 2 * FCH, F32, R0 + 166 * KB)
    h2b = [sb(f"h2big{i}", P, 512, BF16, R0 + 170 * KB + i * KB) for i in range(2)]
    assert R0 + 179 * KB <= SB_LIM, R0 + 179 * KB
    nc.sbuf_base = R0 + 179 * KB

    pbk = [nc.alloc_psum_tensor(f"pb{i}", [P, 512], F32) for i in range(5)]
    pT = nc.alloc_psum_tensor("pT", [P, 1024], BF16)
    pR = nc.alloc_psum_tensor("pR", [1, 512], F32)
    pM = nc.alloc_psum_tensor("pM", [32, 512], F32)

    sy = Sem(nc.alloc_semaphore("sy"))
    gp = Sem(nc.alloc_semaphore("gp"))
    te = Sem(nc.alloc_semaphore("te"))
    ve = Sem(nc.alloc_semaphore("ve"))
    sc = Sem(nc.alloc_semaphore("sc"))
    cc = Sem(nc.alloc_semaphore("cc"))
    sd = Sem(nc.alloc_semaphore("sd"))
    sy2 = Sem(nc.alloc_semaphore("sy2"))
    gp2 = Sem(nc.alloc_semaphore("gp2"))
    sd2 = Sem(nc.alloc_semaphore("sd2"))
    vch = Sem(nc.alloc_semaphore("vch"))
    sch = Sem(nc.alloc_semaphore("sch"))
    PL = Plan()
    PL.set_dma_sems("sync", [sy, sy2])
    PL.set_dma_sems("gpsimd", [gp, gp2])
    PL.set_dma_sems("scalar", [sd, sd2])

    def VC(E, inst):
        vch.n += 1
        inst.then_inc(vch.h, 1)
        E.wait_ge(vch.h, vch.n)
        return inst

    def SC(E, inst):
        sch.n += 1
        inst.then_inc(sch.h, 1)
        E.wait_ge(sch.h, sch.n)
        return inst

    def dma(q, sem, out, in_, waits=()):
        return PL.step(q, lambda E, o_=out, i_=in_: E.dma_start(out=o_, in_=i_),
                       waits=waits, sem=sem, inck=16)

    t0 = []
    t0.append(dma("sync", sy, identb[:], identb_h.ap()))
    t0.append(dma("sync", sy, onesf[:], onesf_h.ap()))
    t0.append(dma("sync", sy, onesr[:], onesr_h.ap()))
    t0.append(dma("sync", sy, onesb[:], onesb_h.ap()))
    t0.append(dma("sync", sy, constsb[:], consts_h.ap()))
    nc.const_aps.register(F32, EPS_LN, constsb[:, 0:1])
    nc.const_aps.register(F32, 1.0, constsb[:, 1:2])
    nc.const_aps.register(F32, EPS_RMS, constsb[:, 2:3])
    nc.const_aps.register(F32, 0.0, constsb[:, 3:4])
    t0.append(dma("sync", sy, vecs[:].rearrange("p (g v) -> p g v", v=4),
                  vecs_h.ap().rearrange("g p v -> p g v")))
    t0.append(dma("sync", sy, convw[:].rearrange("p (g v) -> p g v", v=4),
                  convw_h.ap().rearrange("(g p) v -> p g v", p=P)))
    t0.append(dma("sync", sy, a_sb[:].rearrange("p (g n) -> p g n", n=NST),
                  alog_h.ap().rearrange("(g p) n -> p g n", p=P)))
    t0.append(dma("sync", sy, normw_c[:], normw_h.ap()))
    t0.append(dma("gpsimd", gp, dtprojT[:], dtprojT_h.ap()))
    t0.append(dma("gpsimd", gp, xprojT[:].rearrange("p (g m) -> p g m", m=XDR),
                  xprojT_h.ap().rearrange("(g p) m -> p g m", p=P)))
    tk = PL.step("scalar", lambda E: E.activation(a_sb[:], a_sb[:], AF.Exp, bias=0.0, scale=1.0),
                 waits=t0, sem=sc)
    tk_acst = PL.step("vector", lambda E: E.tensor_scalar_mul(a_sb[:], a_sb[:], -1.0),
                      waits=[tk], sem=ve)

    qchunks = []
    for t in range(QGU_R // P):
        for ccol in range(D // QCW):
            qchunks.append((wgu_q_h, t * P, ccol * QCW, cc_gu_in, t * P))
    for t in range(QP_R // P):
        for ccol in range(HID // QCW):
            qchunks.append((wp_q_h, t * P, ccol * QCW, cc_wp_in, t * P))
    NGR = QCW // GQ
    NKB = QCW // P
    NTG = NKB // 8

    _qstate = dict(prev_vec=None, prev_vec2=None, prev_tr=None, prev_wr=None, prev_wr2=None)
    PT_G = [None]

    def emit_qchunk(ci):
        src, r0, c0, dst, dr0 = qchunks[ci]
        qb = q_wbf[ci % 2]
        prev_vec = _qstate["prev_vec"]; prev_tr = _qstate["prev_tr"]; prev_wr = _qstate["prev_wr"]
        tk_ld = PL.step("gpsimd",
                        lambda E, s=src, r=r0, c=c0, b=qb: E.dma_start(out=b[:], in_=s[r:r + P, c:c + QCW]),
                        waits=[_qstate["prev_vec2"]], sem=gp, inck=16)

        def v1(E, b=qb):
            VC(E, E.tensor_reduce(q_red[:, 0:NGR], b[:].rearrange("p (g k) -> p g k", k=GQ),
                                  AX.X, OP.add, apply_absolute_value=True))
            VC(E, E.tensor_scalar(q_scb[:, 0:NGR], q_red[:, 0:NGR], 1.0 / GQ, 1e-8, OP.mult, OP.max))
            return E.tensor_copy(q_scf[:, 0:NGR], q_scb[:, 0:NGR])
        tk_sc1 = PL.step("vector", v1, waits=[tk_ld, prev_vec], sem=ve)

        def s1(E):
            SC(E, E.activation(q_inv[:, 0:NGR], q_scf[:, 0:NGR], AF.Ln, bias=0.0, scale=1.0))
            return E.activation(q_inv[:, 0:NGR], q_inv[:, 0:NGR], AF.Exp, bias=0.0, scale=-1.0)
        tk_inv = PL.step("scalar", s1, waits=[tk_sc1], sem=sc)

        def v2(E, b=qb):
            for g in range(NGR):
                i = E.tensor_scalar_mul(q_wh[:, g * GQ:(g + 1) * GQ], b[:, g * GQ:(g + 1) * GQ],
                                        q_inv[:, g:g + 1])
            VC(E, i)
            VC(E, E.tensor_scalar(sqD[:], q_wh[:, 0:512], RND, -RND, OP.add, OP.add))
            VC(E, E.tensor_scalar(sqE[:], q_wh[:, 512:1024], RND, -RND, OP.add, OP.add))
            VC(E, E.tensor_scalar(q_wh[:, 0:512], sqD[:], 1.0, -1.0, OP.min, OP.max))
            VC(E, E.tensor_scalar(q_wh[:, 512:1024], sqE[:], 1.0, -1.0, OP.min, OP.max))
            for g in range(NGR):
                i = E.tensor_scalar_mul(q_t2[:, g * GQ:(g + 1) * GQ], q_wh[:, g * GQ:(g + 1) * GQ],
                                        q_scf[:, g:g + 1])
            VC(E, i)
            VC(E, E.tensor_sub(q_fin[:], q_t2[:], b[:]))
            return E.tensor_add(q_fin[:], b[:], q_fin[:])
        prev_vec = PL.step("vector", v2, waits=[tk_inv, prev_tr], sem=ve)
        _qstate["prev_vec2"] = _qstate["prev_vec"]
        _qstate["prev_vec"] = prev_vec

        for tg in range(NTG):
            def tr(E, base=tg * 8):
                i = None
                for k8 in range(8):
                    i = E.transpose(pT[:, k8 * P:(k8 + 1) * P],
                                    q_fin[:, (base + k8) * P:(base + k8 + 1) * P], identb[:])
                return i
            prev_tr = PL.step("tensor", tr, waits=[prev_vec, prev_wr, PT_G[0]], sem=te)
            _qstate["prev_tr"] = prev_tr
            tk_ev = PL.step("scalar", lambda E: E.copy(q_ev[:], pT[:]), waits=[prev_tr], sem=sc)
            PT_G[0] = tk_ev

            def wr(E, d=dst, dd=dr0, c=c0, base=tg * 8):
                return E.dma_start(
                    out=d.ap().rearrange("(kb p) r -> p kb r", p=P)[:, (c // P) + base:(c // P) + base + 8,
                                                                   dd:dd + P],
                    in_=q_ev[:].rearrange("p (kb r) -> p kb r", r=P))
            prev_wr = PL.step("gpsimd", wr, waits=[tk_ev], sem=gp, inck=16)
            _qstate["prev_wr2"] = _qstate["prev_wr"]
            _qstate["prev_wr"] = prev_wr
        if ci == len(qchunks) - 1:
            emit_q_ags()

    AGTOK = {}

    def emit_q_ags():
        AGTOK["ag1"] = PL.step("gpsimd",
                     lambda E: E.collective_compute("AllGather", OP.bypass, replica_groups=groups8,
                                                    ins=[cc_gu_in[:]], outs=[cc_gu_out[:]]),
                     waits=[_qstate["prev_wr"], _qstate["prev_wr2"]], sem=cc, inck=1)
        AGTOK["ag2"] = PL.step("gpsimd",
                     lambda E: E.collective_compute("AllGather", OP.bypass, replica_groups=groups8,
                                                    ins=[cc_wp_in[:]], outs=[cc_wp_out[:]]),
                     waits=(), sem=cc, inck=1)

    NQ_IN_A1 = min(len(qchunks), NTT)
    p1_cons = [None] * NTT
    for tt in range(NTT):
        bi = tt % 2
        war = [p1_cons[tt - 2]] if tt >= 2 else []
        tk_x = PL.step("sync", lambda E, t=tt, b=bi: E.dma_start(out=xin[b][:], in_=x_h[t * P:(t + 1) * P, :]),
                       waits=war, sem=sy, inck=16)
        tk_sq = PL.step("scalar",
                        lambda E, b=bi, t=tt: E.activation(sq_scr[:], xin[b][:], AF.Square, bias=0.0,
                                                           scale=1.0, accum_out=acols[:, NTT + t:NTT + t + 1]),
                        waits=[tk_x], sem=sc)
        tk_rs = PL.step("vector",
                        lambda E, b=bi, t=tt: E.reduce_sum(acols[:, t:t + 1], xin[b][:], AX.X),
                        waits=[tk_x], sem=ve)
        p1_cons[tt] = tk_rs if tt % 2 else tk_sq
        if tt < NQ_IN_A1:
            emit_qchunk(tt)
    p1_done = [tk_sq, tk_rs]

    CS, CQ, CNM, CV, CT, CRS, CR2, CSK = (acols[:, i * NTT:(i + 1) * NTT] for i in range(8))

    def p2v1(E):
        VC(E, E.tensor_scalar_mul(CNM, CS, -1.0 / D))
        VC(E, E.tensor_mul(CT, CNM, CNM))
        return E.scalar_tensor_tensor(CV, CQ, 1.0 / D, CT, OP.mult, OP.subtract)
    tk_p2a = PL.step("vector", p2v1, waits=p1_done, sem=ve)

    def p2s1(E):
        SC(E, E.activation(CT, CV, AF.Ln, bias=EPS_LN, scale=1.0))
        return E.activation(CRS, CT, AF.Exp, bias=0.0, scale=-0.5)
    tk_p2b = PL.step("scalar", p2s1, waits=[tk_p2a], sem=sc)

    def p2v2(E):
        VC(E, E.tensor_mul(CT, CV, CRS))
        return E.tensor_mul(CT, CT, CRS)
    tk_p2c = PL.step("vector", p2v2, waits=[tk_p2b], sem=ve)

    def p2s2(E):
        SC(E, E.activation(CT, CT, AF.Ln, bias=EPS_LN, scale=1.0))
        return E.activation(CR2, CT, AF.Exp, bias=0.0, scale=-0.5)
    tk_p2d = PL.step("scalar", p2s2, waits=[tk_p2c], sem=sc)
    tk_p2e = PL.step("vector", lambda E: E.tensor_mul(CSK, CRS, CR2), waits=[tk_p2d], sem=ve)

    evac_of_tt = []
    ap_of_tt = []
    for tt in range(NTT):
        bi = tt % 2
        war = [ap_of_tt[tt - 2]] if tt >= 2 else [tk_p2a]
        tk_x = PL.step("sync", lambda E, t=tt, b=bi: E.dma_start(out=xin[b][:], in_=x_h[t * P:(t + 1) * P, :]),
                       waits=war, sem=sy, inck=16)

        def vapply(E, b=bi, t=tt):
            return E.tensor_scalar(xn_tm[b][:], xin[b][:], acols[:, 2 * NTT + t:2 * NTT + t + 1],
                                   acols[:, 7 * NTT + t:7 * NTT + t + 1], OP.add, OP.mult)
        tk_ap = PL.step("vector", vapply,
                        waits=[tk_x, tk_p2e, (evac_of_tt[tt - 2] if tt >= 2 else None)], sem=ve)
        ap_of_tt.append(tk_ap)

        last_e = None
        for grp in range(KD // 8):
            def trp(E, b=bi, g=grp):
                i = None
                for k8 in range(8):
                    i = E.transpose(pT[:, k8 * P:(k8 + 1) * P],
                                    xn_tm[b][:, (g * 8 + k8) * P:(g * 8 + k8 + 1) * P], identb[:])
                return i
            tk_tp = PL.step("tensor", trp, waits=[tk_ap, PT_G[0]], sem=te)

            def evac(E, t=tt, g=grp):
                return E.copy(
                    x_nT[:].rearrange("p (k t2) -> p k t2", t2=L)[:, g * 8:(g + 1) * 8, t * P:(t + 1) * P],
                    pT[:].rearrange("p (k t2) -> p k t2", t2=P))
            last_e = PL.step("scalar", evac, waits=[tk_tp], sem=sc)
            PT_G[0] = last_e
        evac_of_tt.append(last_e)


    nw_last = None
    for k in range(KD):
        nw_last = PL.step("vector",
                          lambda E, kk=k: E.tensor_scalar_mul(x_nT[:, kk * L:(kk + 1) * L],
                                                              x_nT[:, kk * L:(kk + 1) * L],
                                                              normw_c[:, kk:kk + 1]),
                          waits=[evac_of_tt[-1]], sem=ve)
    xnT_ready = nw_last
    if DBG:
        dma("sync", sy, dbg_xnT.ap(), x_nT[:, 0:1024], waits=[xnT_ready])

    def padf(E):
        E.memset(xc_pre[0][:, 0:8], 0.0)
        return E.memset(xc_pre[1][:, 0:8], 0.0)
    tk_pad = PL.step("vector", padf, waits=(), sem=ve)
    NO = 2 * NG
    slab_last_mm = [None, None, None]
    psum_cons = {}
    xproj_done = [None] * NG
    spill_xc = [None] * NG
    spill_sr = [None] * NG
    silu_done = [None] * NG
    sres_war = [None] * NG
    NQ_A2_BASE = NQ_IN_A1
    for ot in range(NO):
        bi = ot % 3
        if NQ_A2_BASE + ot < len(qchunks):
            emit_qchunk(NQ_A2_BASE + ot)

        def ldw(E, o_=ot, b=bi):
            return E.dma_start(out=wslab[b][:].rearrange("p (k m) -> p k m", m=P),
                               in_=w_inT_h.ap().rearrange("(k p) (ot m) -> ot p k m", p=P, m=P)[o_])
        tk_w = PL.step("gpsimd", ldw, waits=[slab_last_mm[bi]], sem=gp, inck=16)

        for nch in range(NCHL):
            bank = (ot * NCHL + nch) % 4

            def mms(E, b=bi, n=nch, bk=bank):
                i = None
                for k in range(KD):
                    i = E.matmul(pbk[bk][:], wslab[b][:, k * P:(k + 1) * P],
                                 x_nT[:, k * L + n * 512:k * L + (n + 1) * 512],
                                 start=(k == 0), stop=(k == KD - 1))
                return i
            tk_m = PL.step("tensor", mms, waits=[tk_w, xnT_ready, psum_cons.get(bank)], sem=te)
            if nch == NCHL - 1:
                slab_last_mm[bi] = tk_m

            if ot < NG:
                def ev(E, n=nch, bk=bank, b=ot % 2):
                    return E.copy(xc_pre[b][:, 8 + n * 512:8 + (n + 1) * 512], pbk[bk][:])
                tk_e = PL.step("scalar", ev,
                               waits=[tk_m, (silu_done[ot - 2] if ot >= 2 and nch == 0 else None)],
                               sem=sc)
            else:
                g = ot - NG

                def ev(E, n=nch, bk=bank, gg=g):
                    return E.activation(sresO[gg % 2][:, n * 512:(n + 1) * 512], pbk[bk][:],
                                        AF.Silu, bias=0.0, scale=1.0)
                tk_e = PL.step("scalar", ev,
                               waits=[tk_m, (spill_sr[g - 2] if g >= 2 and nch == 0 else None)],
                               sem=sc)
            psum_cons[bank] = tk_e

        if ot < NG:
            g = ot
            if g == 0:
                def mz(E):
                    E.memset(xdacc1[:], 0.0)
                    return E.memset(xdacc2[:], 0.0)
                tk_mz = PL.step("vector", mz, waits=(), sem=ve)
                prev_acc = tk_mz

            def cv(E, gg=g, b=g % 2):
                VC(E, E.tensor_scalar_mul(conv_acc[:], xc_pre[b][:, 5:5 + L], convw[:, gg * 4:gg * 4 + 1]))
                i = None
                for k in range(1, 4):
                    i = E.scalar_tensor_tensor(conv_acc[:], xc_pre[b][:, 5 + k:5 + k + L],
                                               convw[:, gg * 4 + k:gg * 4 + k + 1], conv_acc[:],
                                               OP.mult, OP.add)
                    if k < 3:
                        VC(E, i)
                return i
            tk_c = PL.step("vector", cv,
                           waits=[psum_cons[(ot * NCHL + NCHL - 1) % 4], tk_pad,
                                  (silu_done[g - 1] if g >= 1 else None)], sem=ve)

            def sl(E, gg=g):
                return E.activation(xcO[gg % 2][:], conv_acc[:], AF.Silu,
                                    bias=vecs[:, gg * 4:gg * 4 + 1], scale=1.0)
            tk_s = PL.step("scalar", sl,
                           waits=[tk_c, (spill_xc[g - 2] if g >= 2 else None),
                                  (xproj_done[g - 2] if g >= 2 else None)], sem=sc)
            silu_done[g] = tk_s
            spill_xc[g] = dma("sync", sy, xc_sp[g * P:(g + 1) * P, :], xcO[g % 2][:], waits=[tk_s])
            for nch in range(NCHL):
                def mmx(E, gg=g, b=g % 2, n=nch):
                    E.matmul(pbk[4][:], xprojT[:, gg * XDR:gg * XDR + P],
                             xcO[b][:, n * 512:(n + 1) * 512], start=True, stop=True)
                    return E.matmul(pM[:], xprojT[:, gg * XDR + P:gg * XDR + XDR],
                                    xcO[b][:, n * 512:(n + 1) * 512], start=True, stop=True)
                tk_mm2 = PL.step("tensor", mmx, waits=[tk_s, prev_acc], sem=te)

                def acc(E, n=nch):
                    E.tensor_add(xdacc1[:, n * 512:(n + 1) * 512],
                                 xdacc1[:, n * 512:(n + 1) * 512], pbk[4][:])
                    return E.tensor_add(xdacc2[:, n * 512:(n + 1) * 512],
                                        xdacc2[:, n * 512:(n + 1) * 512], pM[:])
                prev_acc = PL.step("vector", acc, waits=[tk_mm2], sem=ve)
                xproj_done[g] = tk_mm2
        else:
            g = ot - NG
            spill_sr[g] = dma("sync", sy, sres_sp[g * P:(g + 1) * P, :], sresO[g % 2][:],
                              waits=[psum_cons[(ot * NCHL + NCHL - 1) % 4]])

    if DBG:
        dma("sync", sy, dbg_xc.ap(), xc_sp[0:P, 0:1024], waits=[spill_xc[0]])
        dma("sync", sy, dbg_sr.ap(), sres_sp[0:P, 0:1024], waits=[spill_sr[0]])

    if DBG:
        dma("sync", sy, dbg_wg.ap(), cc_gu_out[0:P, :], waits=[AGTOK["ag1"]])

    tk_xw1 = dma("sync", sy, cc_xd_in[0:RANK, :], xdacc1[:], waits=[prev_acc])
    tk_xw2 = dma("sync", sy, cc_xd_in[RANK:XDR, :], xdacc2[:], waits=[prev_acc])
    tk_ar = PL.step("gpsimd",
                    lambda E: E.collective_compute("AllReduce", OP.add, replica_groups=groups2,
                                                   ins=[cc_xd_in[:]], outs=[cc_xd_out[:]]),
                    waits=[tk_xw1, tk_xw2], sem=cc, inck=1)
    tk_dr = dma("gpsimd", gp, delta_r[:], cc_xd_out[0:RANK, :], waits=[tk_ar])
    if DBG:
        dma("sync", sy, dbg_xd.ap(), cc_xd_out[:, 0:512], waits=[tk_ar])
    tk_br = dma("sync", sy, brow[:], cc_xd_out[RANK:XDR, :], waits=[tk_ar])
    tk_bb = PL.step("vector", lambda E: E.tensor_copy(browb[:], brow[:]), waits=[tk_br], sem=ve)

    if stop_after == "A":
        with nc.Block() as blk:
            PL.play(blk)
        return nc

    tk_selb = dma("sync", sy, selb[:], sel_h.ap(),
                  waits=[prev_acc, silu_done[NG - 1], spill_sr[NG - 1]])
    ywr = []
    fin_read = None
    tk_fb = None
    prev_tkd = None
    BCB = [(b_bc, c_bc), (b_bc2, c_bc2)]
    bc_ready_h = []
    bc_toks = []
    for hf in range(2):
        for i in range(NST):
            for nch in range(NCHB):
                def bc1(E, ii=i, n=nch, h=hf):
                    E.matmul(pbk[4][:], selb[:, ii * P:(ii + 1) * P],
                             browb[:, h * LH + n * 512:h * LH + (n + 1) * 512], start=True, stop=True)
                    return E.matmul(pbk[0][:], selb[:, (NST + ii) * P:(NST + ii + 1) * P],
                                    browb[:, h * LH + n * 512:h * LH + (n + 1) * 512],
                                    start=True, stop=True)
                tk_bm = PL.step("tensor", bc1,
                                waits=[tk_bb, tk_selb, (bc_toks[-1] if bc_toks else None),
                                       (prev_acc if not bc_toks else None)],
                                sem=te)

                def bc2(E, ii=i, n=nch, h=hf):
                    E.copy(BCB[h][0][:, ii * LH + n * 512:ii * LH + (n + 1) * 512], pbk[4][:])
                    return E.copy(BCB[h][1][:, ii * LH + n * 512:ii * LH + (n + 1) * 512],
                                  pbk[0][:])
                bc_toks.append(PL.step("scalar", bc2,
                                       waits=[tk_bm] + ([prev_acc, tk_bb] if not bc_toks else []),
                                       sem=sc))
        bc_ready_h.append(bc_toks[-1])

    for hf in range(2):
        bc_ready = bc_ready_h[hf]
        b_bch, c_bch = BCB[hf]
        for g in range(NG):
            bg = g % 2

            def dtm(E, gg=g, h=hf):
                E.matmul(pbk[2][:], dtprojT[:, gg * P:(gg + 1) * P],
                         delta_r[:, h * LH:h * LH + 512], start=True, stop=True)
                return E.matmul(pbk[3][:], dtprojT[:, gg * P:(gg + 1) * P],
                                delta_r[:, h * LH + 512:h * LH + 1024], start=True, stop=True)
            tk_dt = PL.step("tensor", dtm, waits=[tk_dr, (None if (hf, g) == (0, 0) else sp_done2)],
                            sem=te)

            def spl(E, gg=g):
                i = None
                for n in range(NCHB):
                    src = pbk[2 + n]
                    E.activation(sp_t1[:, n * 512:(n + 1) * 512], src[:], AF.Relu,
                                 bias=vecs[:, gg * 4 + 1:gg * 4 + 2], scale=1.0)
                    SC(E, E.activation(sp_t2[:, n * 512:(n + 1) * 512], src[:], AF.Abs,
                                       bias=vecs[:, gg * 4 + 1:gg * 4 + 2], scale=1.0))
                    SC(E, E.activation(sp_t2[:, n * 512:(n + 1) * 512], sp_t2[:, n * 512:(n + 1) * 512],
                                       AF.Exp, bias=0.0, scale=-1.0))
                    i = E.activation(sp_t2[:, n * 512:(n + 1) * 512], sp_t2[:, n * 512:(n + 1) * 512],
                                     AF.Ln, bias=1.0, scale=1.0)
                return i
            sp_done = PL.step("scalar", spl, waits=[tk_dt, prev_tkd], sem=sc)
            sp_done2 = sp_done

            tk_xc = PL.step("gpsimd", lambda E, gg=g, b=bg, h=hf: E.dma_start(
                out=xcB[b][:], in_=xc_sp[gg * P:(gg + 1) * P, h * LH:(h + 1) * LH]),
                waits=[spill_xc[g], (ywr[-2] if len(ywr) >= 2 else None)], sem=gp, inck=16)
            tk_sr = PL.step("gpsimd", lambda E, gg=g, b=bg, h=hf: E.dma_start(
                out=srB[b][:], in_=sres_sp[gg * P:(gg + 1) * P, h * LH:(h + 1) * LH]),
                waits=[spill_sr[g], (ywr[-2] if len(ywr) >= 2 else None)], sem=gp, inck=16)

            tk_d = PL.step("vector", lambda E: E.tensor_add(deltg[:], sp_t1[:], sp_t2[:]),
                           waits=[sp_done, bc_ready], sem=ve)
            prev_tkd = tk_d
            tk_u = PL.step("vector", lambda E, b=bg: E.tensor_mul(ug[:], deltg[:], xcB[b][:]),
                           waits=[tk_d, tk_xc], sem=ve)
            if DBG and hf == 0 and g == 0:
                dma("sync", sy, dbg_dl.ap(), deltg[:, 0:512], waits=[tk_u])

            scan_t = [None, None]
            hC_t = [None, None]
            ya_t = [None, None]
            for i in range(NST):
                b2 = i % 2

                def dac(E, gg=g, ii=i, b=b2):
                    return E.activation(dAb[b][:], deltg[:], AF.Exp, bias=0.0,
                                        scale=a_sb[:, gg * NST + ii:gg * NST + ii + 1])
                tk_da = PL.step("scalar", dac, waits=[tk_d, scan_t[b2]], sem=sc)

                def dbu(E, ii=i, b=b2):
                    return E.tensor_mul(dBu[b][:], ug[:], b_bch[:, ii * LH:(ii + 1) * LH])
                tk_db = PL.step("vector", dbu, waits=[tk_u, ya_t[b2]], sem=ve)

                def scn(E, gg=g, ii=i, b=b2, h=hf):
                    init = 0.0 if h == 0 else state[:, gg * NST + ii:gg * NST + ii + 1]
                    return E.tensor_tensor_scan(hbf[b][:], dAb[b][:], dBu[b][:], init,
                                                OP.mult, OP.add)
                tk_sn = PL.step("vector", scn, waits=[tk_da, tk_db, hC_t[b2]], sem=ve)
                scan_t[b2] = tk_sn

                tk_ss = None
                if hf == 0:
                    tk_ss = PL.step("scalar",
                                    lambda E, gg=g, ii=i, b=b2: E.copy(
                                        state[:, gg * NST + ii:gg * NST + ii + 1],
                                        hbf[b][:, LH - 1:LH]),
                                    waits=[tk_sn], sem=sc)

                def hc(E, ii=i, b=b2):
                    return E.tensor_mul(dBu[b][:], hbf[b][:], c_bch[:, ii * LH:(ii + 1) * LH])
                tk_hc = PL.step("vector", hc, waits=[tk_sn], sem=ve)
                hC_t[b2] = tk_hc

                def ya(E, ii=i, b=b2):
                    E.matmul(pbk[0][:], identb[:], dBu[b][:, 0:512], start=(ii == 0),
                             stop=(ii == NST - 1))
                    return E.matmul(pbk[1][:], identb[:], dBu[b][:, 512:1024], start=(ii == 0),
                                    stop=(ii == NST - 1))
                tk_ya = PL.step("tensor", ya,
                                waits=[tk_hc, tk_ss, (fin_read if i == 0 else None)], sem=te)
                ya_t[b2] = tk_ya

            def fin(E, gg=g, b=bg):
                E.scalar_tensor_tensor(yf32[:, 0:512], xcB[b][:, 0:512],
                                       vecs[:, gg * 4 + 2:gg * 4 + 3], pbk[0][:], OP.mult, OP.add)
                return E.scalar_tensor_tensor(yf32[:, 512:1024], xcB[b][:, 512:1024],
                                              vecs[:, gg * 4 + 2:gg * 4 + 3], pbk[1][:],
                                              OP.mult, OP.add)
            fin_read = PL.step("vector", fin, waits=[ya_t[(NST - 1) % 2], tk_sr, tk_fb], sem=ve)

            tk_fb = PL.step("vector", lambda E, b=bg: E.tensor_mul(ybf[b][:], yf32[:], srB[b][:]),
                            waits=[fin_read, (ywr[-2] if len(ywr) >= 2 else None)], sem=ve)
            ywr.append(dma("sync", sy, cc_y_in[hf * DH + g * P:hf * DH + (g + 1) * P, :],
                           ybf[bg][:], waits=[tk_fb]))

    if DBG:
        dma("sync", sy, dbg_y.ap(), cc_y_in[0:P, 0:512], waits=[ywr[0]])
    tk_ag_h0 = PL.step("gpsimd",
                       lambda E: E.collective_compute("AllGather", OP.bypass, replica_groups=groups2,
                                                      ins=[cc_y_in[0:DH, :]], outs=[cc_y_ag[0:2 * DH, :]]),
                       waits=[ywr[NG - 1], ywr[NG - 2]], sem=cc, inck=1)
    tk_a2a = PL.step("gpsimd",
                     lambda E: E.collective_compute("AllGather", OP.bypass, replica_groups=groups2,
                                                    ins=[cc_y_in[DH:2 * DH, :]],
                                                    outs=[cc_y_ag[2 * DH:4 * DH, :]]),
                     waits=[ywr[-1], ywr[-2]], sem=cc, inck=1)

    if stop_after == "B":
        with nc.Block() as blk:
            PL.play(blk)
        return nc

    def mk_yload(r):
        def yload(E):
            pid = E.partition_id()
            off = (pid % 2) * (2 * DH * LH) + r * (DH * LH)
            ap = bass.AP(cc_y_ag, off, [[LH, P], [P * LH, NG], [1, LH]])
            return E.dma_start(
                out=y_sb[:].rearrange("p (k t) -> p k t", t=LH)[:, r * NG:(r + 1) * NG, :],
                in_=ap)
        return yload
    tk_yl0 = PL.step("gpsimd", mk_yload(0), waits=[tk_ag_h0, tk_a2a, tk_fb, fin_read], sem=gp, inck=16)
    tk_yl = PL.step("gpsimd", mk_yload(1), waits=(), sem=gp, inck=16)
    oslab_last = [None, None]
    opsum = {}
    mo_evs = []
    for mt in range(NMT):
        bi = mt % 2
        tk_o = PL.step("gpsimd",
                       lambda E, m=mt, b=bi: E.dma_start(
                           out=oslab[b][:].rearrange("p (k m) -> p k m", m=P),
                           in_=oprojT_h.ap().rearrange("(k p) (mt m) -> mt p k m", p=P, m=P)[m]),
                       waits=[oslab_last[bi], (tk_a2a if mt < 2 else None)], sem=gp, inck=16)
        for nch in range(NCHB):
            bank = (mt * NCHB + nch) % 4

            def mmo(E, b=bi, n=nch, bk=bank):
                i = None
                for k in range(KDIN):
                    i = E.matmul(pbk[bk][:], oslab[b][:, k * P:(k + 1) * P],
                                 y_sb[:, k * LH + n * 512:k * LH + (n + 1) * 512],
                                 start=(k == 0), stop=(k == KDIN - 1))
                return i
            tk_m = PL.step("tensor", mmo, waits=[tk_o, tk_yl, opsum.get(bank)], sem=te)
            if nch == NCHB - 1:
                oslab_last[bi] = tk_m
            tk_e = PL.step("scalar",
                           lambda E, m=mt, n=nch, bk=bank: E.copy(
                               mo[:, m * LH + n * 512:m * LH + (n + 1) * 512], pbk[bk][:]),
                           waits=[tk_m], sem=sc)
            opsum[bank] = tk_e
            mo_evs.append(tk_e)
    mo_ready = mo_evs[-1]

    if stop_after == "C":
        with nc.Block() as blk:
            PL.play(blk)
        return nc

    if DBG:
        dma("sync", sy, dbg_mo.ap(), mo[:, 0:512], waits=[mo_ready])
    tk_xT = dma("sync", sy, x2[:].rearrange("p (k t) -> p k t", t=LH),
                xT_h.ap().rearrange("(k p) t -> p k t", p=P), waits=[mo_ready, tk_yl])

    def ln_stats(src_fn, width, wtoks, rowbuf, mslot_off, sslot_off, inv_off=None):
        assert width <= 512
        mm_toks = []
        for k in range(KD):
            def sqf(E, kk=k, b=k % 2):
                return E.activation(sqDB[b][:, 0:width], src_fn(kk), AF.Square, bias=0.0,
                                    scale=1.0)
            tk_sq = PL.step("scalar", sqf,
                            waits=list(wtoks) + ([mm_toks[-2]] if len(mm_toks) >= 2 else []),
                            sem=sc)

            def mms(E, kk=k, b=k % 2):
                E.matmul(pR[:, 0:width], onesf[:, 0:1], src_fn(kk),
                         start=(kk == 0), stop=(kk == KD - 1))
                return E.matmul(pM[0:1, 0:width], onesf[:, 0:1], sqDB[b][:, 0:width],
                                start=(kk == 0), stop=(kk == KD - 1))
            mm_toks.append(PL.step("tensor", mms, waits=list(wtoks) + [tk_sq], sem=te))

        def evr(E):
            E.copy(rowbuf[0:1, mslot_off:mslot_off + width], pR[:, 0:width])
            return E.copy(rowbuf[0:1, sslot_off:sslot_off + width], pM[0:1, 0:width])
        tk_ev = PL.step("scalar", evr, waits=[mm_toks[-1]], sem=sc)

        def rowm(E):
            VC(E, E.tensor_scalar_mul(rowbuf[0:1, mslot_off:mslot_off + width],
                                      rowbuf[0:1, mslot_off:mslot_off + width], -1.0 / D))
            VC(E, E.tensor_mul(sqD[0:1, 0:width], rowbuf[0:1, mslot_off:mslot_off + width],
                               rowbuf[0:1, mslot_off:mslot_off + width]))
            return E.scalar_tensor_tensor(rowbuf[0:1, sslot_off:sslot_off + width],
                                          rowbuf[0:1, sslot_off:sslot_off + width], 1.0 / D,
                                          sqD[0:1, 0:width], OP.mult, OP.subtract)
        tk_rm = PL.step("vector", rowm, waits=[tk_ev], sem=ve)
        if inv_off is not None:
            def pre(E):
                VC(E, E.tensor_mul(sqD[0:1, 0:width], rowbuf[0:1, inv_off:inv_off + width],
                                   rowbuf[0:1, inv_off:inv_off + width]))
                return E.tensor_mul(rowbuf[0:1, sslot_off:sslot_off + width],
                                    rowbuf[0:1, sslot_off:sslot_off + width], sqD[0:1, 0:width])
            tk_rm = PL.step("vector", pre, waits=[tk_rm], sem=ve)

        def rs(E):
            SC(E, E.activation(rowbuf[0:1, sslot_off:sslot_off + width],
                               rowbuf[0:1, sslot_off:sslot_off + width], AF.Ln, bias=EPS_LN,
                               scale=1.0))
            return E.activation(rowbuf[0:1, sslot_off:sslot_off + width],
                                rowbuf[0:1, sslot_off:sslot_off + width], AF.Exp, bias=0.0,
                                scale=-0.5)
        tk_rs = PL.step("scalar", rs, waits=[tk_rm], sem=sc)
        vw = [tk_rs]
        if inv_off is not None:
            tk_iv = PL.step("vector",
                            lambda E: E.tensor_mul(rowbuf[0:1, sslot_off:sslot_off + width],
                                                   rowbuf[0:1, sslot_off:sslot_off + width],
                                                   rowbuf[0:1, inv_off:inv_off + width]),
                            waits=[tk_rs], sem=ve)
            vw = [tk_iv]
        return PL.step("vector",
                       lambda E: E.tensor_mul(rowbuf[0:1, mslot_off:mslot_off + width],
                                              rowbuf[0:1, mslot_off:mslot_off + width],
                                              rowbuf[0:1, sslot_off:sslot_off + width]),
                       waits=vw, sem=ve)

    def bcast_row_at(dst, dstoff, rowbuf, off, width, wtoks):
        def bm(E):
            return E.matmul(pbk[4][:, 0:width], onesr[0:1, :], rowbuf[0:1, off:off + width],
                            start=True, stop=True)
        t1 = PL.step("tensor", bm, waits=list(wtoks), sem=te)
        return PL.step("scalar",
                       lambda E: E.copy(dst[:, dstoff:dstoff + width], pbk[4][:, 0:width]),
                       waits=[t1], sem=sc)

    def ln_to_bc(src_fn, wtoks):
        done = []
        prev = list(wtoks)
        for h5 in range(LH // 512):
            tok = ln_stats(lambda k, h=h5: src_fn(k)[:, h * 512:(h + 1) * 512], 512,
                           prev, srow, 0, 512)
            b1 = bcast_row_at(bigbc1, h5 * 512, srow, 512, 512, [tok])
            b2 = bcast_row_at(bigbc2, h5 * 512, srow, 0, 512, [tok, b1])
            done.extend([b1, b2])
            prev = list(wtoks) + [b1, b2]
        return done

    if stop_after == "D0":
        with nc.Block() as blk:
            PL.play(blk)
        return nc

    st_x = ln_to_bc(lambda k: x2[:, k * LH:(k + 1) * LH], [tk_xT])
    if stop_after == "D1":
        with nc.Block() as blk:
            PL.play(blk)
        return nc
    att_toks = []
    for k in range(KD):
        def attf(E, kk=k):
            i = None
            for n in range(LH // 512):
                sl = slice(kk * LH + n * 512, kk * LH + (n + 1) * 512)
                bl = slice(n * 512, (n + 1) * 512)
                VC(E, E.tensor_mul(sqD[:], x2[:, sl], bigbc1[:, bl]))
                VC(E, E.tensor_add(sqD[:], sqD[:], bigbc2[:, bl]))
                i = E.tensor_add(mo[:, sl], mo[:, sl], sqD[:])
            return i
        att_toks.append(PL.step("vector", attf,
                                waits=st_x + [tk_xT, mo_ready] +
                                ([att_toks[-1]] if att_toks else []), sem=ve))

    if DBG:
        dma("sync", sy, dbg_att.ap(), mo[:, 0:512], waits=[att_toks[-1]])
    st_a = ln_to_bc(lambda k: mo[:, k * LH:(k + 1) * LH], [att_toks[-1]])
    x2_toks = []
    for k in range(KD):
        def x2f(E, kk=k):
            i = None
            for n in range(LH // 512):
                sl = slice(kk * LH + n * 512, kk * LH + (n + 1) * 512)
                bl = slice(n * 512, (n + 1) * 512)
                VC(E, E.tensor_mul(sqD[:], mo[:, sl], bigbc1[:, bl]))
                VC(E, E.tensor_add(sqD[:], sqD[:], bigbc2[:, bl]))
                i = E.tensor_add(x2[:, sl], x2[:, sl], sqD[:])
            return i
        x2_toks.append(PL.step("vector", x2f,
                               waits=st_a + ([x2_toks[-1]] if x2_toks else []), sem=ve))

    st_2 = ln_to_bc(lambda k: x2[:, k * LH:(k + 1) * LH], [x2_toks[-1]])
    x2n_toks = []
    for k in range(KD):
        def x2nf(E, kk=k):
            i = None
            for n in range(LH // 512):
                sl = slice(kk * LH + n * 512, kk * LH + (n + 1) * 512)
                bl = slice(n * 512, (n + 1) * 512)
                VC(E, E.tensor_mul(sqD[:], x2[:, sl], bigbc1[:, bl]))
                VC(E, E.tensor_add(sqD[:], sqD[:], bigbc2[:, bl]))
                i = E.tensor_copy(x2n[:, sl], sqD[:])
            return i
        x2n_toks.append(PL.step("vector", x2nf,
                                waits=st_2 + [mo_evs[-1]] +
                                ([x2n_toks[-1]] if x2n_toks else []), sem=ve))
    x2n_ready = x2n_toks[-1]
    if stop_after == "D3":
        with nc.Block() as blk:
            PL.play(blk)
        return nc


    if DBG:
        dma("sync", sy, dbg_x2n.ap(), x2n[:, 0:512], waits=[x2n_ready])
    NQG = (2 * HID) // 8 // P
    ag_gu = cc_gu_out.ap().rearrange("(r k p) (mtl m) -> r mtl p k m", r=8, p=P, m=P)
    ag_wp = cc_wp_out.ap().rearrange("(r k p) (m2 m) -> r m2 p k m", r=8, p=P, m=P)

    tk_x2sp = dma("sync", sy, x2_sp.ap().rearrange("(k p) t -> p k t", p=P),
                  x2[:].rearrange("p (k t) -> p k t", t=LH), waits=[x2n_ready])

    out_wr_toks = []
    gsl_last = [None] * 4
    wsl_last = [None, None]
    prev_outf = None
    pbk4_guard = None
    prev_evr = None
    wp_done_blk = None
    qk = NKH // 2
    MOFF = 512
    SOFF = 512 + FCH
    for blk in range(LH // 512):
        B0 = blk * 512
        h_of_ht = []
        psg = {}
        for ht in range(NHT):
            b4 = (ht % 2) * 2
            tk_g = PL.step("sync",
                           lambda E, m=ht, b=b4: E.dma_start(
                               out=guslab[b][:].rearrange("p (k m) -> p k m", m=P),
                               in_=ag_gu[m // NQG, m % NQG]),
                           waits=[(AGTOK["ag1"] if (blk == 0 and ht < 2) else None),
                                  (x2n_ready if (blk == 0 and ht < 2) else None),
                                  (tk_x2sp if (blk == 0 and ht < 2) else None),
                                  gsl_last[b4]],
                           sem=sy, inck=16)
            tk_u2 = PL.step("scalar",
                            lambda E, m=NHT + ht, b=b4 + 1: E.dma_start(
                                out=guslab[b][:].rearrange("p (k m) -> p k m", m=P),
                                in_=ag_gu[m // NQG, m % NQG]),
                            waits=[(x2n_ready if (blk == 0 and ht < 2) else None),
                                   (AGTOK["ag1"] if (blk == 0 and ht < 2) else None),
                                   gsl_last[b4 + 1]], sem=sd, inck=16)
            last_h = None
            for nch in range(2):
                bkg = 2 * (nch % 2)
                bku = bkg + 1

                def mmg(E, b=b4, bk=bkg, n=nch, b0=B0):
                    i = None
                    for k in range(KD):
                        i = E.matmul(pbk[bk][:, 0:FCH], guslab[b][:, k * P:(k + 1) * P],
                                     x2n[:, k * LH + b0 + n * FCH:k * LH + b0 + (n + 1) * FCH],
                                     start=(k == 0), stop=(k == KD - 1))
                    return i
                tk_mg = PL.step("tensor", mmg, waits=[tk_g, x2n_ready, psg.get(bkg)], sem=te)

                def mmu(E, b=b4 + 1, bk=bku, n=nch, b0=B0):
                    i = None
                    for k in range(KD):
                        i = E.matmul(pbk[bk][:, 0:FCH], guslab[b][:, k * P:(k + 1) * P],
                                     x2n[:, k * LH + b0 + n * FCH:k * LH + b0 + (n + 1) * FCH],
                                     start=(k == 0), stop=(k == KD - 1))
                    return i
                tk_mu = PL.step("tensor", mmu, waits=[tk_u2, psg.get(bku)], sem=te)

                def evg(E, bk=bkg):
                    return E.activation(bc4a[:], pbk[bk][:, 0:FCH], AF.Silu, bias=0.0, scale=1.0)
                tk_eg = PL.step("scalar", evg, waits=[tk_mg, tk_mu, last_h], sem=sc)

                def evu(E, bk=bku):
                    return E.copy(bc4b[:], pbk[bk][:, 0:FCH])
                tk_eu = PL.step("scalar", evu, waits=[tk_mu], sem=sc)

                def hmk(E, t=ht, n=nch):
                    return E.tensor_mul(h_sb[:, t * 512 + n * FCH:t * 512 + (n + 1) * FCH],
                                        bc4a[:], bc4b[:])
                last_h = PL.step("vector", hmk,
                                 waits=[tk_eg, tk_eu,
                                        (tk_x2sp if (blk == 0 and ht == 0) else None),
                                        (wp_done_blk if (blk > 0 and ht < 2) else None)],
                                 sem=ve)
                psg[bkg] = tk_eg
                psg[bku] = tk_eu
            gsl_last[b4] = tk_mg
            gsl_last[b4 + 1] = tk_mu
            h_of_ht.append(last_h)
        h_all = h_of_ht[-1]

        hrow_done = None
        h2guard = [None, None]
        for ht in range(NHT):
            def h2f(E, t=ht, b=ht % 2):
                return E.activation(h2b[b][:], h_sb[:, t * 512:(t + 1) * 512], AF.Square,
                                    bias=0.0, scale=1.0)
            tk_h2 = PL.step("scalar", h2f, waits=[h_of_ht[ht], h2guard[ht % 2]], sem=sc)

            def hrow(E, t=ht, b=ht % 2):
                E.matmul(pR[:, 0:FCH], onesb[:, 0:1], h2b[b][:, 0:FCH], start=(t == 0),
                         stop=(t == NHT - 1))
                return E.matmul(pM[0:1, 0:FCH], onesb[:, 0:1], h2b[b][:, FCH:2 * FCH],
                                start=(t == 0), stop=(t == NHT - 1))
            hrow_done = PL.step("tensor", hrow,
                                waits=[tk_h2, (prev_evr if ht == 0 else None)], sem=te)
            h2guard[ht % 2] = hrow_done

        def evr2(E):
            E.copy(srow2[0:1, 0:FCH], pR[0:1, 0:FCH])
            return E.copy(srow2[0:1, FCH:2 * FCH], pM[0:1, 0:FCH])
        tk_evr = PL.step("scalar", evr2, waits=[hrow_done], sem=sc)
        prev_evr = tk_evr
        tk_i1 = PL.step("vector",
                        lambda E: E.tensor_scalar_mul(srow2[0:1, 0:512], srow2[0:1, 0:512],
                                                      1.0 / HID),
                        waits=[tk_evr], sem=ve)

        def invs(E):
            SC(E, E.activation(srow2[0:1, 0:512], srow2[0:1, 0:512], AF.Ln, bias=EPS_RMS,
                               scale=1.0))
            return E.activation(srow2[0:1, 0:512], srow2[0:1, 0:512], AF.Exp, bias=0.0,
                                scale=-0.5)
        tk_i2 = PL.step("scalar", invs, waits=[tk_i1], sem=sc)

        for tc in range(2):
            c0 = B0 + tc * FCH
            tk_xc2 = dma("gpsimd", gp, x2ch[:].rearrange("p (k t) -> p k t", t=FCH),
                         x2_sp.ap().rearrange("(k p) t -> p k t", p=P)[:, :, c0:c0 + FCH],
                         waits=[tk_x2sp, prev_outf])
            foev = []
            prev_fo = pbk4_guard
            for mt in range(NMT):
                last_q = None
                for q in range(2):
                    bw = q % 2
                    qeng, qsem = (("gpsimd", gp) if bw == 0 else ("sync", sy))

                    def ldq(E, m=mt, b=bw, qq=q):
                        sw = ag_wp[(m * P) // QP_R, (m * P % QP_R) // P]
                        return E.dma_start(out=wpslab[b][:].rearrange("p (k m) -> p k m", m=P),
                                           in_=sw[:, qq * qk:(qq + 1) * qk, :])
                    tk_wq = PL.step(qeng, ldq,
                                    waits=[wsl_last[bw],
                                           (AGTOK["ag2"] if (blk == 0 and tc == 0 and mt == 0
                                                             and q < 2) else None),
                                           (h_all if (tc == 0 and mt == 0 and q < 2) else None)],
                                    sem=qsem, inck=16)

                    def mmw(E, b=bw, qq=q, t=tc):
                        i = None
                        for k in range(qk):
                            kk = qq * qk + k
                            i = E.matmul(pbk[4][:, 0:FCH], wpslab[b][:, k * P:(k + 1) * P],
                                         h_sb[:, kk * 512 + t * FCH:kk * 512 + (t + 1) * FCH],
                                         start=(qq == 0 and k == 0), stop=(qq == 1 and k == qk - 1))
                        return i
                    last_q = PL.step("tensor", mmw,
                                     waits=[tk_wq, h_all, (prev_fo if q == 0 else None)],
                                     sem=te)
                    wsl_last[bw] = last_q
                tk_fo = PL.step("scalar",
                                lambda E, m=mt: E.copy(fo[:, m * FCH:(m + 1) * FCH],
                                                       pbk[4][:, 0:FCH]),
                                waits=[last_q, (prev_outf if mt == 0 else None)], sem=sc)
                prev_fo = tk_fo
                foev.append(tk_fo)

            st4 = ln_stats(lambda k: fo[:, k * FCH:(k + 1) * FCH], FCH,
                           [foev[-1], tk_i2], srow2, MOFF, SOFF, inv_off=tc * FCH)
            tb_s4 = bcast_row_at(bc4a, 0, srow2, SOFF, FCH, [st4])
            tb_m4 = bcast_row_at(bc4b, 0, srow2, MOFF, FCH, [st4, tb_s4])

            for k in range(KD):
                bo = k % 2

                def outf(E, kk=k, b=bo):
                    VC(E, E.tensor_mul(outD[b][:], fo[:, kk * FCH:(kk + 1) * FCH], bc4a[:]))
                    VC(E, E.tensor_add(outD[b][:], outD[b][:], bc4b[:]))
                    return E.tensor_add(outD[b][:], outD[b][:], x2ch[:, kk * FCH:(kk + 1) * FCH])
                tk_of = PL.step("vector", outf,
                                waits=[tb_s4, tb_m4, tk_xc2,
                                       (out_wr_toks[-2] if len(out_wr_toks) >= 2 else None)],
                                sem=ve)
                out_wr_toks.append(dma("sync", sy, out_h[k * P:(k + 1) * P, c0:c0 + FCH],
                                       outD[bo][:], waits=[tk_of]))
                prev_outf = tk_of
            pbk4_guard = tb_m4
        wp_done_blk = wsl_last[1]

    PL.step("sync", lambda E: None,
            waits=[out_wr_toks[-1], (sy, sy.n), (sy2, sy2.n), (gp, gp.n), (gp2, gp2.n),
                   (sd, sd.n), (sd2, sd2.n)])
    if stop_after == "D4":
        with nc.Block() as blk:
            PL.play(blk)
        return nc

    with nc.Block() as blk:
        PL.play(blk)
    return nc


def _prep_core(inputs, c, cfg):
    global EPS_LN, EPS_RMS
    L = cfg["L"]; D = cfg["D"]; DIN = cfg["DIN"]; NST = cfg["NST"]
    RANK = cfg["RANK"]; HID = cfg["HID"]
    DH = DIN // 2; NG = DH // P; KD = D // P; LH = L // 2
    b, j = c // 2, c % 2
    A = lambda v: np.ascontiguousarray(v)

    x = inputs["L0_attn_x"][b]
    w_in = inputs["L0_attn_in_proj_w"]
    w_inT = A(np.concatenate([w_in[j * DH:(j + 1) * DH],
                              w_in[DIN + j * DH:DIN + (j + 1) * DH]], 0).T)
    convw = A(inputs["L0_attn_conv_w"][j * DH:(j + 1) * DH, 0, :])
    vecs = np.stack([inputs["L0_attn_conv_b"][j * DH:(j + 1) * DH],
                     inputs["L0_attn_dt_proj_b"][j * DH:(j + 1) * DH],
                     inputs["L0_attn_D"][j * DH:(j + 1) * DH],
                     np.zeros(DH, np.float32)], 1).reshape(NG, P, 4)
    xprojT = A(inputs["L0_attn_x_proj_w"][:, j * DH:(j + 1) * DH].T)
    dtprojT = A(inputs["L0_attn_dt_proj_w"][j * DH:(j + 1) * DH, :].T)
    alog = A(inputs["L0_attn_A_log"][j * DH:(j + 1) * DH])
    oprojT = A(inputs["L0_attn_out_proj_w"].T)
    normw = A(inputs["L0_attn_norm_w"].reshape(KD, P).T)
    QGU_R = 2 * 4 * D // 8 * (HID // (4 * D)) if False else (2 * HID) // 8
    QP_R = D // 8
    wgu_q = A(inputs["L0_ffn_w_gate_up"][c * QGU_R:(c + 1) * QGU_R])
    wp_q = A(inputs["L0_ffn_w_proj"][c * QP_R:(c + 1) * QP_R])
    xT = A(x.T[:, j * LH:(j + 1) * LH])

    sel = np.zeros((2 * NST, 2 * NST * P), dtype=NBF)
    for i in range(2 * NST):
        sel[i, i * P:(i + 1) * P] = NBF(1.0)

    return dict(
        x_tm=A(x), x_fm=xT, w_inT=w_inT, convw=convw, vecs=A(vecs.astype(np.float32)),
        xprojT=xprojT, dtprojT=dtprojT, alog=alog, oprojT=oprojT, normw=normw,
        wgu_q=wgu_q, wp_q=wp_q,
        ident_bf=np.eye(P, dtype=NBF),
        ones_f=np.ones((P, 4), np.float32),
        ones_r=np.ones((1, P), np.float32),
        ones_b=np.ones((P, 4), NBF),
        consts=np.broadcast_to(np.array([EPS_LN, 1.0, EPS_RMS, 0.0], np.float32),
                               (P, 4)).copy(),
        sel_bf=sel,
    )


def kernel(**inputs):
    cfg = CFG
    inputs = {k: np.asarray(v) for k, v in inputs.items()}
    nc = build(cfg)
    in_maps = [_prep_core(inputs, c, cfg) for c in range(8)]
    res = run_bass_kernel_spmd(nc, in_maps, core_ids=list(range(8)))
    L = cfg["L"]; D = cfg["D"]; LH = L // 2
    out = np.empty((4, L, D), np.float32)
    for c in range(8):
        b, j = c // 2, c % 2
        out[b, j * LH:(j + 1) * LH, :] = res.results[c]["out_fm"].T
    return out


# version 17
# speedup vs baseline: 1.0304x; 1.0304x over previous
import os
import numpy as np
import ml_dtypes
import concourse.bass as bass
import concourse.mybir as mybir
from concourse.bass_utils import run_bass_kernel_spmd

F32 = mybir.dt.float32
BF16 = mybir.dt.bfloat16
AF = mybir.ActivationFunctionType
OP = mybir.AluOpType
AX = mybir.AxisListType
NBF = ml_dtypes.bfloat16

P = 128
RND = 12582912.0
EPS_LN = 1e-5
EPS_RMS = 1.1920929e-07

CFG = dict(L=2048, D=2048, DIN=4096, NST=16, RANK=128, HID=8192)
ENGS = ("sync", "gpsimd", "tensor", "vector", "scalar")
KB = 1024


class Sem:
    __slots__ = ("h", "n")

    def __init__(self, h):
        self.h = h
        self.n = 0


class Plan:
    def __init__(self):
        self.steps = []
        self.last_dma = {}
        self.dma_sems = {}
        self.dma_rr = {}

    def set_dma_sems(self, eng, sems):
        self.dma_sems[eng] = sems
        self.dma_rr[eng] = 0

    def step(self, eng, fn, waits=(), sem=None, inck=1):
        waits = [w for w in waits if w is not None]
        if inck == 16 and sem is not None:
            sems = self.dma_sems.get(eng)
            if sems:
                r = self.dma_rr[eng]
                self.dma_rr[eng] = (r + 1) % len(sems)
                sem = sems[r]
            prev = self.last_dma.get((eng, id(sem)))
            if prev is not None:
                waits.append(prev)
        tok = None
        if sem is not None:
            sem.n += inck
            tok = (sem, sem.n)
            if inck == 16:
                self.last_dma[(eng, id(sem))] = tok
        self.steps.append((eng, tuple(waits), fn, sem, inck))
        return tok

    def play(self, block):
        def mk(ename):
            def body(E):
                hi = {}
                for eng, waits, fn, sem, inck in self.steps:
                    if eng != ename:
                        continue
                    for (ws, wv) in waits:
                        if wv > 0 and hi.get(id(ws), -1) < wv:
                            E.wait_ge(ws.h, wv)
                            hi[id(ws)] = wv
                    inst = fn(E)
                    if sem is not None:
                        assert inst is not None, f"inc step returned None on {ename}"
                        inst.then_inc(sem.h, inck)
            return body
        for e in ENGS:
            getattr(block, e)(mk(e))


def build(cfg, stop_after=None):
    L = cfg["L"]; D = cfg["D"]; DIN = cfg["DIN"]; NST = cfg["NST"]
    RANK = cfg["RANK"]; HID = cfg["HID"]
    DH = DIN // 2; NG = DH // P; KD = D // P; LH = L // 2; NTT = L // P
    HF = 2 * HID; GQ = 64
    QGU_R = HF // 8; QP_R = D // 8
    NKH = HID // P
    NMT = KD
    NHT = NKH
    FCH = 256; NFC = LH // FCH
    XDR = RANK + 2 * NST
    KDIN = DIN // P
    NCHB = LH // 512
    NCHL = L // 512

    nc = bass.Bass("TRN2", target_bir_lowering=False, detect_race_conditions=False)

    x_h = nc.dram_tensor("x_tm", [L, D], F32, kind="ExternalInput")
    xT_h = nc.dram_tensor("x_fm", [D, LH], F32, kind="ExternalInput")
    w_inT_h = nc.dram_tensor("w_inT", [D, 2 * DH], F32, kind="ExternalInput")
    convw_h = nc.dram_tensor("convw", [DH, 4], F32, kind="ExternalInput")
    vecs_h = nc.dram_tensor("vecs", [NG, P, 4], F32, kind="ExternalInput")
    xprojT_h = nc.dram_tensor("xprojT", [DH, XDR], F32, kind="ExternalInput")
    dtprojT_h = nc.dram_tensor("dtprojT", [RANK, DH], F32, kind="ExternalInput")
    alog_h = nc.dram_tensor("alog", [DH, NST], F32, kind="ExternalInput")
    oprojT_h = nc.dram_tensor("oprojT", [DIN, D], F32, kind="ExternalInput")
    normw_h = nc.dram_tensor("normw", [P, KD], F32, kind="ExternalInput")
    wgu_q_h = nc.dram_tensor("wgu_q", [QGU_R, D], F32, kind="ExternalInput")
    wp_q_h = nc.dram_tensor("wp_q", [QP_R, HID], F32, kind="ExternalInput")
    identb_h = nc.dram_tensor("ident_bf", [P, P], BF16, kind="ExternalInput")
    onesf_h = nc.dram_tensor("ones_f", [P, 4], F32, kind="ExternalInput")
    onesr_h = nc.dram_tensor("ones_r", [1, P], F32, kind="ExternalInput")
    onesb_h = nc.dram_tensor("ones_b", [P, 4], BF16, kind="ExternalInput")
    consts_h = nc.dram_tensor("consts", [P, 4], F32, kind="ExternalInput")
    sel_h = nc.dram_tensor("sel_bf", [2 * NST, 2 * NST * P], BF16, kind="ExternalInput")

    out_h = nc.dram_tensor("out_fm", [D, LH], F32, kind="ExternalOutput")
    DBG = bool(int(os.environ.get("MK_DEBUG", "0")))
    if DBG:
        dbg_xnT = nc.dram_tensor("dbg_xnT", [P, 1024], BF16, kind="ExternalOutput")
        dbg_a1 = nc.dram_tensor("dbg_a1", [P, D], BF16, kind="ExternalOutput")
        dbg_a2 = nc.dram_tensor("dbg_a2", [P, 16], F32, kind="ExternalOutput")
        dbg_a3 = nc.dram_tensor("dbg_a3", [P, 1024], BF16, kind="ExternalOutput")
        dbg_a4 = nc.dram_tensor("dbg_a4", [P, D], F32, kind="ExternalOutput")
        dbg_xc = nc.dram_tensor("dbg_xc", [P, 1024], BF16, kind="ExternalOutput")
        dbg_sr = nc.dram_tensor("dbg_sr", [P, 1024], BF16, kind="ExternalOutput")
        dbg_xd = nc.dram_tensor("dbg_xd", [XDR, 512], F32, kind="ExternalOutput")
        dbg_dl = nc.dram_tensor("dbg_dl", [P, 512], F32, kind="ExternalOutput")
        dbg_y = nc.dram_tensor("dbg_y", [P, 512], BF16, kind="ExternalOutput")
        dbg_mo = nc.dram_tensor("dbg_mo", [P, 512], F32, kind="ExternalOutput")
        dbg_att = nc.dram_tensor("dbg_att", [P, 512], F32, kind="ExternalOutput")
        dbg_x2n = nc.dram_tensor("dbg_x2n", [P, 512], BF16, kind="ExternalOutput")
        dbg_fo = nc.dram_tensor("dbg_fo", [P, 256], F32, kind="ExternalOutput")
        dbg_wg = nc.dram_tensor("dbg_wg", [P, D], BF16, kind="ExternalOutput")
        dbg_wp2 = nc.dram_tensor("dbg_wp2", [P, HID], BF16, kind="ExternalOutput")

    xc_sp = nc.dram_tensor("xc_sp", [DH, L], BF16)
    x2_sp = nc.dram_tensor("x2_sp", [D, LH], F32)
    sres_sp = nc.dram_tensor("sres_sp", [DH, L], BF16)
    cc_xd_in = nc.dram_tensor("cc_xd_in", [XDR, L], F32)
    cc_xd_out = nc.dram_tensor("cc_xd_out", [XDR, L], F32)
    cc_y_in = nc.dram_tensor("cc_y_in", [2 * DH, LH], BF16)
    cc_y_ag = nc.dram_tensor("cc_y_ag", [2 * 2 * DH, LH], BF16)
    cc_gu_in = nc.dram_tensor("cc_gu_in", [D, QGU_R], BF16)
    cc_gu_out = nc.dram_tensor("cc_gu_out", [8 * D, QGU_R], BF16)
    cc_wp_in = nc.dram_tensor("cc_wp_in", [HID, QP_R], BF16)
    cc_wp_out = nc.dram_tensor("cc_wp_out", [8 * HID, QP_R], BF16)

    groups2 = [[0, 1], [2, 3], [4, 5], [6, 7]]
    groups8 = [list(range(8))]

    SB_LIM = nc.SBUF_PARTITION_SIZE_BYTES

    def sb(name, pdim, nel, dtype, off):
        nbytes = nel * (2 if dtype == BF16 else 4)
        assert off % 32 == 0 and off + nbytes <= SB_LIM, f"SBUF {name}: {off}+{nbytes}"
        return nc.alloc_sbuf_tensor_at(name, [pdim, nel], dtype, offset=off)

    o = 17 * KB + 512
    def nxt(nbytes):
        nonlocal o
        r = o
        o = (o + nbytes + 63) & ~63
        return r
    identb = sb("identb", P, P, BF16, nxt(256))
    onesf = sb("onesf", P, 4, F32, nxt(16))
    onesr = sb("onesr", 1, P, F32, nxt(P * 4))
    onesb = sb("onesb", P, 4, BF16, nxt(32))
    constsb = sb("constsb", P, 4, F32, nxt(16))
    vecs = sb("vecs", P, NG * 4, F32, nxt(NG * 16))
    convw = sb("convw", P, NG * 4, F32, nxt(NG * 16))
    a_sb = sb("a_sb", P, NG * NST, F32, nxt(NG * NST * 4))
    normw_c = sb("normw_c", P, KD, F32, nxt(KD * 4))
    bc4a = sb("bc4a", P, FCH, F32, nxt(FCH * 4))
    bc4b = sb("bc4b", P, FCH, F32, nxt(FCH * 4))
    outD = [sb(f"outD{i}", P, FCH, F32, nxt(FCH * 4)) for i in range(2)]
    assert o <= 25 * KB, o

    o = 25 * KB
    browb = sb("browb", 2 * NST, L, BF16, nxt(L * 2))
    delta_r = sb("delta_r", P, L, BF16, nxt(L * 2))
    state = sb("state", P, NG * NST, F32, nxt(NG * NST * 4))
    dtprojT = sb("dtprojT", P, DH, BF16, nxt(DH * 2))
    xprojT = sb("xprojT", P, NG * XDR, BF16, nxt(NG * XDR * 2))
    assert o <= 44 * KB, o
    R0 = 44 * KB

    o = R0
    x_nT = sb("x_nT", P, KD * L, BF16, nxt(KD * L * 2))
    xin = [sb(f"xin{i}", P, D, F32, nxt(D * 4)) for i in range(2)]
    sq_scr = sb("sq_scr", P, D, F32, nxt(D * 4))
    xn_tm = [sb(f"xntm{i}", P, D, BF16, nxt(D * 2)) for i in range(2)]
    wslab = [sb(f"wslab{i}", P, KD * P, BF16, nxt(KD * P * 2)) for i in range(3)]
    xc_pre = [sb(f"xc_pre{i}", P, L + 8, BF16, nxt((L + 8) * 2)) for i in range(2)]
    conv_acc = sb("conv_acc", P, L, F32, nxt(L * 4))
    xcO = [sb(f"xcO{i}", P, L, BF16, nxt(L * 2)) for i in range(2)]
    sresO = [sb(f"sresO{i}", P, L, BF16, nxt(L * 2)) for i in range(2)]
    colt = sb("colt", P, 16, F32, nxt(64))
    acols = sb("acols", P, 8 * NTT, F32, nxt(8 * NTT * 4))
    xdacc1 = sb("xdacc1", P, L, F32, nxt(L * 4))
    xdacc2 = sb("xdacc2", 32, L, F32, nxt(L * 4))
    xcA2 = [sb(f"xcA2{i}", P, L, BF16, nxt(L * 2)) for i in range(2)]
    brow = sb("brow", 2 * NST, L, F32, nxt(L * 4))
    assert o <= SB_LIM, o

    o = R0
    b_bc = sb("b_bc", P, NST * LH, BF16, nxt(NST * LH * 2))
    c_bc = sb("c_bc", P, NST * LH, BF16, nxt(NST * LH * 2))
    dAb = [sb(f"dAb{i}", P, LH, BF16, nxt(LH * 2)) for i in range(2)]
    dBu = [sb(f"dBu{i}", P, LH, BF16, nxt(LH * 2)) for i in range(2)]
    hbf = [sb(f"hbf{i}", P, LH, BF16, nxt(LH * 2)) for i in range(2)]
    deltg = sb("deltg", P, LH, F32, nxt(LH * 4))
    ug = sb("ug", P, LH, BF16, nxt(LH * 2))
    xcB = [sb(f"xcB{i}", P, LH, BF16, nxt(LH * 2)) for i in range(2)]
    srB = [sb(f"srB{i}", P, LH, BF16, nxt(LH * 2)) for i in range(2)]
    sp_t1 = sb("sp_t1", P, LH, F32, nxt(LH * 4))
    sp_t2 = sb("sp_t2", P, LH, F32, nxt(LH * 4))
    yf32 = sb("yf32", P, LH, F32, nxt(LH * 4))
    ybf = [sb(f"ybf{i}", P, LH, BF16, nxt(LH * 2)) for i in range(2)]
    selb = sb("selb", 2 * NST, 2 * NST * P, BF16, nxt(2 * NST * P * 2))
    b_bc2 = sb("b_bc2", P, NST * LH, BF16, nxt(NST * LH * 2))
    c_bc2 = sb("c_bc2", P, NST * LH, BF16, nxt(NST * LH * 2))
    assert o <= SB_LIM, o

    QCW = 1024
    o = R0 + 164 * KB
    q_wbf = [sb(f"q_wbf{i}", P, QCW, BF16, nxt(QCW * 2)) for i in range(2)]
    q_wh = sb("q_wh", P, QCW, BF16, nxt(QCW * 2))
    q_t2 = sb("q_t2", P, QCW, BF16, nxt(QCW * 2))
    q_fin = sb("q_fin", P, QCW, BF16, nxt(QCW * 2))
    q_red = sb("q_red", P, 64, F32, nxt(256))
    q_scb = sb("q_scb", P, 64, BF16, nxt(128))
    q_scf = sb("q_scf", P, 64, F32, nxt(256))
    q_inv = sb("q_inv", P, 64, F32, nxt(256))
    q_ev = sb("q_ev", P, 1024, BF16, nxt(2048))
    assert o <= R0 + 177 * KB, o

    y_sb = sb("y_sb", P, 2 * NG * LH, BF16, R0)
    x2 = sb("x2", P, KD * LH, F32, R0)
    oslab = [sb(f"oslab{i}", P, KDIN * P, BF16, R0 + 64 * KB + i * 8 * KB) for i in range(2)]
    bigbc1 = sb("bigbc1", P, LH, F32, R0 + 64 * KB)
    bigbc2 = sb("bigbc2", P, LH, F32, R0 + 72 * KB)
    mo = sb("mo", P, KD * LH, F32, R0 + 80 * KB)
    x2n = sb("x2n", P, KD * LH, BF16, R0 + 80 * KB)
    h_sb = sb("h_sb", P, NHT * 512, BF16, R0)
    fo = sb("fo", P, KD * FCH, F32, R0 + 64 * KB)
    x2ch = sb("x2ch", P, KD * FCH, F32, R0 + 112 * KB)
    guslab = [sb(f"guslab{i}", P, KD * P, BF16, R0 + 128 * KB + i * 4 * KB) for i in range(4)]
    wpslab = [sb(f"wpslab{i}", P, (NKH // 2) * P, BF16, R0 + 144 * KB + i * 8 * KB)
              for i in range(2)]
    sqD = sb("sqD", P, 512, F32, R0 + 160 * KB)
    sqE = sb("sqE", P, 512, F32, R0 + 177 * KB)
    sqDB = [sqD, sqE]
    srow = sb("srow", 1, 2 * 512, F32, R0 + 162 * KB)
    srow2 = sb("srow2", 1, 512 + 2 * FCH, F32, R0 + 166 * KB)
    h2b = [sb(f"h2big{i}", P, 512, BF16, R0 + 170 * KB + i * KB) for i in range(2)]
    assert R0 + 179 * KB <= SB_LIM, R0 + 179 * KB
    nc.sbuf_base = R0 + 179 * KB

    pbk = [nc.alloc_psum_tensor(f"pb{i}", [P, 512], F32) for i in range(5)]
    pT = nc.alloc_psum_tensor("pT", [P, 1024], BF16)
    pR = nc.alloc_psum_tensor("pR", [1, 512], F32)
    pM = nc.alloc_psum_tensor("pM", [32, 512], F32)

    sy = Sem(nc.alloc_semaphore("sy"))
    gp = Sem(nc.alloc_semaphore("gp"))
    te = Sem(nc.alloc_semaphore("te"))
    ve = Sem(nc.alloc_semaphore("ve"))
    sc = Sem(nc.alloc_semaphore("sc"))
    cc = Sem(nc.alloc_semaphore("cc"))
    sd = Sem(nc.alloc_semaphore("sd"))
    sy2 = Sem(nc.alloc_semaphore("sy2"))
    gp2 = Sem(nc.alloc_semaphore("gp2"))
    sd2 = Sem(nc.alloc_semaphore("sd2"))
    vch = Sem(nc.alloc_semaphore("vch"))
    sch = Sem(nc.alloc_semaphore("sch"))
    PL = Plan()
    PL.set_dma_sems("sync", [sy, sy2])
    PL.set_dma_sems("gpsimd", [gp, gp2])
    PL.set_dma_sems("scalar", [sd, sd2])

    def VC(E, inst):
        vch.n += 1
        inst.then_inc(vch.h, 1)
        E.wait_ge(vch.h, vch.n)
        return inst

    def SC(E, inst):
        sch.n += 1
        inst.then_inc(sch.h, 1)
        E.wait_ge(sch.h, sch.n)
        return inst

    def dma(q, sem, out, in_, waits=()):
        return PL.step(q, lambda E, o_=out, i_=in_: E.dma_start(out=o_, in_=i_),
                       waits=waits, sem=sem, inck=16)

    t0 = []
    t0.append(dma("sync", sy, identb[:], identb_h.ap()))
    t0.append(dma("sync", sy, onesf[:], onesf_h.ap()))
    t0.append(dma("sync", sy, onesr[:], onesr_h.ap()))
    t0.append(dma("sync", sy, onesb[:], onesb_h.ap()))
    t0.append(dma("sync", sy, constsb[:], consts_h.ap()))
    nc.const_aps.register(F32, EPS_LN, constsb[:, 0:1])
    nc.const_aps.register(F32, 1.0, constsb[:, 1:2])
    nc.const_aps.register(F32, EPS_RMS, constsb[:, 2:3])
    nc.const_aps.register(F32, 0.0, constsb[:, 3:4])
    t0.append(dma("sync", sy, vecs[:].rearrange("p (g v) -> p g v", v=4),
                  vecs_h.ap().rearrange("g p v -> p g v")))
    t0.append(dma("sync", sy, convw[:].rearrange("p (g v) -> p g v", v=4),
                  convw_h.ap().rearrange("(g p) v -> p g v", p=P)))
    t0.append(dma("sync", sy, a_sb[:].rearrange("p (g n) -> p g n", n=NST),
                  alog_h.ap().rearrange("(g p) n -> p g n", p=P)))
    t0.append(dma("sync", sy, normw_c[:], normw_h.ap()))
    t0.append(dma("gpsimd", gp, dtprojT[:], dtprojT_h.ap()))
    t0.append(dma("gpsimd", gp, xprojT[:].rearrange("p (g m) -> p g m", m=XDR),
                  xprojT_h.ap().rearrange("(g p) m -> p g m", p=P)))
    tk = PL.step("scalar", lambda E: E.activation(a_sb[:], a_sb[:], AF.Exp, bias=0.0, scale=1.0),
                 waits=t0, sem=sc)
    tk_acst = PL.step("vector", lambda E: E.tensor_scalar_mul(a_sb[:], a_sb[:], -1.0),
                      waits=[tk], sem=ve)

    qchunks = []
    for t in range(QGU_R // P):
        for ccol in range(D // QCW):
            qchunks.append((wgu_q_h, t * P, ccol * QCW, cc_gu_in, t * P))
    for t in range(QP_R // P):
        for ccol in range(HID // QCW):
            qchunks.append((wp_q_h, t * P, ccol * QCW, cc_wp_in, t * P))
    NGR = QCW // GQ
    NKB = QCW // P
    NTG = NKB // 8

    _qstate = dict(prev_vec=None, prev_vec2=None, prev_tr=None, prev_wr=None, prev_wr2=None)
    PT_G = [None]

    def emit_qchunk(ci):
        src, r0, c0, dst, dr0 = qchunks[ci]
        qb = q_wbf[ci % 2]
        prev_vec = _qstate["prev_vec"]; prev_tr = _qstate["prev_tr"]; prev_wr = _qstate["prev_wr"]
        tk_ld = PL.step("gpsimd",
                        lambda E, s=src, r=r0, c=c0, b=qb: E.dma_start(out=b[:], in_=s[r:r + P, c:c + QCW]),
                        waits=[_qstate["prev_vec2"]], sem=gp, inck=16)

        def v1(E, b=qb):
            VC(E, E.tensor_reduce(q_red[:, 0:NGR], b[:].rearrange("p (g k) -> p g k", k=GQ),
                                  AX.X, OP.add, apply_absolute_value=True))
            VC(E, E.tensor_scalar(q_scb[:, 0:NGR], q_red[:, 0:NGR], 1.0 / GQ, 1e-8, OP.mult, OP.max))
            return E.tensor_copy(q_scf[:, 0:NGR], q_scb[:, 0:NGR])
        tk_sc1 = PL.step("vector", v1, waits=[tk_ld, prev_vec], sem=ve)

        def s1(E):
            SC(E, E.activation(q_inv[:, 0:NGR], q_scf[:, 0:NGR], AF.Ln, bias=0.0, scale=1.0))
            return E.activation(q_inv[:, 0:NGR], q_inv[:, 0:NGR], AF.Exp, bias=0.0, scale=-1.0)
        tk_inv = PL.step("scalar", s1, waits=[tk_sc1], sem=sc)

        def v2(E, b=qb):
            for g in range(NGR):
                i = E.tensor_scalar_mul(q_wh[:, g * GQ:(g + 1) * GQ], b[:, g * GQ:(g + 1) * GQ],
                                        q_inv[:, g:g + 1])
            VC(E, i)
            VC(E, E.tensor_scalar(sqD[:], q_wh[:, 0:512], RND, -RND, OP.add, OP.add))
            VC(E, E.tensor_scalar(sqE[:], q_wh[:, 512:1024], RND, -RND, OP.add, OP.add))
            VC(E, E.tensor_scalar(q_wh[:, 0:512], sqD[:], 1.0, -1.0, OP.min, OP.max))
            VC(E, E.tensor_scalar(q_wh[:, 512:1024], sqE[:], 1.0, -1.0, OP.min, OP.max))
            for g in range(NGR):
                i = E.tensor_scalar_mul(q_t2[:, g * GQ:(g + 1) * GQ], q_wh[:, g * GQ:(g + 1) * GQ],
                                        q_scf[:, g:g + 1])
            VC(E, i)
            VC(E, E.tensor_sub(q_fin[:], q_t2[:], b[:]))
            return E.tensor_add(q_fin[:], b[:], q_fin[:])
        prev_vec = PL.step("vector", v2, waits=[tk_inv, prev_tr], sem=ve)
        _qstate["prev_vec2"] = _qstate["prev_vec"]
        _qstate["prev_vec"] = prev_vec

        for tg in range(NTG):
            def tr(E, base=tg * 8):
                i = None
                for k8 in range(8):
                    i = E.transpose(pT[:, k8 * P:(k8 + 1) * P],
                                    q_fin[:, (base + k8) * P:(base + k8 + 1) * P], identb[:])
                return i
            prev_tr = PL.step("tensor", tr, waits=[prev_vec, prev_wr, PT_G[0]], sem=te)
            _qstate["prev_tr"] = prev_tr
            tk_ev = PL.step("scalar", lambda E: E.copy(q_ev[:], pT[:]), waits=[prev_tr], sem=sc)
            PT_G[0] = tk_ev

            def wr(E, d=dst, dd=dr0, c=c0, base=tg * 8):
                return E.dma_start(
                    out=d.ap().rearrange("(kb p) r -> p kb r", p=P)[:, (c // P) + base:(c // P) + base + 8,
                                                                   dd:dd + P],
                    in_=q_ev[:].rearrange("p (kb r) -> p kb r", r=P))
            prev_wr = PL.step("sync", wr, waits=[tk_ev], sem=sy, inck=16)
            _qstate["prev_wr2"] = _qstate["prev_wr"]
            _qstate["prev_wr"] = prev_wr
        if ci == len(qchunks) - 1:
            emit_q_ags()

    AGTOK = {}

    def emit_q_ags():
        AGTOK["ag1"] = PL.step("gpsimd",
                     lambda E: E.collective_compute("AllGather", OP.bypass, replica_groups=groups8,
                                                    ins=[cc_gu_in[:]], outs=[cc_gu_out[:]]),
                     waits=[_qstate["prev_wr"], _qstate["prev_wr2"]], sem=cc, inck=1)
        AGTOK["ag2"] = PL.step("gpsimd",
                     lambda E: E.collective_compute("AllGather", OP.bypass, replica_groups=groups8,
                                                    ins=[cc_wp_in[:]], outs=[cc_wp_out[:]]),
                     waits=(), sem=cc, inck=1)

    NQ_IN_A1 = min(len(qchunks), NTT)
    p1_cons = [None] * NTT
    for tt in range(NTT):
        bi = tt % 2
        war = [p1_cons[tt - 2]] if tt >= 2 else []
        tk_x = PL.step("sync", lambda E, t=tt, b=bi: E.dma_start(out=xin[b][:], in_=x_h[t * P:(t + 1) * P, :]),
                       waits=war, sem=sy, inck=16)
        tk_sq = PL.step("scalar",
                        lambda E, b=bi, t=tt: E.activation(sq_scr[:], xin[b][:], AF.Square, bias=0.0,
                                                           scale=1.0, accum_out=acols[:, NTT + t:NTT + t + 1]),
                        waits=[tk_x], sem=sc)
        tk_rs = PL.step("vector",
                        lambda E, b=bi, t=tt: E.reduce_sum(acols[:, t:t + 1], xin[b][:], AX.X),
                        waits=[tk_x], sem=ve)
        p1_cons[tt] = tk_rs if tt % 2 else tk_sq
        if tt < NQ_IN_A1:
            emit_qchunk(tt)
    p1_done = [tk_sq, tk_rs]

    CS, CQ, CNM, CV, CT, CRS, CR2, CSK = (acols[:, i * NTT:(i + 1) * NTT] for i in range(8))

    def p2v1(E):
        VC(E, E.tensor_scalar_mul(CNM, CS, -1.0 / D))
        VC(E, E.tensor_mul(CT, CNM, CNM))
        return E.scalar_tensor_tensor(CV, CQ, 1.0 / D, CT, OP.mult, OP.subtract)
    tk_p2a = PL.step("vector", p2v1, waits=p1_done, sem=ve)

    def p2s1(E):
        SC(E, E.activation(CT, CV, AF.Ln, bias=EPS_LN, scale=1.0))
        return E.activation(CRS, CT, AF.Exp, bias=0.0, scale=-0.5)
    tk_p2b = PL.step("scalar", p2s1, waits=[tk_p2a], sem=sc)

    def p2v2(E):
        VC(E, E.tensor_mul(CT, CV, CRS))
        return E.tensor_mul(CT, CT, CRS)
    tk_p2c = PL.step("vector", p2v2, waits=[tk_p2b], sem=ve)

    def p2s2(E):
        SC(E, E.activation(CT, CT, AF.Ln, bias=EPS_LN, scale=1.0))
        return E.activation(CR2, CT, AF.Exp, bias=0.0, scale=-0.5)
    tk_p2d = PL.step("scalar", p2s2, waits=[tk_p2c], sem=sc)
    tk_p2e = PL.step("vector", lambda E: E.tensor_mul(CSK, CRS, CR2), waits=[tk_p2d], sem=ve)

    evac_of_tt = []
    ap_of_tt = []
    for tt in range(NTT):
        bi = tt % 2
        war = [ap_of_tt[tt - 2]] if tt >= 2 else [tk_p2a]
        tk_x = PL.step("sync", lambda E, t=tt, b=bi: E.dma_start(out=xin[b][:], in_=x_h[t * P:(t + 1) * P, :]),
                       waits=war, sem=sy, inck=16)

        def vapply(E, b=bi, t=tt):
            return E.tensor_scalar(xn_tm[b][:], xin[b][:], acols[:, 2 * NTT + t:2 * NTT + t + 1],
                                   acols[:, 7 * NTT + t:7 * NTT + t + 1], OP.add, OP.mult)
        tk_ap = PL.step("vector", vapply,
                        waits=[tk_x, tk_p2e, (evac_of_tt[tt - 2] if tt >= 2 else None)], sem=ve)
        ap_of_tt.append(tk_ap)

        last_e = None
        for grp in range(KD // 8):
            def trp(E, b=bi, g=grp):
                i = None
                for k8 in range(8):
                    i = E.transpose(pT[:, k8 * P:(k8 + 1) * P],
                                    xn_tm[b][:, (g * 8 + k8) * P:(g * 8 + k8 + 1) * P], identb[:])
                return i
            tk_tp = PL.step("tensor", trp, waits=[tk_ap, PT_G[0]], sem=te)

            def evac(E, t=tt, g=grp):
                return E.copy(
                    x_nT[:].rearrange("p (k t2) -> p k t2", t2=L)[:, g * 8:(g + 1) * 8, t * P:(t + 1) * P],
                    pT[:].rearrange("p (k t2) -> p k t2", t2=P))
            last_e = PL.step("scalar", evac, waits=[tk_tp], sem=sc)
            PT_G[0] = last_e
        evac_of_tt.append(last_e)


    nw_last = None
    for k in range(KD):
        nw_last = PL.step("vector",
                          lambda E, kk=k: E.tensor_scalar_mul(x_nT[:, kk * L:(kk + 1) * L],
                                                              x_nT[:, kk * L:(kk + 1) * L],
                                                              normw_c[:, kk:kk + 1]),
                          waits=[evac_of_tt[-1]], sem=ve)
    xnT_ready = nw_last
    if DBG:
        dma("sync", sy, dbg_xnT.ap(), x_nT[:, 0:1024], waits=[xnT_ready])

    def padf(E):
        E.memset(xc_pre[0][:, 0:8], 0.0)
        return E.memset(xc_pre[1][:, 0:8], 0.0)
    tk_pad = PL.step("vector", padf, waits=(), sem=ve)
    NO = 2 * NG
    slab_last_mm = [None, None, None]
    psum_cons = {}
    xproj_done = [None] * NG
    spill_xc = [None] * NG
    spill_sr = [None] * NG
    silu_done = [None] * NG
    sres_war = [None] * NG
    NQ_A2_BASE = NQ_IN_A1
    for ot in range(NO):
        bi = ot % 3
        if NQ_A2_BASE + ot < len(qchunks):
            emit_qchunk(NQ_A2_BASE + ot)

        def ldw(E, o_=ot, b=bi):
            return E.dma_start(out=wslab[b][:].rearrange("p (k m) -> p k m", m=P),
                               in_=w_inT_h.ap().rearrange("(k p) (ot m) -> ot p k m", p=P, m=P)[o_])
        tk_w = PL.step("gpsimd", ldw, waits=[slab_last_mm[bi]], sem=gp, inck=16)

        for nch in range(NCHL):
            bank = (ot * NCHL + nch) % 4

            def mms(E, b=bi, n=nch, bk=bank):
                i = None
                for k in range(KD):
                    i = E.matmul(pbk[bk][:], wslab[b][:, k * P:(k + 1) * P],
                                 x_nT[:, k * L + n * 512:k * L + (n + 1) * 512],
                                 start=(k == 0), stop=(k == KD - 1))
                return i
            tk_m = PL.step("tensor", mms, waits=[tk_w, xnT_ready, psum_cons.get(bank)], sem=te)
            if nch == NCHL - 1:
                slab_last_mm[bi] = tk_m

            if ot < NG:
                def ev(E, n=nch, bk=bank, b=ot % 2):
                    return E.copy(xc_pre[b][:, 8 + n * 512:8 + (n + 1) * 512], pbk[bk][:])
                tk_e = PL.step("scalar", ev,
                               waits=[tk_m, (silu_done[ot - 2] if ot >= 2 and nch == 0 else None)],
                               sem=sc)
            else:
                g = ot - NG

                def ev(E, n=nch, bk=bank, gg=g):
                    return E.activation(sresO[gg % 2][:, n * 512:(n + 1) * 512], pbk[bk][:],
                                        AF.Silu, bias=0.0, scale=1.0)
                tk_e = PL.step("scalar", ev,
                               waits=[tk_m, (spill_sr[g - 2] if g >= 2 and nch == 0 else None)],
                               sem=sc)
            psum_cons[bank] = tk_e

        if ot < NG:
            g = ot
            if g == 0:
                def mz(E):
                    E.memset(xdacc1[:], 0.0)
                    return E.memset(xdacc2[:], 0.0)
                tk_mz = PL.step("vector", mz, waits=(), sem=ve)
                prev_acc = tk_mz

            def cv(E, gg=g, b=g % 2):
                VC(E, E.tensor_scalar_mul(conv_acc[:], xc_pre[b][:, 5:5 + L], convw[:, gg * 4:gg * 4 + 1]))
                i = None
                for k in range(1, 4):
                    i = E.scalar_tensor_tensor(conv_acc[:], xc_pre[b][:, 5 + k:5 + k + L],
                                               convw[:, gg * 4 + k:gg * 4 + k + 1], conv_acc[:],
                                               OP.mult, OP.add)
                    if k < 3:
                        VC(E, i)
                return i
            tk_c = PL.step("vector", cv,
                           waits=[psum_cons[(ot * NCHL + NCHL - 1) % 4], tk_pad,
                                  (silu_done[g - 1] if g >= 1 else None)], sem=ve)

            def sl(E, gg=g):
                return E.activation(xcO[gg % 2][:], conv_acc[:], AF.Silu,
                                    bias=vecs[:, gg * 4:gg * 4 + 1], scale=1.0)
            tk_s = PL.step("scalar", sl,
                           waits=[tk_c, (spill_xc[g - 2] if g >= 2 else None),
                                  (xproj_done[g - 2] if g >= 2 else None)], sem=sc)
            silu_done[g] = tk_s
            spill_xc[g] = dma("sync", sy, xc_sp[g * P:(g + 1) * P, :], xcO[g % 2][:], waits=[tk_s])
            for nch in range(NCHL):
                def mmx(E, gg=g, b=g % 2, n=nch):
                    E.matmul(pbk[4][:], xprojT[:, gg * XDR:gg * XDR + P],
                             xcO[b][:, n * 512:(n + 1) * 512], start=True, stop=True)
                    return E.matmul(pM[:], xprojT[:, gg * XDR + P:gg * XDR + XDR],
                                    xcO[b][:, n * 512:(n + 1) * 512], start=True, stop=True)
                tk_mm2 = PL.step("tensor", mmx, waits=[tk_s, prev_acc], sem=te)

                def acc(E, n=nch):
                    E.tensor_add(xdacc1[:, n * 512:(n + 1) * 512],
                                 xdacc1[:, n * 512:(n + 1) * 512], pbk[4][:])
                    return E.tensor_add(xdacc2[:, n * 512:(n + 1) * 512],
                                        xdacc2[:, n * 512:(n + 1) * 512], pM[:])
                prev_acc = PL.step("vector", acc, waits=[tk_mm2], sem=ve)
                xproj_done[g] = tk_mm2
        else:
            g = ot - NG
            spill_sr[g] = dma("sync", sy, sres_sp[g * P:(g + 1) * P, :], sresO[g % 2][:],
                              waits=[psum_cons[(ot * NCHL + NCHL - 1) % 4]])

    if DBG:
        dma("sync", sy, dbg_xc.ap(), xc_sp[0:P, 0:1024], waits=[spill_xc[0]])
        dma("sync", sy, dbg_sr.ap(), sres_sp[0:P, 0:1024], waits=[spill_sr[0]])

    if DBG:
        dma("sync", sy, dbg_wg.ap(), cc_gu_out[0:P, :], waits=[AGTOK["ag1"]])

    tk_xw1 = dma("sync", sy, cc_xd_in[0:RANK, :], xdacc1[:], waits=[prev_acc])
    tk_xw2 = dma("sync", sy, cc_xd_in[RANK:XDR, :], xdacc2[:], waits=[prev_acc])
    tk_ar = PL.step("gpsimd",
                    lambda E: E.collective_compute("AllReduce", OP.add, replica_groups=groups2,
                                                   ins=[cc_xd_in[:]], outs=[cc_xd_out[:]]),
                    waits=[tk_xw1, tk_xw2], sem=cc, inck=1)
    tk_dr = dma("gpsimd", gp, delta_r[:], cc_xd_out[0:RANK, :], waits=[tk_ar])
    if DBG:
        dma("sync", sy, dbg_xd.ap(), cc_xd_out[:, 0:512], waits=[tk_ar])
    tk_br = dma("sync", sy, brow[:], cc_xd_out[RANK:XDR, :], waits=[tk_ar])
    tk_bb = PL.step("vector", lambda E: E.tensor_copy(browb[:], brow[:]), waits=[tk_br], sem=ve)

    if stop_after == "A":
        with nc.Block() as blk:
            PL.play(blk)
        return nc

    tk_selb = dma("sync", sy, selb[:], sel_h.ap(),
                  waits=[prev_acc, silu_done[NG - 1], spill_sr[NG - 1]])
    ywr = []
    fin_read = None
    tk_fb = None
    prev_tkd = None
    BCB = [(b_bc, c_bc), (b_bc2, c_bc2)]
    bc_ready_h = []
    bc_toks = []
    for hf in range(2):
        for i in range(NST):
            for nch in range(NCHB):
                def bc1(E, ii=i, n=nch, h=hf):
                    E.matmul(pbk[4][:], selb[:, ii * P:(ii + 1) * P],
                             browb[:, h * LH + n * 512:h * LH + (n + 1) * 512], start=True, stop=True)
                    return E.matmul(pbk[0][:], selb[:, (NST + ii) * P:(NST + ii + 1) * P],
                                    browb[:, h * LH + n * 512:h * LH + (n + 1) * 512],
                                    start=True, stop=True)
                tk_bm = PL.step("tensor", bc1,
                                waits=[tk_bb, tk_selb, (bc_toks[-1] if bc_toks else None),
                                       (prev_acc if not bc_toks else None)],
                                sem=te)

                def bc2(E, ii=i, n=nch, h=hf):
                    E.copy(BCB[h][0][:, ii * LH + n * 512:ii * LH + (n + 1) * 512], pbk[4][:])
                    return E.copy(BCB[h][1][:, ii * LH + n * 512:ii * LH + (n + 1) * 512],
                                  pbk[0][:])
                bc_toks.append(PL.step("scalar", bc2,
                                       waits=[tk_bm] + ([prev_acc, tk_bb] if not bc_toks else []),
                                       sem=sc))
        bc_ready_h.append(bc_toks[-1])

    for hf in range(2):
        bc_ready = bc_ready_h[hf]
        b_bch, c_bch = BCB[hf]
        for g in range(NG):
            bg = g % 2

            def dtm(E, gg=g, h=hf):
                E.matmul(pbk[2][:], dtprojT[:, gg * P:(gg + 1) * P],
                         delta_r[:, h * LH:h * LH + 512], start=True, stop=True)
                return E.matmul(pbk[3][:], dtprojT[:, gg * P:(gg + 1) * P],
                                delta_r[:, h * LH + 512:h * LH + 1024], start=True, stop=True)
            tk_dt = PL.step("tensor", dtm, waits=[tk_dr, (None if (hf, g) == (0, 0) else sp_done2)],
                            sem=te)

            def spl(E, gg=g):
                i = None
                for n in range(NCHB):
                    src = pbk[2 + n]
                    E.activation(sp_t1[:, n * 512:(n + 1) * 512], src[:], AF.Relu,
                                 bias=vecs[:, gg * 4 + 1:gg * 4 + 2], scale=1.0)
                    SC(E, E.activation(sp_t2[:, n * 512:(n + 1) * 512], src[:], AF.Abs,
                                       bias=vecs[:, gg * 4 + 1:gg * 4 + 2], scale=1.0))
                    SC(E, E.activation(sp_t2[:, n * 512:(n + 1) * 512], sp_t2[:, n * 512:(n + 1) * 512],
                                       AF.Exp, bias=0.0, scale=-1.0))
                    i = E.activation(sp_t2[:, n * 512:(n + 1) * 512], sp_t2[:, n * 512:(n + 1) * 512],
                                     AF.Ln, bias=1.0, scale=1.0)
                return i
            sp_done = PL.step("scalar", spl, waits=[tk_dt, prev_tkd], sem=sc)
            sp_done2 = sp_done

            tk_xc = PL.step("gpsimd", lambda E, gg=g, b=bg, h=hf: E.dma_start(
                out=xcB[b][:], in_=xc_sp[gg * P:(gg + 1) * P, h * LH:(h + 1) * LH]),
                waits=[spill_xc[g], (ywr[-2] if len(ywr) >= 2 else None)], sem=gp, inck=16)
            tk_sr = PL.step("gpsimd", lambda E, gg=g, b=bg, h=hf: E.dma_start(
                out=srB[b][:], in_=sres_sp[gg * P:(gg + 1) * P, h * LH:(h + 1) * LH]),
                waits=[spill_sr[g], (ywr[-2] if len(ywr) >= 2 else None)], sem=gp, inck=16)

            tk_d = PL.step("vector", lambda E: E.tensor_add(deltg[:], sp_t1[:], sp_t2[:]),
                           waits=[sp_done, bc_ready], sem=ve)
            prev_tkd = tk_d
            tk_u = PL.step("vector", lambda E, b=bg: E.tensor_mul(ug[:], deltg[:], xcB[b][:]),
                           waits=[tk_d, tk_xc], sem=ve)
            if DBG and hf == 0 and g == 0:
                dma("sync", sy, dbg_dl.ap(), deltg[:, 0:512], waits=[tk_u])

            scan_t = [None, None]
            hC_t = [None, None]
            ya_t = [None, None]
            for i in range(NST):
                b2 = i % 2

                def dac(E, gg=g, ii=i, b=b2):
                    return E.activation(dAb[b][:], deltg[:], AF.Exp, bias=0.0,
                                        scale=a_sb[:, gg * NST + ii:gg * NST + ii + 1])
                tk_da = PL.step("scalar", dac, waits=[tk_d, scan_t[b2]], sem=sc)

                def dbu(E, ii=i, b=b2):
                    return E.tensor_mul(dBu[b][:], ug[:], b_bch[:, ii * LH:(ii + 1) * LH])
                tk_db = PL.step("vector", dbu, waits=[tk_u, ya_t[b2]], sem=ve)

                def scn(E, gg=g, ii=i, b=b2, h=hf):
                    init = 0.0 if h == 0 else state[:, gg * NST + ii:gg * NST + ii + 1]
                    return E.tensor_tensor_scan(hbf[b][:], dAb[b][:], dBu[b][:], init,
                                                OP.mult, OP.add)
                tk_sn = PL.step("vector", scn, waits=[tk_da, tk_db, hC_t[b2]], sem=ve)
                scan_t[b2] = tk_sn

                tk_ss = None
                if hf == 0:
                    tk_ss = PL.step("scalar",
                                    lambda E, gg=g, ii=i, b=b2: E.copy(
                                        state[:, gg * NST + ii:gg * NST + ii + 1],
                                        hbf[b][:, LH - 1:LH]),
                                    waits=[tk_sn], sem=sc)

                def hc(E, ii=i, b=b2):
                    return E.tensor_mul(dBu[b][:], hbf[b][:], c_bch[:, ii * LH:(ii + 1) * LH])
                tk_hc = PL.step("vector", hc, waits=[tk_sn], sem=ve)
                hC_t[b2] = tk_hc

                def ya(E, ii=i, b=b2):
                    E.matmul(pbk[0][:], identb[:], dBu[b][:, 0:512], start=(ii == 0),
                             stop=(ii == NST - 1))
                    return E.matmul(pbk[1][:], identb[:], dBu[b][:, 512:1024], start=(ii == 0),
                                    stop=(ii == NST - 1))
                tk_ya = PL.step("tensor", ya,
                                waits=[tk_hc, tk_ss, (fin_read if i == 0 else None)], sem=te)
                ya_t[b2] = tk_ya

            def fin(E, gg=g, b=bg):
                E.scalar_tensor_tensor(yf32[:, 0:512], xcB[b][:, 0:512],
                                       vecs[:, gg * 4 + 2:gg * 4 + 3], pbk[0][:], OP.mult, OP.add)
                return E.scalar_tensor_tensor(yf32[:, 512:1024], xcB[b][:, 512:1024],
                                              vecs[:, gg * 4 + 2:gg * 4 + 3], pbk[1][:],
                                              OP.mult, OP.add)
            fin_read = PL.step("vector", fin, waits=[ya_t[(NST - 1) % 2], tk_sr, tk_fb], sem=ve)

            tk_fb = PL.step("vector", lambda E, b=bg: E.tensor_mul(ybf[b][:], yf32[:], srB[b][:]),
                            waits=[fin_read, (ywr[-2] if len(ywr) >= 2 else None)], sem=ve)
            ywr.append(dma("sync", sy, cc_y_in[hf * DH + g * P:hf * DH + (g + 1) * P, :],
                           ybf[bg][:], waits=[tk_fb]))

    if DBG:
        dma("sync", sy, dbg_y.ap(), cc_y_in[0:P, 0:512], waits=[ywr[0]])
    tk_ag_h0 = PL.step("gpsimd",
                       lambda E: E.collective_compute("AllGather", OP.bypass, replica_groups=groups2,
                                                      ins=[cc_y_in[0:DH, :]], outs=[cc_y_ag[0:2 * DH, :]]),
                       waits=[ywr[NG - 1], ywr[NG - 2]], sem=cc, inck=1)
    tk_a2a = PL.step("gpsimd",
                     lambda E: E.collective_compute("AllGather", OP.bypass, replica_groups=groups2,
                                                    ins=[cc_y_in[DH:2 * DH, :]],
                                                    outs=[cc_y_ag[2 * DH:4 * DH, :]]),
                     waits=[ywr[-1], ywr[-2]], sem=cc, inck=1)

    if stop_after == "B":
        with nc.Block() as blk:
            PL.play(blk)
        return nc

    def mk_yload(r):
        def yload(E):
            pid = E.partition_id()
            off = (pid % 2) * (2 * DH * LH) + r * (DH * LH)
            ap = bass.AP(cc_y_ag, off, [[LH, P], [P * LH, NG], [1, LH]])
            return E.dma_start(
                out=y_sb[:].rearrange("p (k t) -> p k t", t=LH)[:, r * NG:(r + 1) * NG, :],
                in_=ap)
        return yload
    tk_yl0 = PL.step("gpsimd", mk_yload(0), waits=[tk_ag_h0, tk_a2a, tk_fb, fin_read], sem=gp, inck=16)
    tk_yl = PL.step("gpsimd", mk_yload(1), waits=(), sem=gp, inck=16)
    oslab_last = [None, None]
    opsum = {}
    mo_evs = []
    for mt in range(NMT):
        bi = mt % 2
        tk_o = PL.step("gpsimd",
                       lambda E, m=mt, b=bi: E.dma_start(
                           out=oslab[b][:].rearrange("p (k m) -> p k m", m=P),
                           in_=oprojT_h.ap().rearrange("(k p) (mt m) -> mt p k m", p=P, m=P)[m]),
                       waits=[oslab_last[bi], (tk_a2a if mt < 2 else None)], sem=gp, inck=16)
        for nch in range(NCHB):
            bank = (mt * NCHB + nch) % 4

            def mmo(E, b=bi, n=nch, bk=bank):
                i = None
                for k in range(KDIN):
                    i = E.matmul(pbk[bk][:], oslab[b][:, k * P:(k + 1) * P],
                                 y_sb[:, k * LH + n * 512:k * LH + (n + 1) * 512],
                                 start=(k == 0), stop=(k == KDIN - 1))
                return i
            tk_m = PL.step("tensor", mmo, waits=[tk_o, tk_yl, opsum.get(bank)], sem=te)
            if nch == NCHB - 1:
                oslab_last[bi] = tk_m
            tk_e = PL.step("scalar",
                           lambda E, m=mt, n=nch, bk=bank: E.copy(
                               mo[:, m * LH + n * 512:m * LH + (n + 1) * 512], pbk[bk][:]),
                           waits=[tk_m], sem=sc)
            opsum[bank] = tk_e
            mo_evs.append(tk_e)
    mo_ready = mo_evs[-1]

    if stop_after == "C":
        with nc.Block() as blk:
            PL.play(blk)
        return nc

    if DBG:
        dma("sync", sy, dbg_mo.ap(), mo[:, 0:512], waits=[mo_ready])
    tk_xT = dma("sync", sy, x2[:].rearrange("p (k t) -> p k t", t=LH),
                xT_h.ap().rearrange("(k p) t -> p k t", p=P), waits=[mo_ready, tk_yl])

    def ln_stats(src_fn, width, wtoks, rowbuf, mslot_off, sslot_off, inv_off=None):
        assert width <= 512
        mm_toks = []
        for k in range(KD):
            def sqf(E, kk=k, b=k % 2):
                return E.activation(sqDB[b][:, 0:width], src_fn(kk), AF.Square, bias=0.0,
                                    scale=1.0)
            tk_sq = PL.step("scalar", sqf,
                            waits=list(wtoks) + ([mm_toks[-2]] if len(mm_toks) >= 2 else []),
                            sem=sc)

            def mms(E, kk=k, b=k % 2):
                E.matmul(pR[:, 0:width], onesf[:, 0:1], src_fn(kk),
                         start=(kk == 0), stop=(kk == KD - 1))
                return E.matmul(pM[0:1, 0:width], onesf[:, 0:1], sqDB[b][:, 0:width],
                                start=(kk == 0), stop=(kk == KD - 1))
            mm_toks.append(PL.step("tensor", mms, waits=list(wtoks) + [tk_sq], sem=te))

        def evr(E):
            E.copy(rowbuf[0:1, mslot_off:mslot_off + width], pR[:, 0:width])
            return E.copy(rowbuf[0:1, sslot_off:sslot_off + width], pM[0:1, 0:width])
        tk_ev = PL.step("scalar", evr, waits=[mm_toks[-1]], sem=sc)

        def rowm(E):
            VC(E, E.tensor_scalar_mul(rowbuf[0:1, mslot_off:mslot_off + width],
                                      rowbuf[0:1, mslot_off:mslot_off + width], -1.0 / D))
            VC(E, E.tensor_mul(sqD[0:1, 0:width], rowbuf[0:1, mslot_off:mslot_off + width],
                               rowbuf[0:1, mslot_off:mslot_off + width]))
            return E.scalar_tensor_tensor(rowbuf[0:1, sslot_off:sslot_off + width],
                                          rowbuf[0:1, sslot_off:sslot_off + width], 1.0 / D,
                                          sqD[0:1, 0:width], OP.mult, OP.subtract)
        tk_rm = PL.step("vector", rowm, waits=[tk_ev], sem=ve)
        if inv_off is not None:
            def pre(E):
                VC(E, E.tensor_mul(sqD[0:1, 0:width], rowbuf[0:1, inv_off:inv_off + width],
                                   rowbuf[0:1, inv_off:inv_off + width]))
                return E.tensor_mul(rowbuf[0:1, sslot_off:sslot_off + width],
                                    rowbuf[0:1, sslot_off:sslot_off + width], sqD[0:1, 0:width])
            tk_rm = PL.step("vector", pre, waits=[tk_rm], sem=ve)

        def rs(E):
            SC(E, E.activation(rowbuf[0:1, sslot_off:sslot_off + width],
                               rowbuf[0:1, sslot_off:sslot_off + width], AF.Ln, bias=EPS_LN,
                               scale=1.0))
            return E.activation(rowbuf[0:1, sslot_off:sslot_off + width],
                                rowbuf[0:1, sslot_off:sslot_off + width], AF.Exp, bias=0.0,
                                scale=-0.5)
        tk_rs = PL.step("scalar", rs, waits=[tk_rm], sem=sc)
        vw = [tk_rs]
        if inv_off is not None:
            tk_iv = PL.step("vector",
                            lambda E: E.tensor_mul(rowbuf[0:1, sslot_off:sslot_off + width],
                                                   rowbuf[0:1, sslot_off:sslot_off + width],
                                                   rowbuf[0:1, inv_off:inv_off + width]),
                            waits=[tk_rs], sem=ve)
            vw = [tk_iv]
        return PL.step("vector",
                       lambda E: E.tensor_mul(rowbuf[0:1, mslot_off:mslot_off + width],
                                              rowbuf[0:1, mslot_off:mslot_off + width],
                                              rowbuf[0:1, sslot_off:sslot_off + width]),
                       waits=vw, sem=ve)

    def bcast_row_at(dst, dstoff, rowbuf, off, width, wtoks):
        def bm(E):
            return E.matmul(pbk[4][:, 0:width], onesr[0:1, :], rowbuf[0:1, off:off + width],
                            start=True, stop=True)
        t1 = PL.step("tensor", bm, waits=list(wtoks), sem=te)
        return PL.step("scalar",
                       lambda E: E.copy(dst[:, dstoff:dstoff + width], pbk[4][:, 0:width]),
                       waits=[t1], sem=sc)

    def ln_to_bc(src_fn, wtoks):
        done = []
        prev = list(wtoks)
        for h5 in range(LH // 512):
            tok = ln_stats(lambda k, h=h5: src_fn(k)[:, h * 512:(h + 1) * 512], 512,
                           prev, srow, 0, 512)
            b1 = bcast_row_at(bigbc1, h5 * 512, srow, 512, 512, [tok])
            b2 = bcast_row_at(bigbc2, h5 * 512, srow, 0, 512, [tok, b1])
            done.extend([b1, b2])
            prev = list(wtoks) + [b1, b2]
        return done

    if stop_after == "D0":
        with nc.Block() as blk:
            PL.play(blk)
        return nc

    st_x = ln_to_bc(lambda k: x2[:, k * LH:(k + 1) * LH], [tk_xT])
    if stop_after == "D1":
        with nc.Block() as blk:
            PL.play(blk)
        return nc
    att_toks = []
    for k in range(KD):
        def attf(E, kk=k):
            i = None
            for n in range(LH // 512):
                sl = slice(kk * LH + n * 512, kk * LH + (n + 1) * 512)
                bl = slice(n * 512, (n + 1) * 512)
                VC(E, E.tensor_mul(sqD[:], x2[:, sl], bigbc1[:, bl]))
                VC(E, E.tensor_add(sqD[:], sqD[:], bigbc2[:, bl]))
                i = E.tensor_add(mo[:, sl], mo[:, sl], sqD[:])
            return i
        att_toks.append(PL.step("vector", attf,
                                waits=st_x + [tk_xT, mo_ready] +
                                ([att_toks[-1]] if att_toks else []), sem=ve))

    if DBG:
        dma("sync", sy, dbg_att.ap(), mo[:, 0:512], waits=[att_toks[-1]])
    st_a = ln_to_bc(lambda k: mo[:, k * LH:(k + 1) * LH], [att_toks[-1]])
    x2_toks = []
    for k in range(KD):
        def x2f(E, kk=k):
            i = None
            for n in range(LH // 512):
                sl = slice(kk * LH + n * 512, kk * LH + (n + 1) * 512)
                bl = slice(n * 512, (n + 1) * 512)
                VC(E, E.tensor_mul(sqD[:], mo[:, sl], bigbc1[:, bl]))
                VC(E, E.tensor_add(sqD[:], sqD[:], bigbc2[:, bl]))
                i = E.tensor_add(x2[:, sl], x2[:, sl], sqD[:])
            return i
        x2_toks.append(PL.step("vector", x2f,
                               waits=st_a + ([x2_toks[-1]] if x2_toks else []), sem=ve))

    st_2 = ln_to_bc(lambda k: x2[:, k * LH:(k + 1) * LH], [x2_toks[-1]])
    x2n_toks = []
    for k in range(KD):
        def x2nf(E, kk=k):
            i = None
            for n in range(LH // 512):
                sl = slice(kk * LH + n * 512, kk * LH + (n + 1) * 512)
                bl = slice(n * 512, (n + 1) * 512)
                VC(E, E.tensor_mul(sqD[:], x2[:, sl], bigbc1[:, bl]))
                VC(E, E.tensor_add(sqD[:], sqD[:], bigbc2[:, bl]))
                i = E.tensor_copy(x2n[:, sl], sqD[:])
            return i
        x2n_toks.append(PL.step("vector", x2nf,
                                waits=st_2 + [mo_evs[-1]] +
                                ([x2n_toks[-1]] if x2n_toks else []), sem=ve))
    x2n_ready = x2n_toks[-1]
    if stop_after == "D3":
        with nc.Block() as blk:
            PL.play(blk)
        return nc


    if DBG:
        dma("sync", sy, dbg_x2n.ap(), x2n[:, 0:512], waits=[x2n_ready])
    NQG = (2 * HID) // 8 // P
    ag_gu = cc_gu_out.ap().rearrange("(r k p) (mtl m) -> r mtl p k m", r=8, p=P, m=P)
    ag_wp = cc_wp_out.ap().rearrange("(r k p) (m2 m) -> r m2 p k m", r=8, p=P, m=P)

    tk_x2sp = dma("sync", sy, x2_sp.ap().rearrange("(k p) t -> p k t", p=P),
                  x2[:].rearrange("p (k t) -> p k t", t=LH), waits=[x2n_ready])

    out_wr_toks = []
    gsl_last = [None] * 4
    wsl_last = [None, None]
    prev_outf = None
    pbk4_guard = None
    prev_evr = None
    wp_done_blk = None
    qk = NKH // 2
    MOFF = 512
    SOFF = 512 + FCH
    for blk in range(LH // 512):
        B0 = blk * 512
        h_of_ht = []
        psg = {}
        for ht in range(NHT):
            b4 = (ht % 2) * 2
            tk_g = PL.step("sync",
                           lambda E, m=ht, b=b4: E.dma_start(
                               out=guslab[b][:].rearrange("p (k m) -> p k m", m=P),
                               in_=ag_gu[m // NQG, m % NQG]),
                           waits=[(AGTOK["ag1"] if (blk == 0 and ht < 2) else None),
                                  (x2n_ready if (blk == 0 and ht < 2) else None),
                                  (tk_x2sp if (blk == 0 and ht < 2) else None),
                                  gsl_last[b4]],
                           sem=sy, inck=16)
            tk_u2 = PL.step("scalar",
                            lambda E, m=NHT + ht, b=b4 + 1: E.dma_start(
                                out=guslab[b][:].rearrange("p (k m) -> p k m", m=P),
                                in_=ag_gu[m // NQG, m % NQG]),
                            waits=[(x2n_ready if (blk == 0 and ht < 2) else None),
                                   (AGTOK["ag1"] if (blk == 0 and ht < 2) else None),
                                   gsl_last[b4 + 1]], sem=sd, inck=16)
            last_h = None
            for nch in range(2):
                bkg = 2 * (nch % 2)
                bku = bkg + 1

                def mmg(E, b=b4, bk=bkg, n=nch, b0=B0):
                    i = None
                    for k in range(KD):
                        i = E.matmul(pbk[bk][:, 0:FCH], guslab[b][:, k * P:(k + 1) * P],
                                     x2n[:, k * LH + b0 + n * FCH:k * LH + b0 + (n + 1) * FCH],
                                     start=(k == 0), stop=(k == KD - 1))
                    return i
                tk_mg = PL.step("tensor", mmg, waits=[tk_g, x2n_ready, psg.get(bkg)], sem=te)

                def mmu(E, b=b4 + 1, bk=bku, n=nch, b0=B0):
                    i = None
                    for k in range(KD):
                        i = E.matmul(pbk[bk][:, 0:FCH], guslab[b][:, k * P:(k + 1) * P],
                                     x2n[:, k * LH + b0 + n * FCH:k * LH + b0 + (n + 1) * FCH],
                                     start=(k == 0), stop=(k == KD - 1))
                    return i
                tk_mu = PL.step("tensor", mmu, waits=[tk_u2, psg.get(bku)], sem=te)

                def evg(E, bk=bkg):
                    return E.activation(bc4a[:], pbk[bk][:, 0:FCH], AF.Silu, bias=0.0, scale=1.0)
                tk_eg = PL.step("scalar", evg, waits=[tk_mg, tk_mu, last_h], sem=sc)

                def evu(E, bk=bku):
                    return E.copy(bc4b[:], pbk[bk][:, 0:FCH])
                tk_eu = PL.step("scalar", evu, waits=[tk_mu], sem=sc)

                def hmk(E, t=ht, n=nch):
                    return E.tensor_mul(h_sb[:, t * 512 + n * FCH:t * 512 + (n + 1) * FCH],
                                        bc4a[:], bc4b[:])
                last_h = PL.step("vector", hmk,
                                 waits=[tk_eg, tk_eu,
                                        (tk_x2sp if (blk == 0 and ht == 0) else None),
                                        (wp_done_blk if (blk > 0 and ht < 2) else None)],
                                 sem=ve)
                psg[bkg] = tk_eg
                psg[bku] = tk_eu
            gsl_last[b4] = tk_mg
            gsl_last[b4 + 1] = tk_mu
            h_of_ht.append(last_h)
        h_all = h_of_ht[-1]

        hrow_done = None
        h2guard = [None, None]
        for ht in range(NHT):
            def h2f(E, t=ht, b=ht % 2):
                return E.activation(h2b[b][:], h_sb[:, t * 512:(t + 1) * 512], AF.Square,
                                    bias=0.0, scale=1.0)
            tk_h2 = PL.step("scalar", h2f, waits=[h_of_ht[ht], h2guard[ht % 2]], sem=sc)

            def hrow(E, t=ht, b=ht % 2):
                E.matmul(pR[:, 0:FCH], onesb[:, 0:1], h2b[b][:, 0:FCH], start=(t == 0),
                         stop=(t == NHT - 1))
                return E.matmul(pM[0:1, 0:FCH], onesb[:, 0:1], h2b[b][:, FCH:2 * FCH],
                                start=(t == 0), stop=(t == NHT - 1))
            hrow_done = PL.step("tensor", hrow,
                                waits=[tk_h2, (prev_evr if ht == 0 else None)], sem=te)
            h2guard[ht % 2] = hrow_done

        def evr2(E):
            E.copy(srow2[0:1, 0:FCH], pR[0:1, 0:FCH])
            return E.copy(srow2[0:1, FCH:2 * FCH], pM[0:1, 0:FCH])
        tk_evr = PL.step("scalar", evr2, waits=[hrow_done], sem=sc)
        prev_evr = tk_evr
        tk_i1 = PL.step("vector",
                        lambda E: E.tensor_scalar_mul(srow2[0:1, 0:512], srow2[0:1, 0:512],
                                                      1.0 / HID),
                        waits=[tk_evr], sem=ve)

        def invs(E):
            SC(E, E.activation(srow2[0:1, 0:512], srow2[0:1, 0:512], AF.Ln, bias=EPS_RMS,
                               scale=1.0))
            return E.activation(srow2[0:1, 0:512], srow2[0:1, 0:512], AF.Exp, bias=0.0,
                                scale=-0.5)
        tk_i2 = PL.step("scalar", invs, waits=[tk_i1], sem=sc)

        for tc in range(2):
            c0 = B0 + tc * FCH
            tk_xc2 = dma("gpsimd", gp, x2ch[:].rearrange("p (k t) -> p k t", t=FCH),
                         x2_sp.ap().rearrange("(k p) t -> p k t", p=P)[:, :, c0:c0 + FCH],
                         waits=[tk_x2sp, prev_outf])
            foev = []
            prev_fo = pbk4_guard
            for mt in range(NMT):
                last_q = None
                for q in range(2):
                    bw = q % 2
                    qeng, qsem = (("gpsimd", gp) if bw == 0 else ("sync", sy))

                    def ldq(E, m=mt, b=bw, qq=q):
                        sw = ag_wp[(m * P) // QP_R, (m * P % QP_R) // P]
                        return E.dma_start(out=wpslab[b][:].rearrange("p (k m) -> p k m", m=P),
                                           in_=sw[:, qq * qk:(qq + 1) * qk, :])
                    tk_wq = PL.step(qeng, ldq,
                                    waits=[wsl_last[bw],
                                           (AGTOK["ag2"] if (blk == 0 and tc == 0 and mt == 0
                                                             and q < 2) else None),
                                           (h_all if (tc == 0 and mt == 0 and q < 2) else None)],
                                    sem=qsem, inck=16)

                    def mmw(E, b=bw, qq=q, t=tc):
                        i = None
                        for k in range(qk):
                            kk = qq * qk + k
                            i = E.matmul(pbk[4][:, 0:FCH], wpslab[b][:, k * P:(k + 1) * P],
                                         h_sb[:, kk * 512 + t * FCH:kk * 512 + (t + 1) * FCH],
                                         start=(qq == 0 and k == 0), stop=(qq == 1 and k == qk - 1))
                        return i
                    last_q = PL.step("tensor", mmw,
                                     waits=[tk_wq, h_all, (prev_fo if q == 0 else None)],
                                     sem=te)
                    wsl_last[bw] = last_q
                tk_fo = PL.step("scalar",
                                lambda E, m=mt: E.copy(fo[:, m * FCH:(m + 1) * FCH],
                                                       pbk[4][:, 0:FCH]),
                                waits=[last_q, (prev_outf if mt == 0 else None)], sem=sc)
                prev_fo = tk_fo
                foev.append(tk_fo)

            st4 = ln_stats(lambda k: fo[:, k * FCH:(k + 1) * FCH], FCH,
                           [foev[-1], tk_i2], srow2, MOFF, SOFF, inv_off=tc * FCH)
            tb_s4 = bcast_row_at(bc4a, 0, srow2, SOFF, FCH, [st4])
            tb_m4 = bcast_row_at(bc4b, 0, srow2, MOFF, FCH, [st4, tb_s4])

            for k in range(KD):
                bo = k % 2

                def outf(E, kk=k, b=bo):
                    VC(E, E.tensor_mul(outD[b][:], fo[:, kk * FCH:(kk + 1) * FCH], bc4a[:]))
                    VC(E, E.tensor_add(outD[b][:], outD[b][:], bc4b[:]))
                    return E.tensor_add(outD[b][:], outD[b][:], x2ch[:, kk * FCH:(kk + 1) * FCH])
                tk_of = PL.step("vector", outf,
                                waits=[tb_s4, tb_m4, tk_xc2,
                                       (out_wr_toks[-2] if len(out_wr_toks) >= 2 else None)],
                                sem=ve)
                out_wr_toks.append(dma("sync", sy, out_h[k * P:(k + 1) * P, c0:c0 + FCH],
                                       outD[bo][:], waits=[tk_of]))
                prev_outf = tk_of
            pbk4_guard = tb_m4
        wp_done_blk = wsl_last[1]

    PL.step("sync", lambda E: None,
            waits=[out_wr_toks[-1], (sy, sy.n), (sy2, sy2.n), (gp, gp.n), (gp2, gp2.n),
                   (sd, sd.n), (sd2, sd2.n)])
    if stop_after == "D4":
        with nc.Block() as blk:
            PL.play(blk)
        return nc

    with nc.Block() as blk:
        PL.play(blk)
    return nc


def _prep_core(inputs, c, cfg):
    global EPS_LN, EPS_RMS
    L = cfg["L"]; D = cfg["D"]; DIN = cfg["DIN"]; NST = cfg["NST"]
    RANK = cfg["RANK"]; HID = cfg["HID"]
    DH = DIN // 2; NG = DH // P; KD = D // P; LH = L // 2
    b, j = c // 2, c % 2
    A = lambda v: np.ascontiguousarray(v)

    x = inputs["L0_attn_x"][b]
    w_in = inputs["L0_attn_in_proj_w"]
    w_inT = A(np.concatenate([w_in[j * DH:(j + 1) * DH],
                              w_in[DIN + j * DH:DIN + (j + 1) * DH]], 0).T)
    convw = A(inputs["L0_attn_conv_w"][j * DH:(j + 1) * DH, 0, :])
    vecs = np.stack([inputs["L0_attn_conv_b"][j * DH:(j + 1) * DH],
                     inputs["L0_attn_dt_proj_b"][j * DH:(j + 1) * DH],
                     inputs["L0_attn_D"][j * DH:(j + 1) * DH],
                     np.zeros(DH, np.float32)], 1).reshape(NG, P, 4)
    xprojT = A(inputs["L0_attn_x_proj_w"][:, j * DH:(j + 1) * DH].T)
    dtprojT = A(inputs["L0_attn_dt_proj_w"][j * DH:(j + 1) * DH, :].T)
    alog = A(inputs["L0_attn_A_log"][j * DH:(j + 1) * DH])
    oprojT = A(inputs["L0_attn_out_proj_w"].T)
    normw = A(inputs["L0_attn_norm_w"].reshape(KD, P).T)
    QGU_R = 2 * 4 * D // 8 * (HID // (4 * D)) if False else (2 * HID) // 8
    QP_R = D // 8
    wgu_q = A(inputs["L0_ffn_w_gate_up"][c * QGU_R:(c + 1) * QGU_R])
    wp_q = A(inputs["L0_ffn_w_proj"][c * QP_R:(c + 1) * QP_R])
    xT = A(x.T[:, j * LH:(j + 1) * LH])

    sel = np.zeros((2 * NST, 2 * NST * P), dtype=NBF)
    for i in range(2 * NST):
        sel[i, i * P:(i + 1) * P] = NBF(1.0)

    return dict(
        x_tm=A(x), x_fm=xT, w_inT=w_inT, convw=convw, vecs=A(vecs.astype(np.float32)),
        xprojT=xprojT, dtprojT=dtprojT, alog=alog, oprojT=oprojT, normw=normw,
        wgu_q=wgu_q, wp_q=wp_q,
        ident_bf=np.eye(P, dtype=NBF),
        ones_f=np.ones((P, 4), np.float32),
        ones_r=np.ones((1, P), np.float32),
        ones_b=np.ones((P, 4), NBF),
        consts=np.broadcast_to(np.array([EPS_LN, 1.0, EPS_RMS, 0.0], np.float32),
                               (P, 4)).copy(),
        sel_bf=sel,
    )


def kernel(**inputs):
    cfg = CFG
    inputs = {k: np.asarray(v) for k, v in inputs.items()}
    nc = build(cfg)
    in_maps = [_prep_core(inputs, c, cfg) for c in range(8)]
    res = run_bass_kernel_spmd(nc, in_maps, core_ids=list(range(8)))
    L = cfg["L"]; D = cfg["D"]; LH = L // 2
    out = np.empty((4, L, D), np.float32)
    for c in range(8):
        b, j = c // 2, c % 2
        out[b, j * LH:(j + 1) * LH, :] = res.results[c]["out_fm"].T
    return out


# version 18 (final)
# speedup vs baseline: 1.1687x; 1.1342x over previous
import os
import numpy as np
import ml_dtypes
import concourse.bass as bass
import concourse.mybir as mybir
from concourse.bass_utils import run_bass_kernel_spmd

F32 = mybir.dt.float32
BF16 = mybir.dt.bfloat16
AF = mybir.ActivationFunctionType
OP = mybir.AluOpType
AX = mybir.AxisListType
NBF = ml_dtypes.bfloat16

P = 128
RND = 12582912.0
EPS_LN = 1e-5
EPS_RMS = 1.1920929e-07

CFG = dict(L=2048, D=2048, DIN=4096, NST=16, RANK=128, HID=8192)
ENGS = ("sync", "gpsimd", "tensor", "vector", "scalar")
KB = 1024


class Sem:
    __slots__ = ("h", "n")

    def __init__(self, h):
        self.h = h
        self.n = 0


class Plan:
    def __init__(self):
        self.steps = []
        self.last_dma = {}
        self.dma_sems = {}
        self.dma_rr = {}

    def set_dma_sems(self, eng, sems):
        self.dma_sems[eng] = sems
        self.dma_rr[eng] = 0

    def step(self, eng, fn, waits=(), sem=None, inck=1):
        waits = [w for w in waits if w is not None]
        if inck == 16 and sem is not None:
            sems = self.dma_sems.get(eng)
            if sems:
                r = self.dma_rr[eng]
                self.dma_rr[eng] = (r + 1) % len(sems)
                sem = sems[r]
            prev = self.last_dma.get((eng, id(sem)))
            if prev is not None:
                waits.append(prev)
        tok = None
        if sem is not None:
            sem.n += inck
            tok = (sem, sem.n)
            if inck == 16:
                self.last_dma[(eng, id(sem))] = tok
        self.steps.append((eng, tuple(waits), fn, sem, inck))
        return tok

    def play(self, block):
        def mk(ename):
            def body(E):
                hi = {}
                for eng, waits, fn, sem, inck in self.steps:
                    if eng != ename:
                        continue
                    for (ws, wv) in waits:
                        if wv > 0 and hi.get(id(ws), -1) < wv:
                            E.wait_ge(ws.h, wv)
                            hi[id(ws)] = wv
                    inst = fn(E)
                    if sem is not None:
                        assert inst is not None, f"inc step returned None on {ename}"
                        inst.then_inc(sem.h, inck)
            return body
        for e in ENGS:
            getattr(block, e)(mk(e))


def build(cfg, stop_after=None):
    L = cfg["L"]; D = cfg["D"]; DIN = cfg["DIN"]; NST = cfg["NST"]
    RANK = cfg["RANK"]; HID = cfg["HID"]
    DH = DIN // 2; NG = DH // P; KD = D // P; LH = L // 2; NTT = L // P
    HF = 2 * HID; GQ = 64
    QGU_R = HF // 8; QP_R = D // 8
    NKH = HID // P
    NMT = KD
    NHT = NKH
    FCH = 256; NFC = LH // FCH
    XDR = RANK + 2 * NST
    KDIN = DIN // P
    NCHB = LH // 512
    NCHL = L // 512

    nc = bass.Bass("TRN2", target_bir_lowering=False, detect_race_conditions=False)

    x_h = nc.dram_tensor("x_tm", [L, D], F32, kind="ExternalInput")
    xT_h = nc.dram_tensor("x_fm", [D, LH], F32, kind="ExternalInput")
    w_inT_h = nc.dram_tensor("w_inT", [D, 2 * DH], F32, kind="ExternalInput")
    convw_h = nc.dram_tensor("convw", [DH, 4], F32, kind="ExternalInput")
    vecs_h = nc.dram_tensor("vecs", [NG, P, 4], F32, kind="ExternalInput")
    xprojT_h = nc.dram_tensor("xprojT", [DH, XDR], F32, kind="ExternalInput")
    dtprojT_h = nc.dram_tensor("dtprojT", [RANK, DH], F32, kind="ExternalInput")
    alog_h = nc.dram_tensor("alog", [DH, NST], F32, kind="ExternalInput")
    oprojT_h = nc.dram_tensor("oprojT", [DIN, D], F32, kind="ExternalInput")
    normw_h = nc.dram_tensor("normw", [P, KD], F32, kind="ExternalInput")
    wgu_q_h = nc.dram_tensor("wgu_q", [QGU_R, D], F32, kind="ExternalInput")
    wp_q_h = nc.dram_tensor("wp_q", [QP_R, HID], F32, kind="ExternalInput")
    identb_h = nc.dram_tensor("ident_bf", [P, P], BF16, kind="ExternalInput")
    onesf_h = nc.dram_tensor("ones_f", [P, 4], F32, kind="ExternalInput")
    onesr_h = nc.dram_tensor("ones_r", [1, P], F32, kind="ExternalInput")
    onesb_h = nc.dram_tensor("ones_b", [P, 4], BF16, kind="ExternalInput")
    consts_h = nc.dram_tensor("consts", [P, 4], F32, kind="ExternalInput")
    sel_h = nc.dram_tensor("sel_bf", [2 * NST, 2 * NST * P], BF16, kind="ExternalInput")

    out_h = nc.dram_tensor("out_fm", [D, LH], F32, kind="ExternalOutput")
    DBG = bool(int(os.environ.get("MK_DEBUG", "0")))
    if DBG:
        dbg_xnT = nc.dram_tensor("dbg_xnT", [P, 1024], BF16, kind="ExternalOutput")
        dbg_a1 = nc.dram_tensor("dbg_a1", [P, D], BF16, kind="ExternalOutput")
        dbg_a2 = nc.dram_tensor("dbg_a2", [P, 16], F32, kind="ExternalOutput")
        dbg_a3 = nc.dram_tensor("dbg_a3", [P, 1024], BF16, kind="ExternalOutput")
        dbg_a4 = nc.dram_tensor("dbg_a4", [P, D], F32, kind="ExternalOutput")
        dbg_xc = nc.dram_tensor("dbg_xc", [P, 1024], BF16, kind="ExternalOutput")
        dbg_sr = nc.dram_tensor("dbg_sr", [P, 1024], BF16, kind="ExternalOutput")
        dbg_xd = nc.dram_tensor("dbg_xd", [XDR, 512], F32, kind="ExternalOutput")
        dbg_dl = nc.dram_tensor("dbg_dl", [P, 512], F32, kind="ExternalOutput")
        dbg_y = nc.dram_tensor("dbg_y", [P, 512], BF16, kind="ExternalOutput")
        dbg_mo = nc.dram_tensor("dbg_mo", [P, 512], F32, kind="ExternalOutput")
        dbg_att = nc.dram_tensor("dbg_att", [P, 512], F32, kind="ExternalOutput")
        dbg_x2n = nc.dram_tensor("dbg_x2n", [P, 512], BF16, kind="ExternalOutput")
        dbg_fo = nc.dram_tensor("dbg_fo", [P, 256], F32, kind="ExternalOutput")
        dbg_wg = nc.dram_tensor("dbg_wg", [P, D], BF16, kind="ExternalOutput")
        dbg_wp2 = nc.dram_tensor("dbg_wp2", [P, HID], BF16, kind="ExternalOutput")

    xc_sp = nc.dram_tensor("xc_sp", [DH, L], BF16)
    x2_sp = nc.dram_tensor("x2_sp", [D, LH], F32)
    x2n_sp = nc.dram_tensor("x2n_sp", [D, LH], BF16)
    sres_sp = nc.dram_tensor("sres_sp", [DH, L], BF16)
    cc_xd_in = nc.dram_tensor("cc_xd_in", [XDR, L], F32)
    cc_xd_out = nc.dram_tensor("cc_xd_out", [XDR, L], F32)
    cc_y_in = nc.dram_tensor("cc_y_in", [2 * DH, LH], BF16)
    cc_y_ag = nc.dram_tensor("cc_y_ag", [2 * 2 * DH, LH], BF16)
    cc_gu_in = nc.dram_tensor("cc_gu_in", [D, QGU_R], BF16)
    cc_gu_out = nc.dram_tensor("cc_gu_out", [8 * D, QGU_R], BF16)
    cc_wp_in = nc.dram_tensor("cc_wp_in", [HID, QP_R], BF16)
    cc_wp_out = nc.dram_tensor("cc_wp_out", [8 * HID, QP_R], BF16)

    groups2 = [[0, 1], [2, 3], [4, 5], [6, 7]]
    groups8 = [list(range(8))]

    SB_LIM = nc.SBUF_PARTITION_SIZE_BYTES

    def sb(name, pdim, nel, dtype, off):
        nbytes = nel * (2 if dtype == BF16 else 4)
        assert off % 32 == 0 and off + nbytes <= SB_LIM, f"SBUF {name}: {off}+{nbytes}"
        return nc.alloc_sbuf_tensor_at(name, [pdim, nel], dtype, offset=off)

    o = 17 * KB + 512
    def nxt(nbytes):
        nonlocal o
        r = o
        o = (o + nbytes + 63) & ~63
        return r
    identb = sb("identb", P, P, BF16, nxt(256))
    onesf = sb("onesf", P, 4, F32, nxt(16))
    onesr = sb("onesr", 1, P, F32, nxt(P * 4))
    onesb = sb("onesb", P, 4, BF16, nxt(32))
    constsb = sb("constsb", P, 4, F32, nxt(16))
    vecs = sb("vecs", P, NG * 4, F32, nxt(NG * 16))
    convw = sb("convw", P, NG * 4, F32, nxt(NG * 16))
    a_sb = sb("a_sb", P, NG * NST, F32, nxt(NG * NST * 4))
    normw_c = sb("normw_c", P, KD, F32, nxt(KD * 4))
    bc4a = sb("bc4a", P, FCH, F32, nxt(FCH * 4))
    bc4b = sb("bc4b", P, FCH, F32, nxt(FCH * 4))
    outD = [sb(f"outD{i}", P, FCH, F32, nxt(FCH * 4)) for i in range(2)]
    assert o <= 25 * KB, o

    o = 25 * KB
    browb = sb("browb", 2 * NST, L, BF16, nxt(L * 2))
    delta_r = sb("delta_r", P, L, BF16, nxt(L * 2))
    state = sb("state", P, NG * NST, F32, nxt(NG * NST * 4))
    dtprojT = sb("dtprojT", P, DH, BF16, nxt(DH * 2))
    xprojT = sb("xprojT", P, NG * XDR, BF16, nxt(NG * XDR * 2))
    assert o <= 44 * KB, o
    R0 = 44 * KB

    o = R0
    x_nT = sb("x_nT", P, KD * L, BF16, nxt(KD * L * 2))
    xin = [sb(f"xin{i}", P, D, F32, nxt(D * 4)) for i in range(2)]
    sq_scr = sb("sq_scr", P, D, F32, nxt(D * 4))
    xn_tm = [sb(f"xntm{i}", P, D, BF16, nxt(D * 2)) for i in range(2)]
    wslab = [sb(f"wslab{i}", P, KD * P, BF16, nxt(KD * P * 2)) for i in range(3)]
    xc_pre = [sb(f"xc_pre{i}", P, L + 8, BF16, nxt((L + 8) * 2)) for i in range(2)]
    conv_acc = sb("conv_acc", P, L, F32, nxt(L * 4))
    xcO = [sb(f"xcO{i}", P, L, BF16, nxt(L * 2)) for i in range(2)]
    sresO = [sb(f"sresO{i}", P, L, BF16, nxt(L * 2)) for i in range(2)]
    colt = sb("colt", P, 16, F32, nxt(64))
    acols = sb("acols", P, 8 * NTT, F32, nxt(8 * NTT * 4))
    xdacc1 = sb("xdacc1", P, L, F32, nxt(L * 4))
    xdacc2 = sb("xdacc2", 32, L, F32, nxt(L * 4))
    xcA2 = [sb(f"xcA2{i}", P, L, BF16, nxt(L * 2)) for i in range(2)]
    brow = sb("brow", 2 * NST, L, F32, nxt(L * 4))
    assert o <= SB_LIM, o

    o = R0
    b_bc = sb("b_bc", P, NST * LH, BF16, nxt(NST * LH * 2))
    c_bc = sb("c_bc", P, NST * LH, BF16, nxt(NST * LH * 2))
    dAb = [sb(f"dAb{i}", P, LH, BF16, nxt(LH * 2)) for i in range(2)]
    dBu = [sb(f"dBu{i}", P, LH, BF16, nxt(LH * 2)) for i in range(2)]
    hbf = [sb(f"hbf{i}", P, LH, BF16, nxt(LH * 2)) for i in range(2)]
    deltg = sb("deltg", P, LH, F32, nxt(LH * 4))
    ug = sb("ug", P, LH, BF16, nxt(LH * 2))
    xcB = [sb(f"xcB{i}", P, LH, BF16, nxt(LH * 2)) for i in range(2)]
    srB = [sb(f"srB{i}", P, LH, BF16, nxt(LH * 2)) for i in range(2)]
    sp_t1 = sb("sp_t1", P, LH, F32, nxt(LH * 4))
    sp_t2 = sb("sp_t2", P, LH, F32, nxt(LH * 4))
    yf32 = sb("yf32", P, LH, F32, nxt(LH * 4))
    ybf = [sb(f"ybf{i}", P, LH, BF16, nxt(LH * 2)) for i in range(2)]
    selb = sb("selb", 2 * NST, 2 * NST * P, BF16, nxt(2 * NST * P * 2))
    b_bc2 = sb("b_bc2", P, NST * LH, BF16, nxt(NST * LH * 2))
    c_bc2 = sb("c_bc2", P, NST * LH, BF16, nxt(NST * LH * 2))
    assert o <= SB_LIM, o

    QCW = 1024
    o = R0 + 164 * KB
    q_wbf = [sb(f"q_wbf{i}", P, QCW, BF16, nxt(QCW * 2)) for i in range(2)]
    q_wh = sb("q_wh", P, QCW, BF16, nxt(QCW * 2))
    q_t2 = sb("q_t2", P, QCW, BF16, nxt(QCW * 2))
    q_fin = sb("q_fin", P, QCW, BF16, nxt(QCW * 2))
    q_red = sb("q_red", P, 64, F32, nxt(256))
    q_scb = sb("q_scb", P, 64, BF16, nxt(128))
    q_scf = sb("q_scf", P, 64, F32, nxt(256))
    q_inv = sb("q_inv", P, 64, F32, nxt(256))
    q_ev = sb("q_ev", P, 1024, BF16, nxt(2048))
    assert o <= R0 + 177 * KB, o

    y_sb = sb("y_sb", P, 2 * NG * LH, BF16, R0)
    x2 = sb("x2", P, KD * LH, F32, R0)
    oslab = [sb(f"oslab{i}", P, KDIN * P, BF16, R0 + 64 * KB + i * 8 * KB) for i in range(2)]
    bigbc1 = sb("bigbc1", P, LH, F32, R0 + 64 * KB)
    bigbc2 = sb("bigbc2", P, LH, F32, R0 + 72 * KB)
    mo = sb("mo", P, KD * LH, F32, R0 + 80 * KB)
    x2n = sb("x2n", P, KD * LH, BF16, R0 + 80 * KB)
    h_sb = sb("h_sb", P, NHT * 512, BF16, R0)
    fo = sb("fo", P, KD * 512, F32, R0 + 64 * KB)
    x2n_blk = sb("x2n_blk", P, KD * 512, BF16, R0 + 96 * KB)
    x2ch = sb("x2ch", P, KD * FCH, F32, R0 + 112 * KB)
    gev = [sb(f"gev{i}", P, 512, F32, R0 + 172 * KB + i * 2 * KB) for i in range(2)]
    guslab = [sb(f"guslab{i}", P, KD * P, BF16, R0 + 128 * KB + i * 4 * KB) for i in range(4)]
    wpslab = [sb(f"wpslab{i}", P, (NKH // 2) * P, BF16, R0 + 144 * KB + i * 8 * KB)
              for i in range(2)]
    sqD = sb("sqD", P, 512, F32, R0 + 160 * KB)
    sqE = sb("sqE", P, 512, F32, R0 + 177 * KB)
    sqDB = [sqD, sqE]
    srow = sb("srow", 1, 2 * 512, F32, R0 + 162 * KB)
    srow2 = sb("srow2", 1, 512 + 2 * FCH, F32, R0 + 166 * KB)
    h2b = [sb(f"h2big{i}", P, 512, BF16, R0 + 170 * KB + i * KB) for i in range(2)]
    assert R0 + 179 * KB <= SB_LIM, R0 + 179 * KB
    nc.sbuf_base = R0 + 179 * KB

    pbk = [nc.alloc_psum_tensor(f"pb{i}", [P, 512], F32) for i in range(5)]
    pT = nc.alloc_psum_tensor("pT", [P, 1024], BF16)
    pR = nc.alloc_psum_tensor("pR", [1, 512], F32)
    pM = nc.alloc_psum_tensor("pM", [32, 512], F32)

    sy = Sem(nc.alloc_semaphore("sy"))
    gp = Sem(nc.alloc_semaphore("gp"))
    te = Sem(nc.alloc_semaphore("te"))
    ve = Sem(nc.alloc_semaphore("ve"))
    sc = Sem(nc.alloc_semaphore("sc"))
    cc = Sem(nc.alloc_semaphore("cc"))
    sd = Sem(nc.alloc_semaphore("sd"))
    sy2 = Sem(nc.alloc_semaphore("sy2"))
    gp2 = Sem(nc.alloc_semaphore("gp2"))
    sd2 = Sem(nc.alloc_semaphore("sd2"))
    vch = Sem(nc.alloc_semaphore("vch"))
    sch = Sem(nc.alloc_semaphore("sch"))
    PL = Plan()
    PL.set_dma_sems("sync", [sy, sy2])
    PL.set_dma_sems("gpsimd", [gp, gp2])
    PL.set_dma_sems("scalar", [sd, sd2])

    def VC(E, inst):
        vch.n += 1
        inst.then_inc(vch.h, 1)
        E.wait_ge(vch.h, vch.n)
        return inst

    def SC(E, inst):
        sch.n += 1
        inst.then_inc(sch.h, 1)
        E.wait_ge(sch.h, sch.n)
        return inst

    def dma(q, sem, out, in_, waits=()):
        return PL.step(q, lambda E, o_=out, i_=in_: E.dma_start(out=o_, in_=i_),
                       waits=waits, sem=sem, inck=16)

    t0 = []
    t0.append(dma("sync", sy, identb[:], identb_h.ap()))
    t0.append(dma("sync", sy, onesf[:], onesf_h.ap()))
    t0.append(dma("sync", sy, onesr[:], onesr_h.ap()))
    t0.append(dma("sync", sy, onesb[:], onesb_h.ap()))
    t0.append(dma("sync", sy, constsb[:], consts_h.ap()))
    nc.const_aps.register(F32, EPS_LN, constsb[:, 0:1])
    nc.const_aps.register(F32, 1.0, constsb[:, 1:2])
    nc.const_aps.register(F32, EPS_RMS, constsb[:, 2:3])
    nc.const_aps.register(F32, 0.0, constsb[:, 3:4])
    t0.append(dma("sync", sy, vecs[:].rearrange("p (g v) -> p g v", v=4),
                  vecs_h.ap().rearrange("g p v -> p g v")))
    t0.append(dma("sync", sy, convw[:].rearrange("p (g v) -> p g v", v=4),
                  convw_h.ap().rearrange("(g p) v -> p g v", p=P)))
    t0.append(dma("sync", sy, a_sb[:].rearrange("p (g n) -> p g n", n=NST),
                  alog_h.ap().rearrange("(g p) n -> p g n", p=P)))
    t0.append(dma("sync", sy, normw_c[:], normw_h.ap()))
    t0.append(dma("gpsimd", gp, dtprojT[:], dtprojT_h.ap()))
    t0.append(dma("gpsimd", gp, xprojT[:].rearrange("p (g m) -> p g m", m=XDR),
                  xprojT_h.ap().rearrange("(g p) m -> p g m", p=P)))
    tk = PL.step("scalar", lambda E: E.activation(a_sb[:], a_sb[:], AF.Exp, bias=0.0, scale=1.0),
                 waits=t0, sem=sc)
    tk_acst = PL.step("vector", lambda E: E.tensor_scalar_mul(a_sb[:], a_sb[:], -1.0),
                      waits=[tk], sem=ve)

    qchunks = []
    for t in range(QGU_R // P):
        for ccol in range(D // QCW):
            qchunks.append((wgu_q_h, t * P, ccol * QCW, cc_gu_in, t * P))
    for t in range(QP_R // P):
        for ccol in range(HID // QCW):
            qchunks.append((wp_q_h, t * P, ccol * QCW, cc_wp_in, t * P))
    NGR = QCW // GQ
    NKB = QCW // P
    NTG = NKB // 8

    _qstate = dict(prev_vec=None, prev_vec2=None, prev_tr=None, prev_wr=None, prev_wr2=None)
    PT_G = [None]

    def emit_qchunk(ci):
        src, r0, c0, dst, dr0 = qchunks[ci]
        qb = q_wbf[ci % 2]
        prev_vec = _qstate["prev_vec"]; prev_tr = _qstate["prev_tr"]; prev_wr = _qstate["prev_wr"]
        tk_ld = PL.step("gpsimd",
                        lambda E, s=src, r=r0, c=c0, b=qb: E.dma_start(out=b[:], in_=s[r:r + P, c:c + QCW]),
                        waits=[_qstate["prev_vec2"]], sem=gp, inck=16)

        def v1(E, b=qb):
            VC(E, E.tensor_reduce(q_red[:, 0:NGR], b[:].rearrange("p (g k) -> p g k", k=GQ),
                                  AX.X, OP.add, apply_absolute_value=True))
            VC(E, E.tensor_scalar(q_scb[:, 0:NGR], q_red[:, 0:NGR], 1.0 / GQ, 1e-8, OP.mult, OP.max))
            return E.tensor_copy(q_scf[:, 0:NGR], q_scb[:, 0:NGR])
        tk_sc1 = PL.step("vector", v1, waits=[tk_ld, prev_vec], sem=ve)

        def s1(E):
            SC(E, E.activation(q_inv[:, 0:NGR], q_scf[:, 0:NGR], AF.Ln, bias=0.0, scale=1.0))
            return E.activation(q_inv[:, 0:NGR], q_inv[:, 0:NGR], AF.Exp, bias=0.0, scale=-1.0)
        tk_inv = PL.step("scalar", s1, waits=[tk_sc1], sem=sc)

        def v2(E, b=qb):
            for g in range(NGR):
                i = E.tensor_scalar_mul(q_wh[:, g * GQ:(g + 1) * GQ], b[:, g * GQ:(g + 1) * GQ],
                                        q_inv[:, g:g + 1])
            VC(E, i)
            VC(E, E.tensor_scalar(sqD[:], q_wh[:, 0:512], RND, -RND, OP.add, OP.add))
            VC(E, E.tensor_scalar(sqE[:], q_wh[:, 512:1024], RND, -RND, OP.add, OP.add))
            VC(E, E.tensor_scalar(q_wh[:, 0:512], sqD[:], 1.0, -1.0, OP.min, OP.max))
            VC(E, E.tensor_scalar(q_wh[:, 512:1024], sqE[:], 1.0, -1.0, OP.min, OP.max))
            for g in range(NGR):
                i = E.tensor_scalar_mul(q_t2[:, g * GQ:(g + 1) * GQ], q_wh[:, g * GQ:(g + 1) * GQ],
                                        q_scf[:, g:g + 1])
            VC(E, i)
            VC(E, E.tensor_sub(q_fin[:], q_t2[:], b[:]))
            return E.tensor_add(q_fin[:], b[:], q_fin[:])
        prev_vec = PL.step("vector", v2, waits=[tk_inv, prev_tr], sem=ve)
        _qstate["prev_vec2"] = _qstate["prev_vec"]
        _qstate["prev_vec"] = prev_vec

        for tg in range(NTG):
            def tr(E, base=tg * 8):
                i = None
                for k8 in range(8):
                    i = E.transpose(pT[:, k8 * P:(k8 + 1) * P],
                                    q_fin[:, (base + k8) * P:(base + k8 + 1) * P], identb[:])
                return i
            prev_tr = PL.step("tensor", tr, waits=[prev_vec, prev_wr, PT_G[0]], sem=te)
            _qstate["prev_tr"] = prev_tr
            tk_ev = PL.step("scalar", lambda E: E.copy(q_ev[:], pT[:]), waits=[prev_tr], sem=sc)
            PT_G[0] = tk_ev

            def wr(E, d=dst, dd=dr0, c=c0, base=tg * 8):
                return E.dma_start(
                    out=d.ap().rearrange("(kb p) r -> p kb r", p=P)[:, (c // P) + base:(c // P) + base + 8,
                                                                   dd:dd + P],
                    in_=q_ev[:].rearrange("p (kb r) -> p kb r", r=P))
            prev_wr = PL.step("sync", wr, waits=[tk_ev], sem=sy, inck=16)
            _qstate["prev_wr2"] = _qstate["prev_wr"]
            _qstate["prev_wr"] = prev_wr
        if ci == len(qchunks) - 1:
            emit_q_ags()

    AGTOK = {}

    def emit_q_ags():
        AGTOK["ag1"] = PL.step("gpsimd",
                     lambda E: E.collective_compute("AllGather", OP.bypass, replica_groups=groups8,
                                                    ins=[cc_gu_in[:]], outs=[cc_gu_out[:]]),
                     waits=[_qstate["prev_wr"], _qstate["prev_wr2"]], sem=cc, inck=1)
        AGTOK["ag2"] = PL.step("gpsimd",
                     lambda E: E.collective_compute("AllGather", OP.bypass, replica_groups=groups8,
                                                    ins=[cc_wp_in[:]], outs=[cc_wp_out[:]]),
                     waits=(), sem=cc, inck=1)

    NQ_IN_A1 = min(len(qchunks), NTT)
    p1_cons = [None] * NTT
    for tt in range(NTT):
        bi = tt % 2
        war = [p1_cons[tt - 2]] if tt >= 2 else []
        tk_x = PL.step("sync", lambda E, t=tt, b=bi: E.dma_start(out=xin[b][:], in_=x_h[t * P:(t + 1) * P, :]),
                       waits=war, sem=sy, inck=16)
        tk_sq = PL.step("scalar",
                        lambda E, b=bi, t=tt: E.activation(sq_scr[:], xin[b][:], AF.Square, bias=0.0,
                                                           scale=1.0, accum_out=acols[:, NTT + t:NTT + t + 1]),
                        waits=[tk_x], sem=sc)
        tk_rs = PL.step("vector",
                        lambda E, b=bi, t=tt: E.reduce_sum(acols[:, t:t + 1], xin[b][:], AX.X),
                        waits=[tk_x], sem=ve)
        p1_cons[tt] = tk_rs if tt % 2 else tk_sq
        if tt < NQ_IN_A1:
            emit_qchunk(tt)
    p1_done = [tk_sq, tk_rs]

    CS, CQ, CNM, CV, CT, CRS, CR2, CSK = (acols[:, i * NTT:(i + 1) * NTT] for i in range(8))

    def p2v1(E):
        VC(E, E.tensor_scalar_mul(CNM, CS, -1.0 / D))
        VC(E, E.tensor_mul(CT, CNM, CNM))
        return E.scalar_tensor_tensor(CV, CQ, 1.0 / D, CT, OP.mult, OP.subtract)
    tk_p2a = PL.step("vector", p2v1, waits=p1_done, sem=ve)

    def p2s1(E):
        SC(E, E.activation(CT, CV, AF.Ln, bias=EPS_LN, scale=1.0))
        return E.activation(CRS, CT, AF.Exp, bias=0.0, scale=-0.5)
    tk_p2b = PL.step("scalar", p2s1, waits=[tk_p2a], sem=sc)

    def p2v2(E):
        VC(E, E.tensor_mul(CT, CV, CRS))
        return E.tensor_mul(CT, CT, CRS)
    tk_p2c = PL.step("vector", p2v2, waits=[tk_p2b], sem=ve)

    def p2s2(E):
        SC(E, E.activation(CT, CT, AF.Ln, bias=EPS_LN, scale=1.0))
        return E.activation(CR2, CT, AF.Exp, bias=0.0, scale=-0.5)
    tk_p2d = PL.step("scalar", p2s2, waits=[tk_p2c], sem=sc)
    tk_p2e = PL.step("vector", lambda E: E.tensor_mul(CSK, CRS, CR2), waits=[tk_p2d], sem=ve)

    evac_of_tt = []
    ap_of_tt = []
    for tt in range(NTT):
        bi = tt % 2
        war = [ap_of_tt[tt - 2]] if tt >= 2 else [tk_p2a]
        tk_x = PL.step("sync", lambda E, t=tt, b=bi: E.dma_start(out=xin[b][:], in_=x_h[t * P:(t + 1) * P, :]),
                       waits=war, sem=sy, inck=16)

        def vapply(E, b=bi, t=tt):
            return E.tensor_scalar(xn_tm[b][:], xin[b][:], acols[:, 2 * NTT + t:2 * NTT + t + 1],
                                   acols[:, 7 * NTT + t:7 * NTT + t + 1], OP.add, OP.mult)
        tk_ap = PL.step("vector", vapply,
                        waits=[tk_x, tk_p2e, (evac_of_tt[tt - 2] if tt >= 2 else None)], sem=ve)
        ap_of_tt.append(tk_ap)

        last_e = None
        for grp in range(KD // 8):
            def trp(E, b=bi, g=grp):
                i = None
                for k8 in range(8):
                    i = E.transpose(pT[:, k8 * P:(k8 + 1) * P],
                                    xn_tm[b][:, (g * 8 + k8) * P:(g * 8 + k8 + 1) * P], identb[:])
                return i
            tk_tp = PL.step("tensor", trp, waits=[tk_ap, PT_G[0]], sem=te)

            def evac(E, t=tt, g=grp):
                return E.copy(
                    x_nT[:].rearrange("p (k t2) -> p k t2", t2=L)[:, g * 8:(g + 1) * 8, t * P:(t + 1) * P],
                    pT[:].rearrange("p (k t2) -> p k t2", t2=P))
            last_e = PL.step("scalar", evac, waits=[tk_tp], sem=sc)
            PT_G[0] = last_e
        evac_of_tt.append(last_e)


    nw_last = None
    for k in range(KD):
        nw_last = PL.step("vector",
                          lambda E, kk=k: E.tensor_scalar_mul(x_nT[:, kk * L:(kk + 1) * L],
                                                              x_nT[:, kk * L:(kk + 1) * L],
                                                              normw_c[:, kk:kk + 1]),
                          waits=[evac_of_tt[-1]], sem=ve)
    xnT_ready = nw_last
    if DBG:
        dma("sync", sy, dbg_xnT.ap(), x_nT[:, 0:1024], waits=[xnT_ready])

    def padf(E):
        E.memset(xc_pre[0][:, 0:8], 0.0)
        return E.memset(xc_pre[1][:, 0:8], 0.0)
    tk_pad = PL.step("vector", padf, waits=(), sem=ve)
    NO = 2 * NG
    slab_last_mm = [None, None, None]
    psum_cons = {}
    xproj_done = [None] * NG
    spill_xc = [None] * NG
    spill_sr = [None] * NG
    silu_done = [None] * NG
    sres_war = [None] * NG
    NQ_A2_BASE = NQ_IN_A1
    for ot in range(NO):
        bi = ot % 3
        if NQ_A2_BASE + ot < len(qchunks):
            emit_qchunk(NQ_A2_BASE + ot)

        def ldw(E, o_=ot, b=bi):
            return E.dma_start(out=wslab[b][:].rearrange("p (k m) -> p k m", m=P),
                               in_=w_inT_h.ap().rearrange("(k p) (ot m) -> ot p k m", p=P, m=P)[o_])
        tk_w = PL.step("gpsimd", ldw, waits=[slab_last_mm[bi]], sem=gp, inck=16)

        for nch in range(NCHL):
            bank = (ot * NCHL + nch) % 4

            def mms(E, b=bi, n=nch, bk=bank):
                i = None
                for k in range(KD):
                    i = E.matmul(pbk[bk][:], wslab[b][:, k * P:(k + 1) * P],
                                 x_nT[:, k * L + n * 512:k * L + (n + 1) * 512],
                                 start=(k == 0), stop=(k == KD - 1))
                return i
            tk_m = PL.step("tensor", mms, waits=[tk_w, xnT_ready, psum_cons.get(bank)], sem=te)
            if nch == NCHL - 1:
                slab_last_mm[bi] = tk_m

            if ot < NG:
                def ev(E, n=nch, bk=bank, b=ot % 2):
                    return E.copy(xc_pre[b][:, 8 + n * 512:8 + (n + 1) * 512], pbk[bk][:])
                tk_e = PL.step("scalar", ev,
                               waits=[tk_m, (silu_done[ot - 2] if ot >= 2 and nch == 0 else None)],
                               sem=sc)
            else:
                g = ot - NG

                def ev(E, n=nch, bk=bank, gg=g):
                    return E.activation(sresO[gg % 2][:, n * 512:(n + 1) * 512], pbk[bk][:],
                                        AF.Silu, bias=0.0, scale=1.0)
                tk_e = PL.step("scalar", ev,
                               waits=[tk_m, (spill_sr[g - 2] if g >= 2 and nch == 0 else None)],
                               sem=sc)
            psum_cons[bank] = tk_e

        if ot < NG:
            g = ot
            if g == 0:
                def mz(E):
                    E.memset(xdacc1[:], 0.0)
                    return E.memset(xdacc2[:], 0.0)
                tk_mz = PL.step("vector", mz, waits=(), sem=ve)
                prev_acc = tk_mz

            def cv(E, gg=g, b=g % 2):
                VC(E, E.tensor_scalar_mul(conv_acc[:], xc_pre[b][:, 5:5 + L], convw[:, gg * 4:gg * 4 + 1]))
                i = None
                for k in range(1, 4):
                    i = E.scalar_tensor_tensor(conv_acc[:], xc_pre[b][:, 5 + k:5 + k + L],
                                               convw[:, gg * 4 + k:gg * 4 + k + 1], conv_acc[:],
                                               OP.mult, OP.add)
                    if k < 3:
                        VC(E, i)
                return i
            tk_c = PL.step("vector", cv,
                           waits=[psum_cons[(ot * NCHL + NCHL - 1) % 4], tk_pad,
                                  (silu_done[g - 1] if g >= 1 else None)], sem=ve)

            def sl(E, gg=g):
                return E.activation(xcO[gg % 2][:], conv_acc[:], AF.Silu,
                                    bias=vecs[:, gg * 4:gg * 4 + 1], scale=1.0)
            tk_s = PL.step("scalar", sl,
                           waits=[tk_c, (spill_xc[g - 2] if g >= 2 else None),
                                  (xproj_done[g - 2] if g >= 2 else None)], sem=sc)
            silu_done[g] = tk_s
            spill_xc[g] = dma("sync", sy, xc_sp[g * P:(g + 1) * P, :], xcO[g % 2][:], waits=[tk_s])
            for nch in range(NCHL):
                def mmx(E, gg=g, b=g % 2, n=nch):
                    E.matmul(pbk[4][:], xprojT[:, gg * XDR:gg * XDR + P],
                             xcO[b][:, n * 512:(n + 1) * 512], start=True, stop=True)
                    return E.matmul(pM[:], xprojT[:, gg * XDR + P:gg * XDR + XDR],
                                    xcO[b][:, n * 512:(n + 1) * 512], start=True, stop=True)
                tk_mm2 = PL.step("tensor", mmx, waits=[tk_s, prev_acc], sem=te)

                def acc(E, n=nch):
                    E.tensor_add(xdacc1[:, n * 512:(n + 1) * 512],
                                 xdacc1[:, n * 512:(n + 1) * 512], pbk[4][:])
                    return E.tensor_add(xdacc2[:, n * 512:(n + 1) * 512],
                                        xdacc2[:, n * 512:(n + 1) * 512], pM[:])
                prev_acc = PL.step("vector", acc, waits=[tk_mm2], sem=ve)
                xproj_done[g] = tk_mm2
        else:
            g = ot - NG
            spill_sr[g] = dma("sync", sy, sres_sp[g * P:(g + 1) * P, :], sresO[g % 2][:],
                              waits=[psum_cons[(ot * NCHL + NCHL - 1) % 4]])

    if DBG:
        dma("sync", sy, dbg_xc.ap(), xc_sp[0:P, 0:1024], waits=[spill_xc[0]])
        dma("sync", sy, dbg_sr.ap(), sres_sp[0:P, 0:1024], waits=[spill_sr[0]])

    if DBG:
        dma("sync", sy, dbg_wg.ap(), cc_gu_out[0:P, :], waits=[AGTOK["ag1"]])

    tk_xw1 = dma("sync", sy, cc_xd_in[0:RANK, :], xdacc1[:], waits=[prev_acc])
    tk_xw2 = dma("sync", sy, cc_xd_in[RANK:XDR, :], xdacc2[:], waits=[prev_acc])
    tk_ar = PL.step("gpsimd",
                    lambda E: E.collective_compute("AllReduce", OP.add, replica_groups=groups2,
                                                   ins=[cc_xd_in[:]], outs=[cc_xd_out[:]]),
                    waits=[tk_xw1, tk_xw2], sem=cc, inck=1)
    tk_dr = dma("gpsimd", gp, delta_r[:], cc_xd_out[0:RANK, :], waits=[tk_ar])
    if DBG:
        dma("sync", sy, dbg_xd.ap(), cc_xd_out[:, 0:512], waits=[tk_ar])
    tk_br = dma("sync", sy, brow[:], cc_xd_out[RANK:XDR, :], waits=[tk_ar])
    tk_bb = PL.step("vector", lambda E: E.tensor_copy(browb[:], brow[:]), waits=[tk_br], sem=ve)

    if stop_after == "A":
        with nc.Block() as blk:
            PL.play(blk)
        return nc

    tk_selb = dma("sync", sy, selb[:], sel_h.ap(),
                  waits=[prev_acc, silu_done[NG - 1], spill_sr[NG - 1]])
    ywr = []
    fin_read = None
    tk_fb = None
    prev_tkd = None
    BCB = [(b_bc, c_bc), (b_bc2, c_bc2)]
    bc_ready_h = []
    bc_toks = []
    for hf in range(2):
        for i in range(NST):
            for nch in range(NCHB):
                def bc1(E, ii=i, n=nch, h=hf):
                    E.matmul(pbk[4][:], selb[:, ii * P:(ii + 1) * P],
                             browb[:, h * LH + n * 512:h * LH + (n + 1) * 512], start=True, stop=True)
                    return E.matmul(pbk[0][:], selb[:, (NST + ii) * P:(NST + ii + 1) * P],
                                    browb[:, h * LH + n * 512:h * LH + (n + 1) * 512],
                                    start=True, stop=True)
                tk_bm = PL.step("tensor", bc1,
                                waits=[tk_bb, tk_selb, (bc_toks[-1] if bc_toks else None),
                                       (prev_acc if not bc_toks else None)],
                                sem=te)

                def bc2(E, ii=i, n=nch, h=hf):
                    E.copy(BCB[h][0][:, ii * LH + n * 512:ii * LH + (n + 1) * 512], pbk[4][:])
                    return E.copy(BCB[h][1][:, ii * LH + n * 512:ii * LH + (n + 1) * 512],
                                  pbk[0][:])
                bc_toks.append(PL.step("scalar", bc2,
                                       waits=[tk_bm] + ([prev_acc, tk_bb] if not bc_toks else []),
                                       sem=sc))
        bc_ready_h.append(bc_toks[-1])

    for hf in range(2):
        bc_ready = bc_ready_h[hf]
        b_bch, c_bch = BCB[hf]
        for g in range(NG):
            bg = g % 2

            def dtm(E, gg=g, h=hf):
                E.matmul(pbk[2][:], dtprojT[:, gg * P:(gg + 1) * P],
                         delta_r[:, h * LH:h * LH + 512], start=True, stop=True)
                return E.matmul(pbk[3][:], dtprojT[:, gg * P:(gg + 1) * P],
                                delta_r[:, h * LH + 512:h * LH + 1024], start=True, stop=True)
            tk_dt = PL.step("tensor", dtm, waits=[tk_dr, (None if (hf, g) == (0, 0) else sp_done2)],
                            sem=te)

            def spl(E, gg=g):
                i = None
                for n in range(NCHB):
                    src = pbk[2 + n]
                    E.activation(sp_t1[:, n * 512:(n + 1) * 512], src[:], AF.Relu,
                                 bias=vecs[:, gg * 4 + 1:gg * 4 + 2], scale=1.0)
                    SC(E, E.activation(sp_t2[:, n * 512:(n + 1) * 512], src[:], AF.Abs,
                                       bias=vecs[:, gg * 4 + 1:gg * 4 + 2], scale=1.0))
                    SC(E, E.activation(sp_t2[:, n * 512:(n + 1) * 512], sp_t2[:, n * 512:(n + 1) * 512],
                                       AF.Exp, bias=0.0, scale=-1.0))
                    i = E.activation(sp_t2[:, n * 512:(n + 1) * 512], sp_t2[:, n * 512:(n + 1) * 512],
                                     AF.Ln, bias=1.0, scale=1.0)
                return i
            sp_done = PL.step("scalar", spl, waits=[tk_dt, prev_tkd], sem=sc)
            sp_done2 = sp_done

            tk_xc = PL.step("gpsimd", lambda E, gg=g, b=bg, h=hf: E.dma_start(
                out=xcB[b][:], in_=xc_sp[gg * P:(gg + 1) * P, h * LH:(h + 1) * LH]),
                waits=[spill_xc[g], (ywr[-2] if len(ywr) >= 2 else None)], sem=gp, inck=16)
            tk_sr = PL.step("gpsimd", lambda E, gg=g, b=bg, h=hf: E.dma_start(
                out=srB[b][:], in_=sres_sp[gg * P:(gg + 1) * P, h * LH:(h + 1) * LH]),
                waits=[spill_sr[g], (ywr[-2] if len(ywr) >= 2 else None)], sem=gp, inck=16)

            tk_d = PL.step("vector", lambda E: E.tensor_add(deltg[:], sp_t1[:], sp_t2[:]),
                           waits=[sp_done, bc_ready], sem=ve)
            prev_tkd = tk_d
            tk_u = PL.step("vector", lambda E, b=bg: E.tensor_mul(ug[:], deltg[:], xcB[b][:]),
                           waits=[tk_d, tk_xc], sem=ve)
            if DBG and hf == 0 and g == 0:
                dma("sync", sy, dbg_dl.ap(), deltg[:, 0:512], waits=[tk_u])

            scan_t = [None, None]
            hC_t = [None, None]
            ya_t = [None, None]
            for i in range(NST):
                b2 = i % 2

                def dac(E, gg=g, ii=i, b=b2):
                    return E.activation(dAb[b][:], deltg[:], AF.Exp, bias=0.0,
                                        scale=a_sb[:, gg * NST + ii:gg * NST + ii + 1])
                tk_da = PL.step("scalar", dac, waits=[tk_d, scan_t[b2]], sem=sc)

                def dbu(E, ii=i, b=b2):
                    return E.tensor_mul(dBu[b][:], ug[:], b_bch[:, ii * LH:(ii + 1) * LH])
                tk_db = PL.step("vector", dbu, waits=[tk_u, ya_t[b2]], sem=ve)

                def scn(E, gg=g, ii=i, b=b2, h=hf):
                    init = 0.0 if h == 0 else state[:, gg * NST + ii:gg * NST + ii + 1]
                    return E.tensor_tensor_scan(hbf[b][:], dAb[b][:], dBu[b][:], init,
                                                OP.mult, OP.add)
                tk_sn = PL.step("vector", scn, waits=[tk_da, tk_db, hC_t[b2]], sem=ve)
                scan_t[b2] = tk_sn

                tk_ss = None
                if hf == 0:
                    tk_ss = PL.step("scalar",
                                    lambda E, gg=g, ii=i, b=b2: E.copy(
                                        state[:, gg * NST + ii:gg * NST + ii + 1],
                                        hbf[b][:, LH - 1:LH]),
                                    waits=[tk_sn], sem=sc)

                def hc(E, ii=i, b=b2):
                    return E.tensor_mul(dBu[b][:], hbf[b][:], c_bch[:, ii * LH:(ii + 1) * LH])
                tk_hc = PL.step("vector", hc, waits=[tk_sn], sem=ve)
                hC_t[b2] = tk_hc

                def ya(E, ii=i, b=b2):
                    E.matmul(pbk[0][:], identb[:], dBu[b][:, 0:512], start=(ii == 0),
                             stop=(ii == NST - 1))
                    return E.matmul(pbk[1][:], identb[:], dBu[b][:, 512:1024], start=(ii == 0),
                                    stop=(ii == NST - 1))
                tk_ya = PL.step("tensor", ya,
                                waits=[tk_hc, tk_ss, (fin_read if i == 0 else None)], sem=te)
                ya_t[b2] = tk_ya

            def fin(E, gg=g, b=bg):
                E.scalar_tensor_tensor(yf32[:, 0:512], xcB[b][:, 0:512],
                                       vecs[:, gg * 4 + 2:gg * 4 + 3], pbk[0][:], OP.mult, OP.add)
                return E.scalar_tensor_tensor(yf32[:, 512:1024], xcB[b][:, 512:1024],
                                              vecs[:, gg * 4 + 2:gg * 4 + 3], pbk[1][:],
                                              OP.mult, OP.add)
            fin_read = PL.step("vector", fin, waits=[ya_t[(NST - 1) % 2], tk_sr, tk_fb], sem=ve)

            tk_fb = PL.step("vector", lambda E, b=bg: E.tensor_mul(ybf[b][:], yf32[:], srB[b][:]),
                            waits=[fin_read, (ywr[-2] if len(ywr) >= 2 else None)], sem=ve)
            ywr.append(dma("sync", sy, cc_y_in[hf * DH + g * P:hf * DH + (g + 1) * P, :],
                           ybf[bg][:], waits=[tk_fb]))

    if DBG:
        dma("sync", sy, dbg_y.ap(), cc_y_in[0:P, 0:512], waits=[ywr[0]])
    tk_ag_h0 = PL.step("gpsimd",
                       lambda E: E.collective_compute("AllGather", OP.bypass, replica_groups=groups2,
                                                      ins=[cc_y_in[0:DH, :]], outs=[cc_y_ag[0:2 * DH, :]]),
                       waits=[ywr[NG - 1], ywr[NG - 2]], sem=cc, inck=1)
    tk_a2a = PL.step("gpsimd",
                     lambda E: E.collective_compute("AllGather", OP.bypass, replica_groups=groups2,
                                                    ins=[cc_y_in[DH:2 * DH, :]],
                                                    outs=[cc_y_ag[2 * DH:4 * DH, :]]),
                     waits=[ywr[-1], ywr[-2]], sem=cc, inck=1)

    if stop_after == "B":
        with nc.Block() as blk:
            PL.play(blk)
        return nc

    def mk_yload(r):
        def yload(E):
            pid = E.partition_id()
            off = (pid % 2) * (2 * DH * LH) + r * (DH * LH)
            ap = bass.AP(cc_y_ag, off, [[LH, P], [P * LH, NG], [1, LH]])
            return E.dma_start(
                out=y_sb[:].rearrange("p (k t) -> p k t", t=LH)[:, r * NG:(r + 1) * NG, :],
                in_=ap)
        return yload
    tk_yl0 = PL.step("gpsimd", mk_yload(0), waits=[tk_ag_h0, tk_a2a, tk_fb, fin_read], sem=gp, inck=16)
    tk_yl = PL.step("gpsimd", mk_yload(1), waits=(), sem=gp, inck=16)
    oslab_last = [None, None]
    opsum = {}
    mo_evs = []
    for mt in range(NMT):
        bi = mt % 2
        tk_o = PL.step("gpsimd",
                       lambda E, m=mt, b=bi: E.dma_start(
                           out=oslab[b][:].rearrange("p (k m) -> p k m", m=P),
                           in_=oprojT_h.ap().rearrange("(k p) (mt m) -> mt p k m", p=P, m=P)[m]),
                       waits=[oslab_last[bi], (tk_a2a if mt < 2 else None)], sem=gp, inck=16)
        for nch in range(NCHB):
            bank = (mt * NCHB + nch) % 4

            def mmo(E, b=bi, n=nch, bk=bank):
                i = None
                for k in range(KDIN):
                    i = E.matmul(pbk[bk][:], oslab[b][:, k * P:(k + 1) * P],
                                 y_sb[:, k * LH + n * 512:k * LH + (n + 1) * 512],
                                 start=(k == 0), stop=(k == KDIN - 1))
                return i
            tk_m = PL.step("tensor", mmo, waits=[tk_o, tk_yl, opsum.get(bank)], sem=te)
            if nch == NCHB - 1:
                oslab_last[bi] = tk_m
            tk_e = PL.step("scalar",
                           lambda E, m=mt, n=nch, bk=bank: E.copy(
                               mo[:, m * LH + n * 512:m * LH + (n + 1) * 512], pbk[bk][:]),
                           waits=[tk_m], sem=sc)
            opsum[bank] = tk_e
            mo_evs.append(tk_e)
    mo_ready = mo_evs[-1]

    if stop_after == "C":
        with nc.Block() as blk:
            PL.play(blk)
        return nc

    if DBG:
        dma("sync", sy, dbg_mo.ap(), mo[:, 0:512], waits=[mo_ready])
    tk_xT = dma("sync", sy, x2[:].rearrange("p (k t) -> p k t", t=LH),
                xT_h.ap().rearrange("(k p) t -> p k t", p=P), waits=[mo_ready, tk_yl])

    def ln_stats(src_fn, width, wtoks, rowbuf, mslot_off, sslot_off, inv_off=None):
        assert width <= 512
        mm_toks = []
        for k in range(KD):
            def sqf(E, kk=k, b=k % 2):
                return E.activation(sqDB[b][:, 0:width], src_fn(kk), AF.Square, bias=0.0,
                                    scale=1.0)
            tk_sq = PL.step("scalar", sqf,
                            waits=list(wtoks) + ([mm_toks[-2]] if len(mm_toks) >= 2 else []),
                            sem=sc)

            def mms(E, kk=k, b=k % 2):
                E.matmul(pR[:, 0:width], onesf[:, 0:1], src_fn(kk),
                         start=(kk == 0), stop=(kk == KD - 1))
                return E.matmul(pM[0:1, 0:width], onesf[:, 0:1], sqDB[b][:, 0:width],
                                start=(kk == 0), stop=(kk == KD - 1))
            mm_toks.append(PL.step("tensor", mms, waits=list(wtoks) + [tk_sq], sem=te))

        def evr(E):
            E.copy(rowbuf[0:1, mslot_off:mslot_off + width], pR[:, 0:width])
            return E.copy(rowbuf[0:1, sslot_off:sslot_off + width], pM[0:1, 0:width])
        tk_ev = PL.step("scalar", evr, waits=[mm_toks[-1]], sem=sc)

        def rowm(E):
            VC(E, E.tensor_scalar_mul(rowbuf[0:1, mslot_off:mslot_off + width],
                                      rowbuf[0:1, mslot_off:mslot_off + width], -1.0 / D))
            VC(E, E.tensor_mul(sqD[0:1, 0:width], rowbuf[0:1, mslot_off:mslot_off + width],
                               rowbuf[0:1, mslot_off:mslot_off + width]))
            return E.scalar_tensor_tensor(rowbuf[0:1, sslot_off:sslot_off + width],
                                          rowbuf[0:1, sslot_off:sslot_off + width], 1.0 / D,
                                          sqD[0:1, 0:width], OP.mult, OP.subtract)
        tk_rm = PL.step("vector", rowm, waits=[tk_ev], sem=ve)
        if inv_off is not None:
            def pre(E):
                VC(E, E.tensor_mul(sqD[0:1, 0:width], rowbuf[0:1, inv_off:inv_off + width],
                                   rowbuf[0:1, inv_off:inv_off + width]))
                return E.tensor_mul(rowbuf[0:1, sslot_off:sslot_off + width],
                                    rowbuf[0:1, sslot_off:sslot_off + width], sqD[0:1, 0:width])
            tk_rm = PL.step("vector", pre, waits=[tk_rm], sem=ve)

        def rs(E):
            SC(E, E.activation(rowbuf[0:1, sslot_off:sslot_off + width],
                               rowbuf[0:1, sslot_off:sslot_off + width], AF.Ln, bias=EPS_LN,
                               scale=1.0))
            return E.activation(rowbuf[0:1, sslot_off:sslot_off + width],
                                rowbuf[0:1, sslot_off:sslot_off + width], AF.Exp, bias=0.0,
                                scale=-0.5)
        tk_rs = PL.step("scalar", rs, waits=[tk_rm], sem=sc)
        vw = [tk_rs]
        if inv_off is not None:
            tk_iv = PL.step("vector",
                            lambda E: E.tensor_mul(rowbuf[0:1, sslot_off:sslot_off + width],
                                                   rowbuf[0:1, sslot_off:sslot_off + width],
                                                   rowbuf[0:1, inv_off:inv_off + width]),
                            waits=[tk_rs], sem=ve)
            vw = [tk_iv]
        return PL.step("vector",
                       lambda E: E.tensor_mul(rowbuf[0:1, mslot_off:mslot_off + width],
                                              rowbuf[0:1, mslot_off:mslot_off + width],
                                              rowbuf[0:1, sslot_off:sslot_off + width]),
                       waits=vw, sem=ve)

    def bcast_row_at(dst, dstoff, rowbuf, off, width, wtoks):
        def bm(E):
            return E.matmul(pbk[4][:, 0:width], onesr[0:1, :], rowbuf[0:1, off:off + width],
                            start=True, stop=True)
        t1 = PL.step("tensor", bm, waits=list(wtoks), sem=te)
        return PL.step("scalar",
                       lambda E: E.copy(dst[:, dstoff:dstoff + width], pbk[4][:, 0:width]),
                       waits=[t1], sem=sc)

    def ln_to_bc(src_fn, wtoks):
        done = []
        prev = list(wtoks)
        for h5 in range(LH // 512):
            tok = ln_stats(lambda k, h=h5: src_fn(k)[:, h * 512:(h + 1) * 512], 512,
                           prev, srow, 0, 512)
            b1 = bcast_row_at(bigbc1, h5 * 512, srow, 512, 512, [tok])
            b2 = bcast_row_at(bigbc2, h5 * 512, srow, 0, 512, [tok, b1])
            done.extend([b1, b2])
            prev = list(wtoks) + [b1, b2]
        return done

    if stop_after == "D0":
        with nc.Block() as blk:
            PL.play(blk)
        return nc

    st_x = ln_to_bc(lambda k: x2[:, k * LH:(k + 1) * LH], [tk_xT])
    if stop_after == "D1":
        with nc.Block() as blk:
            PL.play(blk)
        return nc
    att_toks = []
    for k in range(KD):
        def attf(E, kk=k):
            i = None
            for n in range(LH // 512):
                sl = slice(kk * LH + n * 512, kk * LH + (n + 1) * 512)
                bl = slice(n * 512, (n + 1) * 512)
                VC(E, E.tensor_mul(sqD[:], x2[:, sl], bigbc1[:, bl]))
                VC(E, E.tensor_add(sqD[:], sqD[:], bigbc2[:, bl]))
                i = E.tensor_add(mo[:, sl], mo[:, sl], sqD[:])
            return i
        att_toks.append(PL.step("vector", attf,
                                waits=st_x + [tk_xT, mo_ready] +
                                ([att_toks[-1]] if att_toks else []), sem=ve))

    if DBG:
        dma("sync", sy, dbg_att.ap(), mo[:, 0:512], waits=[att_toks[-1]])
    st_a = ln_to_bc(lambda k: mo[:, k * LH:(k + 1) * LH], [att_toks[-1]])
    x2_toks = []
    for k in range(KD):
        def x2f(E, kk=k):
            i = None
            for n in range(LH // 512):
                sl = slice(kk * LH + n * 512, kk * LH + (n + 1) * 512)
                bl = slice(n * 512, (n + 1) * 512)
                VC(E, E.tensor_mul(sqD[:], mo[:, sl], bigbc1[:, bl]))
                VC(E, E.tensor_add(sqD[:], sqD[:], bigbc2[:, bl]))
                i = E.tensor_add(x2[:, sl], x2[:, sl], sqD[:])
            return i
        x2_toks.append(PL.step("vector", x2f,
                               waits=st_a + ([x2_toks[-1]] if x2_toks else []), sem=ve))

    st_2 = ln_to_bc(lambda k: x2[:, k * LH:(k + 1) * LH], [x2_toks[-1]])
    x2n_toks = []
    for k in range(KD):
        def x2nf(E, kk=k):
            i = None
            for n in range(LH // 512):
                sl = slice(kk * LH + n * 512, kk * LH + (n + 1) * 512)
                bl = slice(n * 512, (n + 1) * 512)
                VC(E, E.tensor_mul(sqD[:], x2[:, sl], bigbc1[:, bl]))
                VC(E, E.tensor_add(sqD[:], sqD[:], bigbc2[:, bl]))
                i = E.tensor_copy(x2n[:, sl], sqD[:])
            return i
        x2n_toks.append(PL.step("vector", x2nf,
                                waits=st_2 + [mo_evs[-1]] +
                                ([x2n_toks[-1]] if x2n_toks else []), sem=ve))
    x2n_ready = x2n_toks[-1]
    if stop_after == "D3":
        with nc.Block() as blk:
            PL.play(blk)
        return nc


    if DBG:
        dma("sync", sy, dbg_x2n.ap(), x2n[:, 0:512], waits=[x2n_ready])
    NQG = (2 * HID) // 8 // P
    ag_gu = cc_gu_out.ap().rearrange("(r k p) (mtl m) -> r mtl p k m", r=8, p=P, m=P)
    ag_wp = cc_wp_out.ap().rearrange("(r k p) (m2 m) -> r m2 p k m", r=8, p=P, m=P)

    tk_x2sp = dma("sync", sy, x2_sp.ap().rearrange("(k p) t -> p k t", p=P),
                  x2[:].rearrange("p (k t) -> p k t", t=LH), waits=[x2n_ready])
    tk_xnsp = dma("sync", sy, x2n_sp.ap().rearrange("(k p) t -> p k t", p=P),
                  x2n[:].rearrange("p (k t) -> p k t", t=LH), waits=[x2n_ready])

    out_wr_toks = []
    gsl_last = [None] * 4
    wsl_last = [None, None]
    prev_outf = None
    pbk4_guard = None
    prev_evr = None
    wp_done_blk = None
    qk = NKH // 2
    psg = {}
    last_h = None
    MOFF = 512
    SOFF = 512 + FCH
    for blk in range(LH // 512):
        B0 = blk * 512
        tk_xnb = dma("gpsimd", gp, x2n_blk[:].rearrange("p (k t) -> p k t", t=512),
                     x2n_sp.ap().rearrange("(k p) t -> p k t", p=P)[:, :, B0:B0 + 512],
                     waits=[tk_xnsp, (wp_done_blk if blk > 0 else None)])
        h_of_ht = []
        for ht in range(NHT):
            b4 = (ht % 2) * 2
            tk_g = PL.step("sync",
                           lambda E, m=ht, b=b4: E.dma_start(
                               out=guslab[b][:].rearrange("p (k m) -> p k m", m=P),
                               in_=ag_gu[m // NQG, m % NQG]),
                           waits=[(AGTOK["ag1"] if (blk == 0 and ht < 2) else None),
                                  gsl_last[b4]],
                           sem=sy, inck=16)
            tk_u2 = PL.step("gpsimd",
                            lambda E, m=NHT + ht, b=b4 + 1: E.dma_start(
                                out=guslab[b][:].rearrange("p (k m) -> p k m", m=P),
                                in_=ag_gu[m // NQG, m % NQG]),
                            waits=[(AGTOK["ag1"] if (blk == 0 and ht < 2) else None),
                                   gsl_last[b4 + 1]], sem=gp, inck=16)
            bkg = 2 * (ht % 2)
            bku = bkg + 1

            def mmg(E, b=b4, bk=bkg):
                i = None
                for k in range(KD):
                    i = E.matmul(pbk[bk][:], guslab[b][:, k * P:(k + 1) * P],
                                 x2n_blk[:, k * 512:(k + 1) * 512],
                                 start=(k == 0), stop=(k == KD - 1))
                return i
            tk_mg = PL.step("tensor", mmg, waits=[tk_g, tk_xnb, psg.get(bkg)], sem=te)

            def mmu(E, b=b4 + 1, bk=bku):
                i = None
                for k in range(KD):
                    i = E.matmul(pbk[bk][:], guslab[b][:, k * P:(k + 1) * P],
                                 x2n_blk[:, k * 512:(k + 1) * 512],
                                 start=(k == 0), stop=(k == KD - 1))
                return i
            tk_mu = PL.step("tensor", mmu, waits=[tk_u2, tk_xnb, psg.get(bku)], sem=te)

            def evg(E, bk=bkg):
                return E.activation(gev[0][:], pbk[bk][:], AF.Silu, bias=0.0, scale=1.0)
            tk_eg = PL.step("scalar", evg, waits=[tk_mg, tk_mu, last_h], sem=sc)

            def evu(E, bk=bku):
                return E.copy(gev[1][:], pbk[bk][:])
            tk_eu = PL.step("scalar", evu, waits=[tk_mu], sem=sc)

            def hmk(E, t=ht):
                return E.tensor_mul(h_sb[:, t * 512:(t + 1) * 512], gev[0][:], gev[1][:])
            last_h = PL.step("vector", hmk,
                             waits=[tk_eg, tk_eu,
                                    (tk_x2sp if (blk == 0 and ht == 0) else None),
                                    (wp_done_blk if (blk > 0 and ht < 2) else None)],
                             sem=ve)
            psg[bkg] = tk_eg
            psg[bku] = tk_eu
            gsl_last[b4] = tk_mg
            gsl_last[b4 + 1] = tk_mu
            h_of_ht.append(last_h)
        h_all = h_of_ht[-1]

        hrow_done = None
        h2guard = [None, None]
        for ht in range(NHT):
            def h2f(E, t=ht, b=ht % 2):
                return E.activation(h2b[b][:], h_sb[:, t * 512:(t + 1) * 512], AF.Square,
                                    bias=0.0, scale=1.0)
            tk_h2 = PL.step("scalar", h2f, waits=[h_of_ht[ht], h2guard[ht % 2]], sem=sc)

            def hrow(E, t=ht, b=ht % 2):
                E.matmul(pR[:, 0:FCH], onesb[:, 0:1], h2b[b][:, 0:FCH], start=(t == 0),
                         stop=(t == NHT - 1))
                return E.matmul(pM[0:1, 0:FCH], onesb[:, 0:1], h2b[b][:, FCH:2 * FCH],
                                start=(t == 0), stop=(t == NHT - 1))
            hrow_done = PL.step("tensor", hrow,
                                waits=[tk_h2, (prev_evr if ht == 0 else None)], sem=te)
            h2guard[ht % 2] = hrow_done

        def evr2(E):
            E.copy(srow2[0:1, 0:FCH], pR[0:1, 0:FCH])
            return E.copy(srow2[0:1, FCH:2 * FCH], pM[0:1, 0:FCH])
        tk_evr = PL.step("scalar", evr2, waits=[hrow_done], sem=sc)
        prev_evr = tk_evr
        tk_i1 = PL.step("vector",
                        lambda E: E.tensor_scalar_mul(srow2[0:1, 0:512], srow2[0:1, 0:512],
                                                      1.0 / HID),
                        waits=[tk_evr], sem=ve)

        def invs(E):
            SC(E, E.activation(srow2[0:1, 0:512], srow2[0:1, 0:512], AF.Ln, bias=EPS_RMS,
                               scale=1.0))
            return E.activation(srow2[0:1, 0:512], srow2[0:1, 0:512], AF.Exp, bias=0.0,
                                scale=-0.5)
        tk_i2 = PL.step("scalar", invs, waits=[tk_i1], sem=sc)

        foev = []
        prev_fo = pbk4_guard
        for mt in range(NMT):
            last_q = None
            for q in range(2):
                bw = q % 2
                qeng, qsem = (("gpsimd", gp) if bw == 0 else ("sync", sy))

                def ldq(E, m=mt, b=bw, qq=q):
                    sw = ag_wp[(m * P) // QP_R, (m * P % QP_R) // P]
                    return E.dma_start(out=wpslab[b][:].rearrange("p (k m) -> p k m", m=P),
                                       in_=sw[:, qq * qk:(qq + 1) * qk, :])
                tk_wq = PL.step(qeng, ldq,
                                waits=[wsl_last[bw],
                                       (AGTOK["ag2"] if (blk == 0 and mt == 0 and q < 2) else None),
                                       (h_all if (mt == 0 and q < 2) else None)],
                                sem=qsem, inck=16)

                def mmw(E, b=bw, qq=q):
                    i = None
                    for k in range(qk):
                        kk = qq * qk + k
                        i = E.matmul(pbk[4][:], wpslab[b][:, k * P:(k + 1) * P],
                                     h_sb[:, kk * 512:(kk + 1) * 512],
                                     start=(qq == 0 and k == 0), stop=(qq == 1 and k == qk - 1))
                    return i
                last_q = PL.step("tensor", mmw,
                                 waits=[tk_wq, h_all, (prev_fo if q == 0 else None)],
                                 sem=te)
                wsl_last[bw] = last_q
            tk_fo = PL.step("scalar",
                            lambda E, m=mt: E.copy(fo[:, m * 512:(m + 1) * 512], pbk[4][:]),
                            waits=[last_q, (prev_outf if mt == 0 else None)], sem=sc)
            prev_fo = tk_fo
            foev.append(tk_fo)
        wp_done_blk = wsl_last[1]

        for tc in range(2):
            c0 = B0 + tc * FCH
            tk_xc2 = dma("gpsimd", gp, x2ch[:].rearrange("p (k t) -> p k t", t=FCH),
                         x2_sp.ap().rearrange("(k p) t -> p k t", p=P)[:, :, c0:c0 + FCH],
                         waits=[tk_x2sp, prev_outf])

            st4 = ln_stats(lambda k, t=tc: fo[:, k * 512 + t * FCH:k * 512 + (t + 1) * FCH],
                           FCH, [foev[-1], tk_i2], srow2, MOFF, SOFF, inv_off=tc * FCH)
            tb_s4 = bcast_row_at(bc4a, 0, srow2, SOFF, FCH, [st4, (h_of_ht[-1] if True else None)])
            tb_m4 = bcast_row_at(bc4b, 0, srow2, MOFF, FCH, [st4, tb_s4])

            for k in range(KD):
                bo = k % 2

                def outf(E, kk=k, b=bo, t=tc):
                    VC(E, E.tensor_mul(outD[b][:], fo[:, kk * 512 + t * FCH:kk * 512 + (t + 1) * FCH],
                                       bc4a[:]))
                    VC(E, E.tensor_add(outD[b][:], outD[b][:], bc4b[:]))
                    return E.tensor_add(outD[b][:], outD[b][:], x2ch[:, kk * FCH:(kk + 1) * FCH])
                tk_of = PL.step("vector", outf,
                                waits=[tb_s4, tb_m4, tk_xc2,
                                       (out_wr_toks[-2] if len(out_wr_toks) >= 2 else None)],
                                sem=ve)
                out_wr_toks.append(dma("sync", sy, out_h[k * P:(k + 1) * P, c0:c0 + FCH],
                                       outD[bo][:], waits=[tk_of]))
                prev_outf = tk_of
            pbk4_guard = tb_m4

    PL.step("sync", lambda E: None,
            waits=[out_wr_toks[-1], (sy, sy.n), (sy2, sy2.n), (gp, gp.n), (gp2, gp2.n),
                   (sd, sd.n), (sd2, sd2.n)])
    if stop_after == "D4":
        with nc.Block() as blk:
            PL.play(blk)
        return nc

    with nc.Block() as blk:
        PL.play(blk)
    return nc


def _prep_core(inputs, c, cfg):
    global EPS_LN, EPS_RMS
    L = cfg["L"]; D = cfg["D"]; DIN = cfg["DIN"]; NST = cfg["NST"]
    RANK = cfg["RANK"]; HID = cfg["HID"]
    DH = DIN // 2; NG = DH // P; KD = D // P; LH = L // 2
    b, j = c // 2, c % 2
    A = lambda v: np.ascontiguousarray(v)

    x = inputs["L0_attn_x"][b]
    w_in = inputs["L0_attn_in_proj_w"]
    w_inT = A(np.concatenate([w_in[j * DH:(j + 1) * DH],
                              w_in[DIN + j * DH:DIN + (j + 1) * DH]], 0).T)
    convw = A(inputs["L0_attn_conv_w"][j * DH:(j + 1) * DH, 0, :])
    vecs = np.stack([inputs["L0_attn_conv_b"][j * DH:(j + 1) * DH],
                     inputs["L0_attn_dt_proj_b"][j * DH:(j + 1) * DH],
                     inputs["L0_attn_D"][j * DH:(j + 1) * DH],
                     np.zeros(DH, np.float32)], 1).reshape(NG, P, 4)
    xprojT = A(inputs["L0_attn_x_proj_w"][:, j * DH:(j + 1) * DH].T)
    dtprojT = A(inputs["L0_attn_dt_proj_w"][j * DH:(j + 1) * DH, :].T)
    alog = A(inputs["L0_attn_A_log"][j * DH:(j + 1) * DH])
    oprojT = A(inputs["L0_attn_out_proj_w"].T)
    normw = A(inputs["L0_attn_norm_w"].reshape(KD, P).T)
    QGU_R = 2 * 4 * D // 8 * (HID // (4 * D)) if False else (2 * HID) // 8
    QP_R = D // 8
    wgu_q = A(inputs["L0_ffn_w_gate_up"][c * QGU_R:(c + 1) * QGU_R])
    wp_q = A(inputs["L0_ffn_w_proj"][c * QP_R:(c + 1) * QP_R])
    xT = A(x.T[:, j * LH:(j + 1) * LH])

    sel = np.zeros((2 * NST, 2 * NST * P), dtype=NBF)
    for i in range(2 * NST):
        sel[i, i * P:(i + 1) * P] = NBF(1.0)

    return dict(
        x_tm=A(x), x_fm=xT, w_inT=w_inT, convw=convw, vecs=A(vecs.astype(np.float32)),
        xprojT=xprojT, dtprojT=dtprojT, alog=alog, oprojT=oprojT, normw=normw,
        wgu_q=wgu_q, wp_q=wp_q,
        ident_bf=np.eye(P, dtype=NBF),
        ones_f=np.ones((P, 4), np.float32),
        ones_r=np.ones((1, P), np.float32),
        ones_b=np.ones((P, 4), NBF),
        consts=np.broadcast_to(np.array([EPS_LN, 1.0, EPS_RMS, 0.0], np.float32),
                               (P, 4)).copy(),
        sel_bf=sel,
    )


def kernel(**inputs):
    cfg = CFG
    inputs = {k: np.asarray(v) for k, v in inputs.items()}
    nc = build(cfg)
    in_maps = [_prep_core(inputs, c, cfg) for c in range(8)]
    res = run_bass_kernel_spmd(nc, in_maps, core_ids=list(range(8)))
    L = cfg["L"]; D = cfg["D"]; LH = L // 2
    out = np.empty((4, L, D), np.float32)
    for c in range(8):
        b, j = c // 2, c % 2
        out[b, j * LH:(j + 1) * LH, :] = res.results[c]["out_fm"].T
    return out
